# version 06
# baseline (speedup 1.0000x reference)
import numpy as np
import concourse.bass as bass
import concourse.mybir as mybir
from concourse.bass_utils import run_bass_kernel_spmd
from contextlib import ExitStack

F32 = mybir.dt.float32
BF16 = mybir.dt.bfloat16
FP8 = mybir.dt.float8e4
AF = mybir.ActivationFunctionType
ALU = mybir.AluOpType
AX = mybir.AxisListType

NCORES = 8
TOK = 2048
D = 2048
NT = 2
TT = 1024
NCH = 16
NB = 32
C_INV = float(1.0 / np.sqrt(np.float32(128.0)))
ENGS = ("sync", "tensor", "vector", "scalar", "gpsimd")


class Prog:
    def __init__(self):
        self.ops = {e: [] for e in ENGS}
        self.cnt = {}

    def op(self, eng, waits, incs, fn):
        for s, a in incs:
            self.cnt[s] = self.cnt.get(s, 0) + a
        for s, v in waits:
            self.cnt.setdefault(s, 0)
        self.ops[eng].append((tuple(waits), tuple(incs), fn))

    def check(self):
        pos = {e: 0 for e in ENGS}
        cur = {s: 0 for s in self.cnt}
        progress = True
        while progress:
            progress = False
            for e in ENGS:
                while pos[e] < len(self.ops[e]):
                    waits, incs, _ = self.ops[e][pos[e]]
                    if all(cur.get(s, 0) >= v for s, v in waits):
                        for s, a in incs:
                            cur[s] = cur.get(s, 0) + a
                        pos[e] += 1
                        progress = True
                    else:
                        break
        stuck = {e: pos[e] for e in ENGS if pos[e] < len(self.ops[e])}
        if stuck:
            det = {}
            for e in stuck:
                waits, _, _ = self.ops[e][pos[e]]
                det[e] = [(s, v, cur.get(s, 0)) for s, v in waits]
            raise RuntimeError(f"deadlock: stuck={stuck} waits={det}")


def build_nc():
    nc = bass.Bass("TRN2", target_bir_lowering=False)

    d_x = nc.dram_tensor("x", [TOK, D], F32, kind="ExternalInput").ap()
    d_wqkv = nc.dram_tensor("wqkv", [D, 6144], F32, kind="ExternalInput").ap()
    d_bqkv = nc.dram_tensor("bqkv", [48, 128], F32, kind="ExternalInput").ap()
    d_wout = nc.dram_tensor("wout", [D, D], F32, kind="ExternalInput").ap()
    d_bout = nc.dram_tensor("bout", [16, 128], F32, kind="ExternalInput").ap()
    d_wffn = nc.dram_tensor("wffn", [D, D], F32, kind="ExternalInput").ap()
    d_bffn = nc.dram_tensor("bffn", [16, 128], F32, kind="ExternalInput").ap()
    d_ws = nc.dram_tensor("wscale", [1, 1], F32, kind="ExternalInput").ap()
    d_ident = nc.dram_tensor("ident", [128, 128], F32, kind="ExternalInput").ap()
    d_mask4 = nc.dram_tensor("mask4", [128, 512], F32, kind="ExternalInput").ap()
    d_hsel = nc.dram_tensor("hsel", [16, 2048], F32, kind="ExternalInput").ap()
    d_out = nc.dram_tensor("out", [TOK, D], F32, kind="ExternalOutput").ap()
    t_ccin = nc.dram_tensor("cc_in", [1, 1], F32)
    t_ccout = nc.dram_tensor("cc_out", [1, 1], F32)
    d_ccin = t_ccin.ap()
    d_ccout = t_ccout.ap()

    es = ExitStack()
    SBt = lambda n, sh, dt=F32: es.enter_context(nc.sbuf_tensor(n, sh, dt))
    PSt = lambda n, sh: es.enter_context(nc.psum_tensor(n, sh, F32))

    with es:
        AR = SBt("AR", [128, 55936], BF16)
        def ar(lo, n, p=128):
            return AR[0:p, lo:lo + n]
        I_bf = lambda: ar(0, 128)
        wvsum = lambda k: ar(128 + k * 128, 128)
        mask4_bf = lambda: ar(2176, 512)
        hsel_bf = lambda h: ar(2688 + h * 128, 128, 16)
        rs16 = lambda c0, n: ar(4736 + c0, n, 16)
        xnT = lambda k, c0, n: ar(5760 + k * 1024 + c0, n)
        Pb = lambda b2: ar(5760 + b2 * 512, 512)
        Ptb = lambda b2: ar(6784 + b2 * 512, 512)
        rsst = lambda b2: ar(7808 + b2 * 512, 512, 1)
        attnT = lambda c0, n: ar(5760 + c0, n)
        Qg = lambda c0, n: ar(22144 + c0, n)
        Kg = lambda c0, n: ar(38528 + c0, n)
        svT = lambda c0, n: ar(54912 + c0, n)

        xb = [SBt("xb0", [128, D]), SBt("xb1", [128, D])]
        xnb = SBt("xnb", [128, D])
        wring = [SBt("wr0", [128, 2048], BF16), SBt("wr1", [128, 2048], BF16)]
        xqT8 = SBt("xqT8", [128, 16 * 1024], FP8)
        w8T = SBt("w8T", [128, 16 * 2048], FP8)
        expb = [SBt("expb0", [128, 512]), SBt("expb1", [128, 512])]
        sg_sb = [SBt("sg0", [128, 512]), SBt("sg1", [128, 512])]
        silu_sb = expb
        outb = [SBt("outb0", [128, 512]), SBt("outb1", [128, 512])]
        sIf = SBt("sIf", [128, 128])
        stats_m = SBt("st_m", [128, 16])
        stats_r = SBt("st_r", [128, 16])
        stats_b = SBt("st_b", [128, 16])
        stats_s1 = SBt("st_s1", [128, 16])
        stats_s2 = SBt("st_s2", [128, 16])
        stats_mx = SBt("st_mx", [128, 16])
        stats_mn = SBt("st_mn", [128, 16])
        tmp1 = SBt("tmp1", [128, 16])
        tmp2 = SBt("tmp2", [128, 16])
        gmax = SBt("gmax", [128, 1])
        gsc = SBt("gsc", [1, 136])
        inv2s = SBt("inv2s", [128, 1])
        tws = SBt("tws", [128, 1])
        onesrow = SBt("onesrow", [1, 128])
        ones_bf = SBt("ones_bf", [128, 1], BF16)
        cols = SBt("cols", [128, 80])
        cbq = SBt("cbq", [128, 16])
        bv_col = SBt("bv_col", [128, 1])
        nmx = [SBt("nmx0", [128, 4]), SBt("nmx1", [128, 4])]
        sumb = [SBt("sumb0", [128, 4]), SBt("sumb1", [128, 4])]
        rb = [SBt("rb0", [128, 4]), SBt("rb1", [128, 4])]

        psA = PSt("psA", [128, 512])
        psB = PSt("psB", [128, 512])
        psC = PSt("psC", [128, 512])
        psD = PSt("psD", [128, 512])
        psE = PSt("psE", [128, 512])
        psE2 = PSt("psE2", [128, 512])
        psF = PSt("psF", [1, 512])

        P = Prog()
        sems = {}
        for n in ["sld", "spc", "sbias", "swv", "swvs", "sx", "sxa", "sxv",
                  "ssta", "sva", "ssq", "sfin", "sgt1", "sgr", "scc1",
                  "scc", "sccb", "srct", "sbc", "sinv", "sxn", "stp", "stpe",
                  "swq", "sqkv", "sqkve", "satt", "sattm", "satte", "sattp",
                  "sattt", "sattpe", "sattrs", "sattre", "srs16", "swf",
                  "sw8t", "sw8e", "srsb", "srsbe", "swo", "syg", "ssilu",
                  "ssg", "sgt", "sgte", "sout", "spre", "stwc", "sfz", "svr", "srcb"]:
            sems[n] = es.enter_context(nc.semaphore(n))

        def OP(eng, waits, incs, build):
            waits = [(s, v) for s, v in waits if v > 0]
            def fn(e):
                for s, v in waits:
                    e.wait_ge(sems[s], v)
                inst = build(e)
                for s, a in incs:
                    inst = inst.then_inc(sems[s], a)
            P.op(eng, waits, incs, fn)

        OP("sync", [], [("sld", 16)], lambda e: e.dma_start(out=sIf[:, :], in_=d_ident))
        OP("sync", [], [("sld", 16)], lambda e: e.dma_start(out=xb[0][:, 0:512], in_=d_mask4))
        OP("sync", [], [("sld", 16)], lambda e: e.dma_start(out=xnb[0:16, :], in_=d_hsel))
        OP("sync", [], [("sld", 16)], lambda e: e.dma_start(out=xb[1][0:48, 0:128], in_=d_bqkv))
        OP("sync", [], [("sld", 16)], lambda e: e.dma_start(out=xb[1][0:16, 128:256], in_=d_bout))
        OP("sync", [], [("sld", 16)], lambda e: e.dma_start(out=xb[1][0:16, 256:384], in_=d_bffn))
        OP("sync", [], [("sld", 16)], lambda e: e.dma_start(out=gsc[0:1, 0:1], in_=d_ws))
        for i in range(4):
            OP("gpsimd", [], [("spre", 16)],
               lambda e, i=i: e.dma_start(out=d_out[i * 512:(i + 1) * 512, :],
                                          in_=d_x[i * 512:(i + 1) * 512, :]))
        NPC = 20
        OP("vector", [("sld", 112)], [("spc", 1)], lambda e: e.tensor_copy(I_bf(), sIf[:, :]))
        OP("vector", [], [("spc", 1)], lambda e: e.tensor_copy(mask4_bf(), xb[0][:, 0:512]))
        for h in range(16):
            OP("vector", [], [("spc", 1)],
               lambda e, h=h: e.tensor_copy(hsel_bf(h), xnb[0:16, h * 128:(h + 1) * 128]))
        OP("vector", [], [("spc", 1)], lambda e: e.memset(ones_bf[:, :], 1.0))
        OP("vector", [], [("spc", 1)], lambda e: e.memset(onesrow[:, :], 1.0))
        OP("tensor", [("spc", NPC)], [("sbias", 1)],
           lambda e: e.matmul(psE[:, 0:48], xb[1][0:48, 0:128], sIf[0:48, 0:48],
                              start=True, stop=True))
        OP("tensor", [], [("sbias", 1)],
           lambda e: e.matmul(psE[:, 48:64], xb[1][0:16, 128:256], sIf[0:16, 0:16],
                              start=True, stop=True))
        OP("tensor", [], [("sbias", 1)],
           lambda e: e.matmul(psE[:, 64:80], xb[1][0:16, 256:384], sIf[0:16, 0:16],
                              start=True, stop=True))
        OP("vector", [("sbias", 3)], [("sbias", 1)],
           lambda e: e.tensor_copy(cols[:, :], psE[:, 0:80]))
        bq_view = cols[:, 0:48].rearrange("p (h j) -> p j h", h=16, j=3)
        OP("vector", [("sbias", 4)], [("sbias", 1)],
           lambda e: e.tensor_scalar(cbq[:, :], bq_view[:, 0, :], C_INV, None, ALU.mult))
        OP("vector", [("sbias", 5)], [("sbias", 1)],
           lambda e: e.tensor_reduce(bv_col[:, :], bq_view[:, 2, :], AX.X, ALU.add))
        OP("tensor", [("sbias", 6)], [("stwc", 1)],
           lambda e: e.matmul(psE[:, 80:81], onesrow[:, :], gsc[0:1, 0:1],
                              start=True, stop=True))
        OP("vector", [("stwc", 1)], [("stwc", 1)],
           lambda e: e.tensor_scalar(tws[:, :], psE[:, 80:81], 2.0, None, ALU.mult))

        wq_v = d_wqkv.rearrange("m (h j d) -> m h j d", h=16, j=3, d=128)
        for k in range(16):
            w = [("swvs", k - 1)] if k >= 2 else []
            OP("gpsimd", w, [("swv", 16)],
               lambda e, k=k: e.dma_start(out=wring[k % 2][:, :],
                                          in_=wq_v[k * 128:(k + 1) * 128, :, 2, :]))
            wv_view = wring[k % 2][:, :].rearrange("p (h d) -> p d h", h=16, d=128)
            OP("vector", [("swv", 16 * (k + 1))], [("svr", 1)],
               lambda e, v=wv_view: e.tensor_reduce(outb[0][:, 0:128], v, AX.X, ALU.add))
            OP("vector", [("svr", k + 1)], [("swvs", 1)],
               lambda e, k=k: e.tensor_copy(wvsum(k), outb[0][:, 0:128]))

        for oc in range(16):
            if oc < 2:
                w = [("svr", 16)]
            else:
                w = [("sw8e", 4 * (oc - 1))]
            OP("gpsimd", w, [("swf", 16)],
               lambda e, oc=oc: e.dma_start(out=wring[oc % 2][:, :],
                                            in_=d_wffn[oc * 128:(oc + 1) * 128, :]))
        for oc in range(16):
            for r in range(4):
                gri = oc * 4 + r
                ebank = psE if gri % 2 == 0 else psE2
                for i in range(4):
                    ic = r * 4 + i
                    waits = []
                    if i == 0:
                        waits.append(("swf", 16 * (oc + 1)))
                        if gri == 0:
                            waits.append(("sbias", 4))
                            waits.append(("stwc", 2))
                        if gri >= 2:
                            waits.append(("sw8e", gri - 1))
                    incs = [("sw8t", 1)] if i == 3 else []
                    OP("tensor", waits, incs,
                       lambda e, eb=ebank, oc=oc, ic=ic, i=i: e.matmul(
                           eb[:, i * 128:(i + 1) * 128],
                           wring[oc % 2][:, ic * 128:(ic + 1) * 128], I_bf(),
                           start=True, stop=True))
                for i in range(4):
                    ic = r * 4 + i
                    waits = [("sw8t", gri + 1)] if i == 0 else []
                    incs = [("sw8e", 1)] if i == 3 else []
                    OP("vector", waits, incs,
                       lambda e, eb=ebank, oc=oc, ic=ic, i=i: e.tensor_copy(
                           w8T[:, ic * 2048 + oc * 128:ic * 2048 + (oc + 1) * 128],
                           eb[:, i * 128:(i + 1) * 128]))

        for c in range(NCH):
            if c == 0:
                w = [("spc", NPC)]
            elif c == 1:
                w = [("sbias", 3)]
            else:
                w = [("ssta", c - 1), ("sxv", c - 1)]
            OP("sync", w, [("sx", 16)],
               lambda e, c=c: e.dma_start(out=xb[c % 2][:, :], in_=d_x[c * 128:(c + 1) * 128, :]))
            xc = xb[c % 2]
            aw = [("sx", 16 * (c + 1))]
            if c == 0:
                aw.append(("spc", NPC))
            OP("scalar", aw, [],
               lambda e, xc=xc, c=c: e.activation(xnb[:, :], xc[:, :], AF.Square,
                                                  accum_out=stats_s2[:, c:c + 1]))
            OP("scalar", [], [("ssta", 1)],
               lambda e, xc=xc, c=c: e.activation(xnb[:, :], xc[:, :], AF.Identity,
                                                  accum_out=stats_s1[:, c:c + 1]))
            OP("vector", [("sx", 16 * (c + 1))], [],
               lambda e, xc=xc, c=c: e.tensor_reduce(stats_mx[:, c:c + 1], xc[:, :],
                                                     AX.X, ALU.max))
            OP("vector", [], [("sxv", 1)],
               lambda e, xc=xc, c=c: e.tensor_reduce(stats_mn[:, c:c + 1], xc[:, :],
                                                     AX.X, ALU.min))
        _fz = [0]
        def VC(waits, incs, build):
            assert not incs
            OP("vector", waits + [("sfz", _fz[0])], [("sfz", 1)], build)
            _fz[0] += 1
            return _fz[0]
        VC([("ssta", NCH), ("sxv", NCH)], [],
           lambda e: e.tensor_scalar(stats_m[:, :], stats_s1[:, :],
                                     1.0 / 2048.0, None, ALU.mult))
        VC([], [], lambda e: e.tensor_mul(tmp2[:, :], stats_m[:, :], stats_m[:, :]))
        VC([], [], lambda e: e.tensor_scalar(tmp1[:, :], stats_s2[:, :],
                                             1.0 / 2048.0, None, ALU.mult))
        VC([], [], lambda e: e.tensor_sub(tmp1[:, :], tmp1[:, :], tmp2[:, :]))
        _va = VC([], [], lambda e: e.tensor_scalar(tmp1[:, :], tmp1[:, :], 1e-5, None, ALU.add))
        OP("scalar", [("sfz", _va)], [("ssq", 1)],
           lambda e: e.activation(tmp2[:, :], tmp1[:, :], AF.Sqrt))
        VC([("ssq", 1)], [], lambda e: e.reciprocal(stats_r[:, :], tmp2[:, :]))
        VC([], [], lambda e: e.tensor_mul(tmp2[:, :], stats_r[:, :], stats_r[:, :]))
        VC([], [], lambda e: e.tensor_mul(tmp2[:, :], tmp2[:, :], tmp1[:, :]))
        VC([], [], lambda e: e.tensor_scalar(tmp2[:, :], tmp2[:, :], -0.5, 1.5,
                                             ALU.mult, ALU.add))
        VC([], [], lambda e: e.tensor_mul(stats_r[:, :], stats_r[:, :], tmp2[:, :]))
        VC([], [], lambda e: e.tensor_mul(tmp2[:, :], stats_m[:, :], stats_r[:, :]))
        VC([], [], lambda e: e.tensor_scalar(stats_b[:, :], tmp2[:, :], -1.0, None,
                                             ALU.mult))
        FZ_STATS = _fz[0]
        VC([], [], lambda e: e.tensor_sub(tmp2[:, :], stats_mx[:, :], stats_m[:, :]))
        VC([], [], lambda e: e.tensor_sub(tmp1[:, :], stats_m[:, :], stats_mn[:, :]))
        VC([], [], lambda e: e.tensor_max(tmp2[:, :], tmp2[:, :], tmp1[:, :]))
        VC([], [], lambda e: e.tensor_mul(tmp2[:, :], tmp2[:, :], stats_r[:, :]))
        _fin = VC([], [], lambda e: e.tensor_reduce(gmax[:, :], tmp2[:, :], AX.X, ALU.max))
        FZ_FIN = _fin

        OP("tensor", [("sfz", FZ_FIN)], [("sgt1", 1)],
           lambda e: e.transpose(psF[0:1, 0:128], gmax[:, :], sIf[:, :]))
        VC([("sgt1", 1)], [], lambda e: e.tensor_copy(gsc[0:1, 8:136], psF[0:1, 0:128]))
        _gr = VC([], [],
           lambda e: e.tensor_reduce(gsc[0:1, 1:2], gsc[0:1, 8:136], AX.X, ALU.max))
        OP("gpsimd", [("sfz", _gr)], [("scc1", 16)],
           lambda e: e.dma_start(out=d_ccin, in_=gsc[0:1, 1:2]))
        OP("gpsimd", [("scc1", 16)], [("scc", 1)],
           lambda e: e.collective_compute("AllReduce", ALU.max,
                                          replica_groups=[list(range(NCORES))],
                                          ins=[t_ccin.ap().opt()],
                                          outs=[t_ccout.ap().opt()]))
        OP("sync", [("scc", 1)], [("sccb", 16)],
           lambda e: e.dma_start(out=gsc[0:1, 2:3], in_=d_ccout))
        VC([("sccb", 16)], [],
           lambda e: e.tensor_scalar(gsc[0:1, 3:4], gsc[0:1, 2:3], 1.0 / 448.0, 1e-7,
                                     ALU.mult, ALU.max))
        VC([], [],
           lambda e: e.tensor_scalar(gsc[0:1, 4:5], gsc[0:1, 3:4], 2.0, None, ALU.mult))
        _rct = VC([], [],
           lambda e: e.reciprocal(gsc[0:1, 5:6], gsc[0:1, 4:5]))
        OP("tensor", [("sfz", _rct), ("sw8e", 64)], [("sbc", 1)],
           lambda e: e.matmul(psE[:, 81:82], onesrow[:, :], gsc[0:1, 5:6],
                              start=True, stop=True))
        OP("vector", [("sbc", 1)], [("sinv", 1)],
           lambda e: e.tensor_copy(inv2s[:, :], psE[:, 81:82]))

        for T in range(NT):
            for c8 in range(8):
                gc = T * 8 + c8
                widx = NCH + gc
                if gc >= 2:
                    w = [("sxn", gc - 1)]
                else:
                    w = [("sxv", NCH), ("ssta", NCH)]
                OP("sync", w, [("sx", 16)],
                   lambda e, c=gc, b=xb[gc % 2]: e.dma_start(
                       out=b[:, :], in_=d_x[c * 128:(c + 1) * 128, :]))
                aw = [("sx", 16 * (widx + 1)), ("sfz", FZ_STATS)]
                if gc >= 1:
                    aw.append(("stp", 4 * gc))
                OP("scalar", aw, [("sxn", 1)],
                   lambda e, c=gc, b=xb[gc % 2]: e.activation(
                       xnb[:, :], b[:, :], AF.Identity,
                       bias=stats_b[:, c:c + 1], scale=stats_r[:, c:c + 1]))
                for r in range(4):
                    gri = gc * 4 + r
                    bank = psC if gri % 2 == 0 else psD
                    for i in range(4):
                        k = r * 4 + i
                        waits = []
                        if i == 0:
                            waits.append(("sxn", gc + 1))
                            if gri >= 2:
                                waits.append(("stpe", gri - 1))
                            if T == 1 and r < 2 and c8 == 0:
                                waits.append(("ssg", 32))
                        incs = [("stp", 1)] if i == 3 else []
                        OP("tensor", waits, incs,
                           lambda e, bank=bank, i=i, k=k: e.transpose(
                               bank[:, i * 128:(i + 1) * 128],
                               xnb[:, k * 128:(k + 1) * 128], sIf[:, :]))
                    for i in range(4):
                        k = r * 4 + i
                        waits = []
                        if i == 0:
                            waits.append(("stp", gri + 1))
                            waits.append(("sinv", 1))
                        OP("vector", waits, [],
                           lambda e, bank=bank, i=i, k=k, c8=c8: e.tensor_copy(
                               xnT(k, c8 * 128, 128), bank[:, i * 128:(i + 1) * 128]))
                        incs = [("stpe", 1)] if i == 3 else []
                        OP("vector", [], incs,
                           lambda e, bank=bank, i=i, k=k, c8=c8: e.tensor_scalar(
                               xqT8[:, k * 1024 + c8 * 128:k * 1024 + c8 * 128 + 128],
                               bank[:, i * 128:(i + 1) * 128],
                               inv2s[:, :], None, ALU.mult))

            for j in range(33):
                jj = T * 33 + j
                if j < 32:
                    h = j % 16
                    colbase = (3 * h + (0 if j < 16 else 1)) * 128
                    wq_blk = d_wqkv.rearrange("(k p) f -> p k f", p=128)[:, :, colbase:colbase + 128]
                    if jj < 2:
                        w = [("sw8t", 64)]
                    else:
                        w = [("sqkv", 2 * (jj - 1))]
                    OP("gpsimd", w, [("swq", 16)],
                       lambda e, blk=wq_blk, j=j: e.dma_start(out=wring[j % 2][:, :], in_=blk))
                for half in range(2):
                    ihalf = jj * 2 + half
                    pbank = psA if ihalf % 2 == 0 else psB
                    for k in range(16):
                        waits = []
                        if k == 0:
                            if j < 32:
                                waits.append(("swq", 16 * (T * 32 + j + 1)))
                            else:
                                waits.append(("swvs", 16))
                            if half == 0 and j == 0:
                                waits.append(("stpe", (T + 1) * 32))
                            if T == 1 and jj == 33 and half == 0 and k == 0:
                                waits.append(("ssg", 32))
                            if ihalf >= 2:
                                waits.append(("sqkve", ihalf - 1))
                        lhs = (wring[j % 2][:, k * 128:(k + 1) * 128] if j < 32
                               else wvsum(k))
                        incs = [("sqkv", 1)] if k == 15 else []
                        OP("tensor", waits, incs,
                           lambda e, p=pbank, lhs=lhs, k=k, half=half: e.matmul(
                               p[:, :], lhs, xnT(k, half * 512, 512),
                               start=(k == 0), stop=(k == 15)))
                    if j < 32:
                        dstf = Qg(0, 16384) if j < 16 else Kg(0, 16384)
                        dst3 = dstf.rearrange("p (g hh t) -> p g hh t", g=128, hh=16, t=8)
                        dst = dst3[:, half * 64:(half + 1) * 64, h, :]
                        sc = C_INV if j < 16 else 1.0
                        bias = cbq[:, h:h + 1] if j < 16 else cols[:, 3 * h + 1:3 * h + 2]
                        OP("scalar", [("sqkv", ihalf + 1)], [("sqkve", 1)],
                           lambda e, p=pbank, dst=dst, sc=sc, bias=bias: e.activation(
                               dst, p[:, :], AF.Identity, bias=bias, scale=sc))
                    else:
                        OP("scalar", [("sqkv", ihalf + 1)], [("sqkve", 1)],
                           lambda e, p=pbank, half=half: e.activation(
                               svT(half * 512, 512), p[:, :], AF.Identity,
                               bias=bv_col[:, :], scale=1.0))

            for b in range(NB):
                gb = T * NB + b
                bank = [psC, psD, psE, psE2][b % 4]
                b2 = b % 2
                for g in range(4):
                    waits = []
                    if g == 0:
                        waits.append(("sqkve", (T + 1) * 66))
                        if T == 0 and b == 2:
                            waits.append(("sinv", 1))
                        if gb >= 4:
                            waits.append(("satte", gb - 3))
                    OP("tensor", waits, [],
                       lambda e, bank=bank, b=b, g=g: e.matmul(
                           bank[:, g * 128:(g + 1) * 128],
                           Qg(b * 512 + g * 128, 128), Kg(b * 512 + g * 128, 128),
                           start=(g == 0), stop=False))
                OP("tensor", [], [("satt", 1)],
                   lambda e, bank=bank: e.matmul(bank[:, :], I_bf(), mask4_bf(),
                                                 start=False, stop=True))
                mw = [("satt", gb + 1)]
                if gb >= 2:
                    mw.append(("satte", gb - 1))
                OP("vector", mw, [("sattm", 1)],
                   lambda e, bank=bank, b2=b2: e.tensor_reduce(
                       nmx[b2][:, :],
                       bank[:, :].rearrange("p (g n) -> p g n", g=4, n=128),
                       AX.X, ALU.max, negate=True))
                for g in range(4):
                    waits = []
                    if g == 0:
                        waits.append(("sattm", gb + 1))
                        if gb >= 2:
                            waits.append(("sattp", gb - 1))
                    incs = [("satte", 1)] if g == 3 else []
                    OP("scalar", waits, incs,
                       lambda e, bank=bank, b2=b2, g=g: e.activation(
                           expb[b2][:, g * 128:(g + 1) * 128],
                           bank[:, g * 128:(g + 1) * 128], AF.Exp,
                           bias=nmx[b2][:, g:g + 1], scale=1.0,
                           accum_out=sumb[b2][:, g:g + 1]))
                rw = [("satte", gb + 1)]
                if gb >= 2:
                    rw.append(("sattt", gb - 1))
                OP("vector", rw, [("srcb", 1)],
                   lambda e, b2=b2: e.reciprocal(rb[b2][:, :], sumb[b2][:, :]))
                for g in range(4):
                    incs = [("sattp", 1)] if g == 3 else []
                    OP("vector", [("srcb", gb + 1)] if g == 0 else [], incs,
                       lambda e, b2=b2, g=g: e.tensor_scalar(
                           Pb(b2)[:, g * 128:(g + 1) * 128],
                           expb[b2][:, g * 128:(g + 1) * 128],
                           rb[b2][:, g:g + 1], None, ALU.mult))
                ebank = psA if b % 2 == 0 else psB
                for g in range(4):
                    waits = []
                    if g == 0:
                        waits.append(("sattp", gb + 1))
                        if gb >= 2:
                            waits.append(("sattpe", gb - 1))
                    incs = [("sattt", 1)] if g == 3 else []
                    OP("tensor", waits, incs,
                       lambda e, eb=ebank, b2=b2, g=g: e.matmul(
                           eb[:, g * 128:(g + 1) * 128],
                           Pb(b2)[:, g * 128:(g + 1) * 128], I_bf(),
                           start=True, stop=True))
                pw = [("sattt", gb + 1)]
                if gb >= 2:
                    pw.append(("sattrs", gb - 1))
                OP("scalar", pw, [("sattpe", 1)],
                   lambda e, eb=ebank, b2=b2: e.activation(Ptb(b2), eb[:, :], AF.Copy))
                for g in range(4):
                    waits = []
                    if g == 0:
                        waits.append(("sattpe", gb + 1))
                        if gb >= 1:
                            waits.append(("sattre", gb))
                    incs = [("sattrs", 1)] if g == 3 else []
                    OP("tensor", waits, incs,
                       lambda e, b2=b2, g=g: e.matmul(
                           psF[0:1, g * 128:(g + 1) * 128], ones_bf[:, :],
                           Ptb(b2)[:, g * 128:(g + 1) * 128], start=True, stop=True))
                rdst = rsst(b2).rearrange("p (q g t) -> p g q t", q=16, g=4, t=8)
                rsrc = psF[0:1, :].rearrange("p (g q t) -> p g q t", g=4, q=16, t=8)
                ew = [("sattrs", gb + 1)]
                if gb >= 2:
                    ew.append(("srs16", 16 * (gb - 1)))
                OP("scalar", ew, [("sattre", 1)],
                   lambda e, rdst=rdst, rsrc=rsrc: e.activation(rdst, rsrc, AF.Copy))
                rsv = rsst(b2).rearrange("p (q gt) -> p q gt", q=16, gt=32)
                OP("sync", [("sattre", gb + 1)], [("srs16", 16)],
                   lambda e, rsv=rsv, b=b: e.dma_start(out=rs16(b * 32, 32), in_=rsv))

            for h in range(16):
                for half in range(2):
                    ih = (T * 16 + h) * 2 + half
                    ebank = psE if ih % 2 == 0 else psE2
                    waits = [("srs16", 16 * (T + 1) * NB), ("sw8e", 64),
                             ("satte", (T + 1) * NB)]
                    if ih >= 2:
                        waits.append(("srsbe", ih - 1))
                    OP("tensor", waits, [("srsb", 1)],
                       lambda e, eb=ebank, h=h, half=half: e.matmul(
                           eb[:, :], hsel_bf(h), rs16(half * 512, 512),
                           start=True, stop=True))
                    OP("vector", [("srsb", ih + 1)], [("srsbe", 1)],
                       lambda e, eb=ebank, h=h, half=half: e.tensor_tensor(
                           attnT(h * 1024 + half * 512, 512), eb[:, :],
                           svT(half * 512, 512), ALU.mult))

            for j in range(16):
                jj = T * 16 + j
                if j < 2:
                    w = [("srsbe", (T + 1) * 32)]
                else:
                    w = [("syg", 2 * (jj - 1))]
                wo_blk = d_wout.rearrange("(k p) f -> p k f", p=128)[:, :, j * 128:(j + 1) * 128]
                OP("gpsimd", w, [("swo", 16)],
                   lambda e, blk=wo_blk, j=j: e.dma_start(out=wring[j % 2][:, :], in_=blk))
                for half in range(2):
                    ihalf = jj * 2 + half
                    i2 = ihalf % 2
                    ybank = psA if i2 == 0 else psC
                    gbank2 = psB if i2 == 0 else psD
                    tbank = psE if i2 == 0 else psE2
                    for k in range(16):
                        waits = []
                        if k == 0:
                            waits.append(("swo", 16 * (T * 16 + j + 1)))
                            waits.append(("srsbe", (T + 1) * 32))
                            if T == 1 and ihalf % 32 < 2:
                                waits.append(("ssg", 32))
                            if ihalf >= 2:
                                waits.append(("ssg", ihalf - 1))
                        OP("tensor", waits, [],
                           lambda e, j=j, k=k, half=half, yb=ybank: e.matmul(
                               yb[:, :], wring[j % 2][:, k * 128:(k + 1) * 128],
                               attnT(k * 1024 + half * 512, 512),
                               start=(k == 0), stop=(k == 15)))
                    for k in range(16):
                        incs = [("syg", 1)] if k == 15 else []
                        OP("tensor", [], incs,
                           lambda e, j=j, k=k, half=half, gb2=gbank2: e.matmul(
                               gb2[:, :],
                               w8T[:, k * 2048 + j * 128:k * 2048 + (j + 1) * 128],
                               xqT8[:, k * 1024 + half * 512:k * 1024 + half * 512 + 512],
                               start=(k == 0), stop=(k == 15)))
                    sw = [("syg", ihalf + 1)]
                    if ihalf >= 2:
                        sw.append(("ssg", ihalf - 1))
                    OP("scalar", sw, [("ssilu", 1)],
                       lambda e, j=j, gb2=gbank2, i2=i2: e.activation(
                           silu_sb[i2][:, :], gb2[:, :], AF.Silu,
                           bias=cols[:, 64 + j:65 + j], scale=tws[:, :]))
                    gw = [("ssilu", ihalf + 1)]
                    if ihalf >= 2:
                        gw.append(("sgt", ihalf - 1))
                    OP("vector", gw, [("ssg", 1)],
                       lambda e, j=j, yb=ybank, i2=i2: e.scalar_tensor_tensor(
                           sg_sb[i2][:, :], yb[:, :], cols[:, 48 + j:49 + j],
                           silu_sb[i2][:, :], ALU.add, ALU.mult))
                    for tg in range(4):
                        waits = []
                        if tg == 0:
                            waits.append(("ssg", ihalf + 1))
                            if T * 32 + 0 == ihalf or ihalf == T * 32 + 1:
                                waits.append(("srsbe", (T + 1) * 32))
                            if ihalf >= 2:
                                waits.append(("sgte", ihalf - 1))
                        incs = [("sgt", 1)] if tg == 3 else []
                        OP("tensor", waits, incs,
                           lambda e, gb=tbank, tg=tg, i2=i2: e.transpose(
                               gb[:, tg * 128:(tg + 1) * 128],
                               sg_sb[i2][:, tg * 128:(tg + 1) * 128], sIf[:, :]))
                    ow = [("sgt", ihalf + 1)]
                    if ihalf >= 2:
                        ow.append(("sout", 16 * (ihalf - 1)))
                    OP("vector", ow, [("sgte", 1)],
                       lambda e, gb=tbank, b2=ihalf % 2: e.tensor_copy(outb[b2][:, :],
                                                                      gb[:, :]))
                    rbase = T * 1024 + half * 512
                    dst = d_out[rbase:rbase + 512, j * 128:(j + 1) * 128] \
                        .rearrange("(tg t) m -> t tg m", tg=4, t=128)
                    src = outb[ihalf % 2][:, :].rearrange("p (tg m) -> p tg m", tg=4, m=128)
                    OP("gpsimd", [("sgte", ihalf + 1), ("spre", 64)], [("sout", 16)],
                       lambda e, dst=dst, src=src: e.dma_start(out=dst, in_=src,
                                                               accum_op=ALU.add))

        OP("sync", [("sout", 16 * 64)], [], lambda e: e.nop())

        P.check()
        blk = es.enter_context(nc.Block())
        for engname in ENGS:
            lst = P.ops[engname]
            def make(lst):
                def emit(eng):
                    for _, _, fn in lst:
                        fn(eng)
                return emit
            getattr(blk, engname)(make(lst))

    return nc


_NC_CACHE = None


def kernel(L0_attn_x, L0_attn_Wqkv, L0_attn_bqkv, L0_attn_Wout, L0_attn_bout,
           L0_ffn_weight, L0_ffn_weight_scale, L0_ffn_bias):
    global _NC_CACHE
    x = np.ascontiguousarray(np.asarray(L0_attn_x, dtype=np.float32)).reshape(16384, 2048)
    wqkv = np.ascontiguousarray(np.asarray(L0_attn_Wqkv, dtype=np.float32)).reshape(2048, 6144)
    bqkv = np.ascontiguousarray(np.asarray(L0_attn_bqkv, dtype=np.float32)).reshape(48, 128)
    wout = np.ascontiguousarray(np.asarray(L0_attn_Wout, dtype=np.float32)).reshape(2048, 2048)
    bout = np.ascontiguousarray(np.asarray(L0_attn_bout, dtype=np.float32)).reshape(16, 128)
    wffn = np.ascontiguousarray(np.asarray(L0_ffn_weight, dtype=np.float32)).reshape(2048, 2048)
    bffn = np.ascontiguousarray(np.asarray(L0_ffn_bias, dtype=np.float32)).reshape(16, 128)
    ws = np.asarray(L0_ffn_weight_scale, dtype=np.float32).reshape(1, 1)

    ident = np.eye(128, dtype=np.float32)
    m = np.full((128, 128), np.float32(-998244352.0), dtype=np.float32)
    for qh in range(16):
        for tq in range(8):
            for kh in range(qh + 1):
                m[qh * 8 + tq, kh * 8 + tq] = 0.0
    mask4 = np.tile(m, (1, 4))
    hsel = np.zeros((16, 2048), dtype=np.float32)
    for h in range(16):
        hsel[h, h * 128:(h + 1) * 128] = 1.0

    if _NC_CACHE is None:
        _NC_CACHE = build_nc()
    nc = _NC_CACHE

    in_maps = []
    for c in range(NCORES):
        in_maps.append({
            "x": np.ascontiguousarray(x[c * TOK:(c + 1) * TOK, :]),
            "wqkv": wqkv, "bqkv": bqkv, "wout": wout, "bout": bout,
            "wffn": wffn, "bffn": bffn, "wscale": ws,
            "ident": ident, "mask4": mask4, "hsel": hsel,
        })
    res = run_bass_kernel_spmd(nc, in_maps, core_ids=list(range(NCORES)))
    out = np.concatenate([res.results[c]["out"] for c in range(NCORES)], axis=0)
    return np.ascontiguousarray(out.reshape(4, 4096, 2048).astype(np.float32, copy=False))


if __name__ == "__main__":
    nc = build_nc()
    print("build ok; instructions:",
          sum(1 for _ in nc.all_instructions()) if hasattr(nc, "all_instructions") else "?")


# version 07
# speedup vs baseline: 1.1219x; 1.1219x over previous
import numpy as np
import concourse.bass as bass
import concourse.mybir as mybir
from concourse.bass_utils import run_bass_kernel_spmd
from contextlib import ExitStack

F32 = mybir.dt.float32
BF16 = mybir.dt.bfloat16
FP8 = mybir.dt.float8e4
AF = mybir.ActivationFunctionType
ALU = mybir.AluOpType
AX = mybir.AxisListType

NCORES = 8
TOK = 2048
D = 2048
NT = 2
TT = 1024
NCH = 16
NB = 32
C_INV = float(1.0 / np.sqrt(np.float32(128.0)))
ENGS = ("sync", "tensor", "vector", "scalar", "gpsimd")


class Prog:
    def __init__(self):
        self.ops = {e: [] for e in ENGS}
        self.cnt = {}

    def op(self, eng, waits, incs, fn):
        for s, a in incs:
            self.cnt[s] = self.cnt.get(s, 0) + a
        for s, v in waits:
            self.cnt.setdefault(s, 0)
        self.ops[eng].append((tuple(waits), tuple(incs), fn))

    def check(self):
        pos = {e: 0 for e in ENGS}
        cur = {s: 0 for s in self.cnt}
        progress = True
        while progress:
            progress = False
            for e in ENGS:
                while pos[e] < len(self.ops[e]):
                    waits, incs, _ = self.ops[e][pos[e]]
                    if all(cur.get(s, 0) >= v for s, v in waits):
                        for s, a in incs:
                            cur[s] = cur.get(s, 0) + a
                        pos[e] += 1
                        progress = True
                    else:
                        break
        stuck = {e: pos[e] for e in ENGS if pos[e] < len(self.ops[e])}
        if stuck:
            det = {}
            for e in stuck:
                waits, _, _ = self.ops[e][pos[e]]
                det[e] = [(s, v, cur.get(s, 0)) for s, v in waits]
            raise RuntimeError(f"deadlock: stuck={stuck} waits={det}")


def build_nc():
    nc = bass.Bass("TRN2", target_bir_lowering=False)

    d_x = nc.dram_tensor("x", [TOK, D], F32, kind="ExternalInput").ap()
    d_wqkv = nc.dram_tensor("wqkv", [D, 6144], F32, kind="ExternalInput").ap()
    d_bqkv = nc.dram_tensor("bqkv", [48, 128], F32, kind="ExternalInput").ap()
    d_wout = nc.dram_tensor("wout", [D, D], F32, kind="ExternalInput").ap()
    d_bout = nc.dram_tensor("bout", [16, 128], F32, kind="ExternalInput").ap()
    d_wffn = nc.dram_tensor("wffn", [D, D], F32, kind="ExternalInput").ap()
    d_bffn = nc.dram_tensor("bffn", [16, 128], F32, kind="ExternalInput").ap()
    d_ws = nc.dram_tensor("wscale", [1, 1], F32, kind="ExternalInput").ap()
    d_ident = nc.dram_tensor("ident", [128, 128], F32, kind="ExternalInput").ap()
    d_mask4 = nc.dram_tensor("mask4", [128, 512], F32, kind="ExternalInput").ap()
    d_hsel = nc.dram_tensor("hsel", [16, 2048], F32, kind="ExternalInput").ap()
    d_out = nc.dram_tensor("out", [TOK, D], F32, kind="ExternalOutput").ap()
    t_ccin = nc.dram_tensor("cc_in", [1, 1], F32)
    t_ccout = nc.dram_tensor("cc_out", [1, 1], F32)
    d_ccin = t_ccin.ap()
    d_ccout = t_ccout.ap()

    es = ExitStack()
    SBt = lambda n, sh, dt=F32: es.enter_context(nc.sbuf_tensor(n, sh, dt))
    PSt = lambda n, sh: es.enter_context(nc.psum_tensor(n, sh, F32))

    with es:
        AR = SBt("AR", [128, 55936], BF16)
        def ar(lo, n, p=128):
            return AR[0:p, lo:lo + n]
        I_bf = lambda: ar(0, 128)
        wvsum = lambda k: ar(128 + k * 128, 128)
        mask4_bf = lambda: ar(2176, 512)
        hsel_bf = lambda h: ar(2688 + h * 128, 128, 16)
        rs16 = lambda c0, n: ar(4736 + c0, n, 16)
        xnT = lambda k, c0, n: ar(5760 + k * 1024 + c0, n)
        Pb = lambda b2: ar(5760 + b2 * 512, 512)
        Ptb = lambda b2: ar(6784 + b2 * 512, 512)
        rsst = lambda b2: ar(7808 + b2 * 512, 512, 1)
        attnT = lambda c0, n: ar(5760 + c0, n)
        Qg = lambda c0, n: ar(22144 + c0, n)
        Kg = lambda c0, n: ar(38528 + c0, n)
        svT = lambda c0, n: ar(54912 + c0, n)

        xb = [SBt("xb0", [128, D]), SBt("xb1", [128, D])]
        xnb = SBt("xnb", [128, D])
        wring = [SBt("wr0", [128, 2048], BF16), SBt("wr1", [128, 2048], BF16)]
        xqT8 = SBt("xqT8", [128, 16 * 1024], FP8)
        w8T = SBt("w8T", [128, 16 * 2048], FP8)
        expb = [SBt("expb0", [128, 512]), SBt("expb1", [128, 512])]
        sg_sb = [SBt("sg0", [128, 512]), SBt("sg1", [128, 512])]
        silu_sb = expb
        outb = [SBt("outb0", [128, 512]), SBt("outb1", [128, 512])]
        sIf = SBt("sIf", [128, 128])
        stats_m = SBt("st_m", [128, 16])
        stats_r = SBt("st_r", [128, 16])
        stats_b = SBt("st_b", [128, 16])
        stats_s1 = SBt("st_s1", [128, 16])
        stats_s2 = SBt("st_s2", [128, 16])
        stats_mx = SBt("st_mx", [128, 16])
        stats_mn = SBt("st_mn", [128, 16])
        tmp1 = SBt("tmp1", [128, 16])
        tmp2 = SBt("tmp2", [128, 16])
        gmax = SBt("gmax", [128, 1])
        gsc = SBt("gsc", [1, 136])
        inv2s = SBt("inv2s", [128, 1])
        tws = SBt("tws", [128, 1])
        onesrow = SBt("onesrow", [1, 128])
        ones_bf = SBt("ones_bf", [128, 1], BF16)
        cols = SBt("cols", [128, 80])
        cbq = SBt("cbq", [128, 16])
        bv_col = SBt("bv_col", [128, 1])
        nmx = [SBt("nmx0", [128, 4]), SBt("nmx1", [128, 4])]
        sumb = [SBt("sumb0", [128, 4]), SBt("sumb1", [128, 4])]
        rb = [SBt("rb0", [128, 4]), SBt("rb1", [128, 4])]

        psA = PSt("psA", [128, 512])
        psB = PSt("psB", [128, 512])
        psC = PSt("psC", [128, 512])
        psD = PSt("psD", [128, 512])
        psE = PSt("psE", [128, 512])
        psE2 = PSt("psE2", [128, 512])
        psF = PSt("psF", [1, 512])

        P = Prog()
        sems = {}
        for n in ["sld", "spc", "sbias", "swv", "swvs", "sx", "sxa", "sxv",
                  "ssta", "sva", "ssq", "sfin", "sgt1", "sgr", "scc1",
                  "scc", "sccb", "srct", "sbc", "sinv", "sxn", "stp", "stpe",
                  "swq", "sqkv", "sqkve", "satt", "sattm", "satte", "sattp",
                  "sattt", "sattpe", "sattrs", "sattre", "srs16", "swf",
                  "sw8t", "sw8e", "srsb", "srsbe", "swo", "syg", "ssilu",
                  "ssg", "sgt", "sgte", "sout", "spre", "stwc", "sfz", "svr", "srcb"]:
            sems[n] = es.enter_context(nc.semaphore(n))

        def OP(eng, waits, incs, build):
            waits = [(s, v) for s, v in waits if v > 0]
            def fn(e):
                for s, v in waits:
                    e.wait_ge(sems[s], v)
                inst = build(e)
                for s, a in incs:
                    inst = inst.then_inc(sems[s], a)
            P.op(eng, waits, incs, fn)

        OP("sync", [], [("sld", 16)], lambda e: e.dma_start(out=sIf[:, :], in_=d_ident))
        OP("sync", [], [("sld", 16)], lambda e: e.dma_start(out=xb[0][:, 0:512], in_=d_mask4))
        OP("sync", [], [("sld", 16)], lambda e: e.dma_start(out=xnb[0:16, :], in_=d_hsel))
        OP("sync", [], [("sld", 16)], lambda e: e.dma_start(out=xb[1][0:48, 0:128], in_=d_bqkv))
        OP("sync", [], [("sld", 16)], lambda e: e.dma_start(out=xb[1][0:16, 128:256], in_=d_bout))
        OP("sync", [], [("sld", 16)], lambda e: e.dma_start(out=xb[1][0:16, 256:384], in_=d_bffn))
        OP("sync", [], [("sld", 16)], lambda e: e.dma_start(out=gsc[0:1, 0:1], in_=d_ws))
        for i in range(4):
            OP("gpsimd", [], [("spre", 16)],
               lambda e, i=i: e.dma_start(out=d_out[i * 512:(i + 1) * 512, :],
                                          in_=d_x[i * 512:(i + 1) * 512, :]))
        NPC = 20
        OP("vector", [("sld", 112)], [("spc", 1)], lambda e: e.tensor_copy(I_bf(), sIf[:, :]))
        OP("vector", [], [("spc", 1)], lambda e: e.tensor_copy(mask4_bf(), xb[0][:, 0:512]))
        for h in range(16):
            OP("vector", [], [("spc", 1)],
               lambda e, h=h: e.tensor_copy(hsel_bf(h), xnb[0:16, h * 128:(h + 1) * 128]))
        OP("vector", [], [("spc", 1)], lambda e: e.memset(ones_bf[:, :], 1.0))
        OP("vector", [], [("spc", 1)], lambda e: e.memset(onesrow[:, :], 1.0))
        OP("tensor", [("spc", NPC)], [("sbias", 1)],
           lambda e: e.matmul(psE[:, 0:48], xb[1][0:48, 0:128], sIf[0:48, 0:48],
                              start=True, stop=True))
        OP("tensor", [], [("sbias", 1)],
           lambda e: e.matmul(psE[:, 48:64], xb[1][0:16, 128:256], sIf[0:16, 0:16],
                              start=True, stop=True))
        OP("tensor", [], [("sbias", 1)],
           lambda e: e.matmul(psE[:, 64:80], xb[1][0:16, 256:384], sIf[0:16, 0:16],
                              start=True, stop=True))
        OP("vector", [("sbias", 3)], [("sbias", 1)],
           lambda e: e.tensor_copy(cols[:, :], psE[:, 0:80]))
        bq_view = cols[:, 0:48].rearrange("p (h j) -> p j h", h=16, j=3)
        OP("vector", [("sbias", 4)], [("sbias", 1)],
           lambda e: e.tensor_scalar(cbq[:, :], bq_view[:, 0, :], C_INV, None, ALU.mult))
        OP("vector", [("sbias", 5)], [("sbias", 1)],
           lambda e: e.tensor_reduce(bv_col[:, :], bq_view[:, 2, :], AX.X, ALU.add))
        OP("tensor", [("sbias", 6)], [("stwc", 1)],
           lambda e: e.matmul(psE[:, 80:81], onesrow[:, :], gsc[0:1, 0:1],
                              start=True, stop=True))
        OP("vector", [("stwc", 1)], [("stwc", 1)],
           lambda e: e.tensor_scalar(tws[:, :], psE[:, 80:81], 2.0, None, ALU.mult))

        wq_v = d_wqkv.rearrange("m (h j d) -> m h j d", h=16, j=3, d=128)
        for k in range(16):
            w = [("swvs", k - 1)] if k >= 2 else []
            OP("gpsimd", w, [("swv", 16)],
               lambda e, k=k: e.dma_start(out=wring[k % 2][:, :],
                                          in_=wq_v[k * 128:(k + 1) * 128, :, 2, :]))
            wv_view = wring[k % 2][:, :].rearrange("p (h d) -> p d h", h=16, d=128)
            OP("vector", [("swv", 16 * (k + 1))], [("svr", 1)],
               lambda e, v=wv_view: e.tensor_reduce(outb[0][:, 0:128], v, AX.X, ALU.add))
            OP("vector", [("svr", k + 1)], [("swvs", 1)],
               lambda e, k=k: e.tensor_copy(wvsum(k), outb[0][:, 0:128]))

        for oc in range(16):
            if oc < 2:
                w = [("svr", 16)]
            else:
                w = [("sw8e", 4 * (oc - 1))]
            OP("gpsimd", w, [("swf", 16)],
               lambda e, oc=oc: e.dma_start(out=wring[oc % 2][:, :],
                                            in_=d_wffn[oc * 128:(oc + 1) * 128, :]))
        for oc in range(16):
            for r in range(4):
                gri = oc * 4 + r
                ebank = psE if gri % 2 == 0 else psE2
                for i in range(4):
                    ic = r * 4 + i
                    waits = []
                    if i == 0:
                        waits.append(("swf", 16 * (oc + 1)))
                        if gri == 0:
                            waits.append(("sbias", 4))
                            waits.append(("stwc", 2))
                        if gri >= 2:
                            waits.append(("sw8e", gri - 1))
                    incs = [("sw8t", 1)] if i == 3 else []
                    OP("tensor", waits, incs,
                       lambda e, eb=ebank, oc=oc, ic=ic, i=i: e.matmul(
                           eb[:, i * 128:(i + 1) * 128],
                           wring[oc % 2][:, ic * 128:(ic + 1) * 128], I_bf(),
                           start=True, stop=True))
                for i in range(4):
                    ic = r * 4 + i
                    waits = [("sw8t", gri + 1)] if i == 0 else []
                    incs = [("sw8e", 1)] if i == 3 else []
                    OP("vector", waits, incs,
                       lambda e, eb=ebank, oc=oc, ic=ic, i=i: e.tensor_copy(
                           w8T[:, ic * 2048 + oc * 128:ic * 2048 + (oc + 1) * 128],
                           eb[:, i * 128:(i + 1) * 128]))

        for c in range(NCH):
            if c == 0:
                w = [("spc", NPC)]
            elif c == 1:
                w = [("sbias", 3)]
            else:
                w = [("ssta", c - 1), ("sxv", c - 1)]
            OP("sync", w, [("sx", 16)],
               lambda e, c=c: e.dma_start(out=xb[c % 2][:, :], in_=d_x[c * 128:(c + 1) * 128, :]))
            xc = xb[c % 2]
            aw = [("sx", 16 * (c + 1))]
            if c == 0:
                aw.append(("spc", NPC))
            OP("scalar", aw, [],
               lambda e, xc=xc, c=c: e.activation(xnb[:, :], xc[:, :], AF.Square,
                                                  accum_out=stats_s2[:, c:c + 1]))
            OP("scalar", [], [("ssta", 1)],
               lambda e, xc=xc, c=c: e.activation(xnb[:, :], xc[:, :], AF.Identity,
                                                  accum_out=stats_s1[:, c:c + 1]))
            OP("vector", [("sx", 16 * (c + 1))], [],
               lambda e, xc=xc, c=c: e.tensor_reduce(stats_mx[:, c:c + 1], xc[:, :],
                                                     AX.X, ALU.max))
            OP("vector", [], [("sxv", 1)],
               lambda e, xc=xc, c=c: e.tensor_reduce(stats_mn[:, c:c + 1], xc[:, :],
                                                     AX.X, ALU.min))
        _fz = [0]
        def VC(waits, incs, build):
            assert not incs
            OP("vector", waits + [("sfz", _fz[0])], [("sfz", 1)], build)
            _fz[0] += 1
            return _fz[0]
        VC([("ssta", NCH), ("sxv", NCH)], [],
           lambda e: e.tensor_scalar(stats_m[:, :], stats_s1[:, :],
                                     1.0 / 2048.0, None, ALU.mult))
        VC([], [], lambda e: e.tensor_mul(tmp2[:, :], stats_m[:, :], stats_m[:, :]))
        VC([], [], lambda e: e.tensor_scalar(tmp1[:, :], stats_s2[:, :],
                                             1.0 / 2048.0, None, ALU.mult))
        VC([], [], lambda e: e.tensor_sub(tmp1[:, :], tmp1[:, :], tmp2[:, :]))
        _va = VC([], [], lambda e: e.tensor_scalar(tmp1[:, :], tmp1[:, :], 1e-5, None, ALU.add))
        OP("scalar", [("sfz", _va)], [("ssq", 1)],
           lambda e: e.activation(tmp2[:, :], tmp1[:, :], AF.Sqrt))
        VC([("ssq", 1)], [], lambda e: e.reciprocal(stats_r[:, :], tmp2[:, :]))
        VC([], [], lambda e: e.tensor_mul(tmp2[:, :], stats_r[:, :], stats_r[:, :]))
        VC([], [], lambda e: e.tensor_mul(tmp2[:, :], tmp2[:, :], tmp1[:, :]))
        VC([], [], lambda e: e.tensor_scalar(tmp2[:, :], tmp2[:, :], -0.5, 1.5,
                                             ALU.mult, ALU.add))
        VC([], [], lambda e: e.tensor_mul(stats_r[:, :], stats_r[:, :], tmp2[:, :]))
        VC([], [], lambda e: e.tensor_mul(tmp2[:, :], stats_m[:, :], stats_r[:, :]))
        VC([], [], lambda e: e.tensor_scalar(stats_b[:, :], tmp2[:, :], -1.0, None,
                                             ALU.mult))
        FZ_STATS = _fz[0]
        VC([], [], lambda e: e.tensor_sub(tmp2[:, :], stats_mx[:, :], stats_m[:, :]))
        VC([], [], lambda e: e.tensor_sub(tmp1[:, :], stats_m[:, :], stats_mn[:, :]))
        VC([], [], lambda e: e.tensor_max(tmp2[:, :], tmp2[:, :], tmp1[:, :]))
        VC([], [], lambda e: e.tensor_mul(tmp2[:, :], tmp2[:, :], stats_r[:, :]))
        _fin = VC([], [], lambda e: e.tensor_reduce(gmax[:, :], tmp2[:, :], AX.X, ALU.max))
        FZ_FIN = _fin

        OP("tensor", [("sfz", FZ_FIN)], [("sgt1", 1)],
           lambda e: e.transpose(psF[0:1, 0:128], gmax[:, :], sIf[:, :]))
        VC([("sgt1", 1)], [], lambda e: e.tensor_copy(gsc[0:1, 8:136], psF[0:1, 0:128]))
        _gr = VC([], [],
           lambda e: e.tensor_reduce(gsc[0:1, 1:2], gsc[0:1, 8:136], AX.X, ALU.max))
        OP("gpsimd", [("sfz", _gr)], [("scc1", 16)],
           lambda e: e.dma_start(out=d_ccin, in_=gsc[0:1, 1:2]))
        OP("gpsimd", [("scc1", 16)], [("scc", 1)],
           lambda e: e.collective_compute("AllReduce", ALU.max,
                                          replica_groups=[list(range(NCORES))],
                                          ins=[t_ccin.ap().opt()],
                                          outs=[t_ccout.ap().opt()]))
        OP("sync", [("scc", 1)], [("sccb", 16)],
           lambda e: e.dma_start(out=gsc[0:1, 2:3], in_=d_ccout))
        VC([("sccb", 16)], [],
           lambda e: e.tensor_scalar(gsc[0:1, 3:4], gsc[0:1, 2:3], 1.0 / 448.0, 1e-7,
                                     ALU.mult, ALU.max))
        VC([], [],
           lambda e: e.tensor_scalar(gsc[0:1, 4:5], gsc[0:1, 3:4], 2.0, None, ALU.mult))
        _rct = VC([], [],
           lambda e: e.reciprocal(gsc[0:1, 5:6], gsc[0:1, 4:5]))
        OP("tensor", [("sfz", _rct), ("sw8e", 64)], [("sbc", 1)],
           lambda e: e.matmul(psE[:, 81:82], onesrow[:, :], gsc[0:1, 5:6],
                              start=True, stop=True))
        OP("vector", [("sbc", 1)], [("sinv", 1)],
           lambda e: e.tensor_copy(inv2s[:, :], psE[:, 81:82]))

        for T in range(NT):
            for c8 in range(8):
                gc = T * 8 + c8
                widx = NCH + gc
                if gc >= 2:
                    w = [("sxn", gc - 1)]
                else:
                    w = [("sxv", NCH), ("ssta", NCH)]
                OP("sync", w, [("sx", 16)],
                   lambda e, c=gc, b=xb[gc % 2]: e.dma_start(
                       out=b[:, :], in_=d_x[c * 128:(c + 1) * 128, :]))
                aw = [("sx", 16 * (widx + 1)), ("sfz", FZ_STATS)]
                if gc >= 1:
                    aw.append(("stp", 4 * gc))
                OP("scalar", aw, [("sxn", 1)],
                   lambda e, c=gc, b=xb[gc % 2]: e.activation(
                       xnb[:, :], b[:, :], AF.Identity,
                       bias=stats_b[:, c:c + 1], scale=stats_r[:, c:c + 1]))
                for r in range(4):
                    gri = gc * 4 + r
                    bank = psC if gri % 2 == 0 else psD
                    for i in range(4):
                        k = r * 4 + i
                        waits = []
                        if i == 0:
                            waits.append(("sxn", gc + 1))
                            if gri >= 2:
                                waits.append(("stpe", gri - 1))
                            if T == 1 and r < 2 and c8 == 0:
                                waits.append(("ssg", 32))
                        incs = [("stp", 1)] if i == 3 else []
                        OP("tensor", waits, incs,
                           lambda e, bank=bank, i=i, k=k: e.transpose(
                               bank[:, i * 128:(i + 1) * 128],
                               xnb[:, k * 128:(k + 1) * 128], sIf[:, :]))
                    for i in range(4):
                        k = r * 4 + i
                        waits = []
                        if i == 0:
                            waits.append(("stp", gri + 1))
                            waits.append(("sinv", 1))
                        OP("vector", waits, [],
                           lambda e, bank=bank, i=i, k=k, c8=c8: e.tensor_copy(
                               xnT(k, c8 * 128, 128), bank[:, i * 128:(i + 1) * 128]))
                        incs = [("stpe", 1)] if i == 3 else []
                        OP("vector", [], incs,
                           lambda e, bank=bank, i=i, k=k, c8=c8: e.tensor_scalar(
                               xqT8[:, k * 1024 + c8 * 128:k * 1024 + c8 * 128 + 128],
                               bank[:, i * 128:(i + 1) * 128],
                               inv2s[:, :], None, ALU.mult))

            for j in range(33):
                jj = T * 33 + j
                if j < 32:
                    h = j % 16
                    colbase = (3 * h + (0 if j < 16 else 1)) * 128
                    wq_blk = d_wqkv.rearrange("(k p) f -> p k f", p=128)[:, :, colbase:colbase + 128]
                    if jj < 2:
                        w = [("sw8t", 64)]
                    else:
                        w = [("sqkv", 2 * (jj - 1))]
                    OP("gpsimd", w, [("swq", 16)],
                       lambda e, blk=wq_blk, j=j: e.dma_start(out=wring[j % 2][:, :], in_=blk))
                for half in range(2):
                    ihalf = jj * 2 + half
                    pbank = psA if ihalf % 2 == 0 else psB
                    for k in range(16):
                        waits = []
                        if k == 0:
                            if j < 32:
                                waits.append(("swq", 16 * (T * 32 + j + 1)))
                            else:
                                waits.append(("swvs", 16))
                            if half == 0 and j == 0:
                                waits.append(("stpe", (T + 1) * 32))
                            if T == 1 and jj == 33 and half == 0 and k == 0:
                                waits.append(("ssg", 32))
                            if ihalf >= 2:
                                waits.append(("sqkve", ihalf - 1))
                        lhs = (wring[j % 2][:, k * 128:(k + 1) * 128] if j < 32
                               else wvsum(k))
                        incs = [("sqkv", 1)] if k == 15 else []
                        OP("tensor", waits, incs,
                           lambda e, p=pbank, lhs=lhs, k=k, half=half: e.matmul(
                               p[:, :], lhs, xnT(k, half * 512, 512),
                               start=(k == 0), stop=(k == 15)))
                    if j < 32:
                        dstf = Qg(0, 16384) if j < 16 else Kg(0, 16384)
                        dst3 = dstf.rearrange("p (g hh t) -> p g hh t", g=128, hh=16, t=8)
                        dst = dst3[:, half * 64:(half + 1) * 64, h, :]
                        sc = C_INV if j < 16 else 1.0
                        bias = cbq[:, h:h + 1] if j < 16 else cols[:, 3 * h + 1:3 * h + 2]
                        OP("scalar", [("sqkv", ihalf + 1)], [("sqkve", 1)],
                           lambda e, p=pbank, dst=dst, sc=sc, bias=bias: e.activation(
                               dst, p[:, :], AF.Identity, bias=bias, scale=sc))
                    else:
                        OP("scalar", [("sqkv", ihalf + 1)], [("sqkve", 1)],
                           lambda e, p=pbank, half=half: e.activation(
                               svT(half * 512, 512), p[:, :], AF.Identity,
                               bias=bv_col[:, :], scale=1.0))

            def c_qk(b):
                gb = T * NB + b
                bank = [psC, psD, psE, psE2][b % 4]
                for g in range(4):
                    waits = []
                    if g == 0:
                        waits.append(("sqkve", (T + 1) * 66))
                        if T == 0 and b == 2:
                            waits.append(("sinv", 1))
                        if gb >= 4:
                            waits.append(("satte", gb - 3))
                    OP("tensor", waits, [],
                       lambda e, bank=bank, b=b, g=g: e.matmul(
                           bank[:, g * 128:(g + 1) * 128],
                           Qg(b * 512 + g * 128, 128), Kg(b * 512 + g * 128, 128),
                           start=(g == 0), stop=False))
                OP("tensor", [], [("satt", 1)],
                   lambda e, bank=bank: e.matmul(bank[:, :], I_bf(), mask4_bf(),
                                                 start=False, stop=True))

            def c_negmax(b):
                gb = T * NB + b
                bank = [psC, psD, psE, psE2][b % 4]
                b2 = b % 2
                mw = [("satt", gb + 1)]
                if gb >= 2:
                    mw.append(("satte", gb - 1))
                OP("vector", mw, [("sattm", 1)],
                   lambda e, bank=bank, b2=b2: e.tensor_reduce(
                       nmx[b2][:, :],
                       bank[:, :].rearrange("p (g n) -> p g n", g=4, n=128),
                       AX.X, ALU.max, negate=True))

            def c_exps(b):
                gb = T * NB + b
                bank = [psC, psD, psE, psE2][b % 4]
                b2 = b % 2
                for g in range(4):
                    waits = []
                    if g == 0:
                        waits.append(("sattm", gb + 1))
                        if gb >= 2:
                            waits.append(("sattp", gb - 1))
                    incs = [("satte", 1)] if g == 3 else []
                    OP("scalar", waits, incs,
                       lambda e, bank=bank, b2=b2, g=g: e.activation(
                           expb[b2][:, g * 128:(g + 1) * 128],
                           bank[:, g * 128:(g + 1) * 128], AF.Exp,
                           bias=nmx[b2][:, g:g + 1], scale=1.0,
                           accum_out=sumb[b2][:, g:g + 1]))

            def c_pmult(b):
                gb = T * NB + b
                b2 = b % 2
                rw = [("satte", gb + 1)]
                if gb >= 2:
                    rw.append(("sattt", gb - 1))
                OP("vector", rw, [("srcb", 1)],
                   lambda e, b2=b2: e.reciprocal(rb[b2][:, :], sumb[b2][:, :]))
                for g in range(4):
                    incs = [("sattp", 1)] if g == 3 else []
                    OP("vector", [("srcb", gb + 1)] if g == 0 else [], incs,
                       lambda e, b2=b2, g=g: e.tensor_scalar(
                           Pb(b2)[:, g * 128:(g + 1) * 128],
                           expb[b2][:, g * 128:(g + 1) * 128],
                           rb[b2][:, g:g + 1], None, ALU.mult))

            def c_pt(b):
                gb = T * NB + b
                b2 = b % 2
                ebank = psA if b % 2 == 0 else psB
                for g in range(4):
                    waits = []
                    if g == 0:
                        waits.append(("sattp", gb + 1))
                        if gb >= 2:
                            waits.append(("sattpe", gb - 1))
                    incs = [("sattt", 1)] if g == 3 else []
                    OP("tensor", waits, incs,
                       lambda e, eb=ebank, b2=b2, g=g: e.matmul(
                           eb[:, g * 128:(g + 1) * 128],
                           Pb(b2)[:, g * 128:(g + 1) * 128], I_bf(),
                           start=True, stop=True))

            def c_ptevac(b):
                gb = T * NB + b
                b2 = b % 2
                ebank = psA if b % 2 == 0 else psB
                pw = [("sattt", gb + 1)]
                if gb >= 2:
                    pw.append(("sattrs", gb - 1))
                OP("vector", pw, [("sattpe", 1)],
                   lambda e, eb=ebank, b2=b2: e.tensor_copy(Ptb(b2), eb[:, :]))

            def c_rsmm(b):
                gb = T * NB + b
                b2 = b % 2
                for g in range(4):
                    waits = []
                    if g == 0:
                        waits.append(("sattpe", gb + 1))
                        if gb >= 1:
                            waits.append(("sattre", gb))
                    incs = [("sattrs", 1)] if g == 3 else []
                    OP("tensor", waits, incs,
                       lambda e, b2=b2, g=g: e.matmul(
                           psF[0:1, g * 128:(g + 1) * 128], ones_bf[:, :],
                           Ptb(b2)[:, g * 128:(g + 1) * 128], start=True, stop=True))

            def c_rsevac(b):
                gb = T * NB + b
                b2 = b % 2
                rdst = rsst(b2).rearrange("p (q g t) -> p g q t", q=16, g=4, t=8)
                rsrc = psF[0:1, :].rearrange("p (g q t) -> p g q t", g=4, q=16, t=8)
                ew = [("sattrs", gb + 1)]
                if gb >= 2:
                    ew.append(("srs16", 16 * (gb - 1)))
                OP("scalar", ew, [("sattre", 1)],
                   lambda e, rdst=rdst, rsrc=rsrc: e.activation(rdst, rsrc, AF.Copy))
                rsv = rsst(b2).rearrange("p (q gt) -> p q gt", q=16, gt=32)
                OP("sync", [("sattre", gb + 1)], [("srs16", 16)],
                   lambda e, rsv=rsv, b=b: e.dma_start(out=rs16(b * 32, 32), in_=rsv))

            for it in range(NB + 2):
                if it < NB:
                    c_qk(it)
                    c_negmax(it)
                    c_exps(it)
                if 0 <= it - 1 < NB:
                    c_pmult(it - 1)
                    c_pt(it - 1)
                if 0 <= it - 2 < NB:
                    c_ptevac(it - 2)
                    c_rsmm(it - 2)
                    c_rsevac(it - 2)

            for h in range(16):
                for half in range(2):
                    ih = (T * 16 + h) * 2 + half
                    ebank = psE if ih % 2 == 0 else psE2
                    waits = [("srs16", 16 * (T + 1) * NB), ("sw8e", 64),
                             ("satte", (T + 1) * NB)]
                    if ih >= 2:
                        waits.append(("srsbe", ih - 1))
                    OP("tensor", waits, [("srsb", 1)],
                       lambda e, eb=ebank, h=h, half=half: e.matmul(
                           eb[:, :], hsel_bf(h), rs16(half * 512, 512),
                           start=True, stop=True))
                    OP("vector", [("srsb", ih + 1)], [("srsbe", 1)],
                       lambda e, eb=ebank, h=h, half=half: e.tensor_tensor(
                           attnT(h * 1024 + half * 512, 512), eb[:, :],
                           svT(half * 512, 512), ALU.mult))

            def f_load(j):
                jj = T * 16 + j
                if j < 2:
                    w = [("srsbe", (T + 1) * 32)]
                else:
                    w = [("syg", 2 * (jj - 1))]
                wo_blk = d_wout.rearrange("(k p) f -> p k f", p=128)[:, :, j * 128:(j + 1) * 128]
                OP("gpsimd", w, [("swo", 16)],
                   lambda e, blk=wo_blk, j=j: e.dma_start(out=wring[j % 2][:, :], in_=blk))

            def f_mm(j, half):
                ihalf = (T * 16 + j) * 2 + half
                i2 = ihalf % 2
                ybank = psA if i2 == 0 else psC
                gbank2 = psB if i2 == 0 else psD
                for k in range(16):
                    waits = []
                    if k == 0:
                        waits.append(("swo", 16 * (T * 16 + j + 1)))
                        waits.append(("srsbe", (T + 1) * 32))
                        if T == 1 and ihalf % 32 < 2:
                            waits.append(("ssg", 32))
                        if ihalf >= 2:
                            waits.append(("ssg", ihalf - 1))
                    OP("tensor", waits, [],
                       lambda e, j=j, k=k, half=half, yb=ybank: e.matmul(
                           yb[:, :], wring[j % 2][:, k * 128:(k + 1) * 128],
                           attnT(k * 1024 + half * 512, 512),
                           start=(k == 0), stop=(k == 15)))
                for k in range(16):
                    incs = [("syg", 1)] if k == 15 else []
                    OP("tensor", [], incs,
                       lambda e, j=j, k=k, half=half, gb2=gbank2: e.matmul(
                           gb2[:, :],
                           w8T[:, k * 2048 + j * 128:k * 2048 + (j + 1) * 128],
                           xqT8[:, k * 1024 + half * 512:k * 1024 + half * 512 + 512],
                           start=(k == 0), stop=(k == 15)))

            def f_silu_stt(j, half):
                ihalf = (T * 16 + j) * 2 + half
                i2 = ihalf % 2
                ybank = psA if i2 == 0 else psC
                gbank2 = psB if i2 == 0 else psD
                sw = [("syg", ihalf + 1)]
                if ihalf >= 2:
                    sw.append(("ssg", ihalf - 1))
                OP("scalar", sw, [("ssilu", 1)],
                   lambda e, j=j, gb2=gbank2, i2=i2: e.activation(
                       silu_sb[i2][:, :], gb2[:, :], AF.Silu,
                       bias=cols[:, 64 + j:65 + j], scale=tws[:, :]))
                gw = [("ssilu", ihalf + 1)]
                if ihalf >= 2:
                    gw.append(("sgt", ihalf - 1))
                OP("vector", gw, [("ssg", 1)],
                   lambda e, j=j, yb=ybank, i2=i2: e.scalar_tensor_tensor(
                       sg_sb[i2][:, :], yb[:, :], cols[:, 48 + j:49 + j],
                       silu_sb[i2][:, :], ALU.add, ALU.mult))

            def f_tgate(j, half):
                ihalf = (T * 16 + j) * 2 + half
                i2 = ihalf % 2
                tbank = psE if i2 == 0 else psE2
                for tg in range(4):
                    waits = []
                    if tg == 0:
                        waits.append(("ssg", ihalf + 1))
                        if ihalf % 32 < 2:
                            waits.append(("srsbe", (T + 1) * 32))
                        if ihalf >= 2:
                            waits.append(("sgte", ihalf - 1))
                    incs = [("sgt", 1)] if tg == 3 else []
                    OP("tensor", waits, incs,
                       lambda e, gb=tbank, tg=tg, i2=i2: e.transpose(
                           gb[:, tg * 128:(tg + 1) * 128],
                           sg_sb[i2][:, tg * 128:(tg + 1) * 128], sIf[:, :]))

            def f_out(j, half):
                ihalf = (T * 16 + j) * 2 + half
                i2 = ihalf % 2
                tbank = psE if i2 == 0 else psE2
                ow = [("sgt", ihalf + 1)]
                if ihalf >= 2:
                    ow.append(("sout", 16 * (ihalf - 1)))
                OP("vector", ow, [("sgte", 1)],
                   lambda e, gb=tbank, b2=i2: e.tensor_copy(outb[b2][:, :], gb[:, :]))
                rbase = T * 1024 + half * 512
                dst = d_out[rbase:rbase + 512, j * 128:(j + 1) * 128] \
                    .rearrange("(tg t) m -> t tg m", tg=4, t=128)
                srcb_ = outb[i2][:, :].rearrange("p (tg m) -> p tg m", tg=4, m=128)
                OP("gpsimd", [("sgte", ihalf + 1), ("spre", 64)], [("sout", 16)],
                   lambda e, dst=dst, src=srcb_: e.dma_start(out=dst, in_=src,
                                                             accum_op=ALU.add))

            prev = None
            for j in range(16):
                f_load(j)
                for half in range(2):
                    f_mm(j, half)
                    if prev is not None:
                        f_tgate(*prev)
                        f_out(*prev)
                    f_silu_stt(j, half)
                    prev = (j, half)
            f_tgate(*prev)
            f_out(*prev)
            prev = None

        OP("sync", [("sout", 16 * 64)], [], lambda e: e.nop())

        P.check()
        blk = es.enter_context(nc.Block())
        for engname in ENGS:
            lst = P.ops[engname]
            def make(lst):
                def emit(eng):
                    for _, _, fn in lst:
                        fn(eng)
                return emit
            getattr(blk, engname)(make(lst))

    return nc


_NC_CACHE = None


def kernel(L0_attn_x, L0_attn_Wqkv, L0_attn_bqkv, L0_attn_Wout, L0_attn_bout,
           L0_ffn_weight, L0_ffn_weight_scale, L0_ffn_bias):
    global _NC_CACHE
    x = np.ascontiguousarray(np.asarray(L0_attn_x, dtype=np.float32)).reshape(16384, 2048)
    wqkv = np.ascontiguousarray(np.asarray(L0_attn_Wqkv, dtype=np.float32)).reshape(2048, 6144)
    bqkv = np.ascontiguousarray(np.asarray(L0_attn_bqkv, dtype=np.float32)).reshape(48, 128)
    wout = np.ascontiguousarray(np.asarray(L0_attn_Wout, dtype=np.float32)).reshape(2048, 2048)
    bout = np.ascontiguousarray(np.asarray(L0_attn_bout, dtype=np.float32)).reshape(16, 128)
    wffn = np.ascontiguousarray(np.asarray(L0_ffn_weight, dtype=np.float32)).reshape(2048, 2048)
    bffn = np.ascontiguousarray(np.asarray(L0_ffn_bias, dtype=np.float32)).reshape(16, 128)
    ws = np.asarray(L0_ffn_weight_scale, dtype=np.float32).reshape(1, 1)

    ident = np.eye(128, dtype=np.float32)
    m = np.full((128, 128), np.float32(-998244352.0), dtype=np.float32)
    for qh in range(16):
        for tq in range(8):
            for kh in range(qh + 1):
                m[qh * 8 + tq, kh * 8 + tq] = 0.0
    mask4 = np.tile(m, (1, 4))
    hsel = np.zeros((16, 2048), dtype=np.float32)
    for h in range(16):
        hsel[h, h * 128:(h + 1) * 128] = 1.0

    if _NC_CACHE is None:
        _NC_CACHE = build_nc()
    nc = _NC_CACHE

    in_maps = []
    for c in range(NCORES):
        in_maps.append({
            "x": np.ascontiguousarray(x[c * TOK:(c + 1) * TOK, :]),
            "wqkv": wqkv, "bqkv": bqkv, "wout": wout, "bout": bout,
            "wffn": wffn, "bffn": bffn, "wscale": ws,
            "ident": ident, "mask4": mask4, "hsel": hsel,
        })
    res = run_bass_kernel_spmd(nc, in_maps, core_ids=list(range(NCORES)))
    out = np.concatenate([res.results[c]["out"] for c in range(NCORES)], axis=0)
    return np.ascontiguousarray(out.reshape(4, 4096, 2048).astype(np.float32, copy=False))


if __name__ == "__main__":
    nc = build_nc()
    print("build ok; instructions:",
          sum(1 for _ in nc.all_instructions()) if hasattr(nc, "all_instructions") else "?")


# version 08
# speedup vs baseline: 1.1565x; 1.0308x over previous
import numpy as np
import concourse.bass as bass
import concourse.mybir as mybir
from concourse.bass_utils import run_bass_kernel_spmd
from contextlib import ExitStack

F32 = mybir.dt.float32
BF16 = mybir.dt.bfloat16
FP8 = mybir.dt.float8e4
AF = mybir.ActivationFunctionType
ALU = mybir.AluOpType
AX = mybir.AxisListType

NCORES = 8
TOK = 2048
D = 2048
NT = 2
TT = 1024
NCH = 16
NB = 32
C_INV = float(1.0 / np.sqrt(np.float32(128.0)))
ENGS = ("sync", "tensor", "vector", "scalar", "gpsimd")


class Prog:
    def __init__(self):
        self.ops = {e: [] for e in ENGS}
        self.cnt = {}

    def op(self, eng, waits, incs, fn):
        for s, a in incs:
            self.cnt[s] = self.cnt.get(s, 0) + a
        for s, v in waits:
            self.cnt.setdefault(s, 0)
        self.ops[eng].append((tuple(waits), tuple(incs), fn))

    def check(self):
        pos = {e: 0 for e in ENGS}
        cur = {s: 0 for s in self.cnt}
        progress = True
        while progress:
            progress = False
            for e in ENGS:
                while pos[e] < len(self.ops[e]):
                    waits, incs, _ = self.ops[e][pos[e]]
                    if all(cur.get(s, 0) >= v for s, v in waits):
                        for s, a in incs:
                            cur[s] = cur.get(s, 0) + a
                        pos[e] += 1
                        progress = True
                    else:
                        break
        stuck = {e: pos[e] for e in ENGS if pos[e] < len(self.ops[e])}
        if stuck:
            det = {}
            for e in stuck:
                waits, _, _ = self.ops[e][pos[e]]
                det[e] = [(s, v, cur.get(s, 0)) for s, v in waits]
            raise RuntimeError(f"deadlock: stuck={stuck} waits={det}")


def build_nc():
    nc = bass.Bass("TRN2", target_bir_lowering=False)

    d_x = nc.dram_tensor("x", [TOK, D], F32, kind="ExternalInput").ap()
    d_wqkv = nc.dram_tensor("wqkv", [D, 6144], F32, kind="ExternalInput").ap()
    d_bqkv = nc.dram_tensor("bqkv", [48, 128], F32, kind="ExternalInput").ap()
    d_wout = nc.dram_tensor("wout", [D, D], F32, kind="ExternalInput").ap()
    d_bout = nc.dram_tensor("bout", [16, 128], F32, kind="ExternalInput").ap()
    d_wffn = nc.dram_tensor("wffn", [D, D], F32, kind="ExternalInput").ap()
    d_bffn = nc.dram_tensor("bffn", [16, 128], F32, kind="ExternalInput").ap()
    d_ws = nc.dram_tensor("wscale", [1, 1], F32, kind="ExternalInput").ap()
    d_ident = nc.dram_tensor("ident", [128, 128], F32, kind="ExternalInput").ap()
    d_mask4 = nc.dram_tensor("mask4", [128, 512], F32, kind="ExternalInput").ap()
    d_hsel = nc.dram_tensor("hsel", [16, 2048], F32, kind="ExternalInput").ap()
    d_out = nc.dram_tensor("out", [TOK, D], F32, kind="ExternalOutput").ap()
    t_ccin = nc.dram_tensor("cc_in", [1, 1], F32)
    t_ccout = nc.dram_tensor("cc_out", [1, 1], F32)
    d_ccin = t_ccin.ap()
    d_ccout = t_ccout.ap()

    es = ExitStack()
    SBt = lambda n, sh, dt=F32: es.enter_context(nc.sbuf_tensor(n, sh, dt))
    PSt = lambda n, sh: es.enter_context(nc.psum_tensor(n, sh, F32))

    with es:
        AR = SBt("AR", [128, 55936], BF16)
        def ar(lo, n, p=128):
            return AR[0:p, lo:lo + n]
        I_bf = lambda: ar(0, 128)
        wvsum = lambda k: ar(128 + k * 128, 128)
        mask4_bf = lambda: ar(2176, 512)
        hsel_bf = lambda h: ar(2688 + h * 128, 128, 16)
        rs16 = lambda c0, n: ar(4736 + c0, n, 16)
        xnT = lambda k, c0, n: ar(5760 + k * 1024 + c0, n)
        Pb = lambda b2: ar(5760 + b2 * 512, 512)
        Ptb = lambda b2: ar(6784 + b2 * 512, 512)
        rsst = lambda b2: ar(7808 + b2 * 512, 512, 1)
        attnT = lambda c0, n: ar(5760 + c0, n)
        Qg = lambda c0, n: ar(22144 + c0, n)
        Kg = lambda c0, n: ar(38528 + c0, n)
        svT = lambda c0, n: ar(54912 + c0, n)

        xb = [SBt("xb0", [128, D]), SBt("xb1", [128, D])]
        xnb = SBt("xnb", [128, D])
        wring = [SBt("wr0", [128, 2048], BF16), SBt("wr1", [128, 2048], BF16)]
        xqT8 = SBt("xqT8", [128, 16 * 1024], FP8)
        w8T = SBt("w8T", [128, 16 * 2048], FP8)
        expb = [SBt("expb0", [128, 512]), SBt("expb1", [128, 512])]
        sg_sb = [SBt("sg0", [128, 512]), SBt("sg1", [128, 512])]
        silu_sb = expb
        outb = [SBt("outb0", [128, 512]), SBt("outb1", [128, 512])]
        sIf = SBt("sIf", [128, 128])
        stats_m = SBt("st_m", [128, 16])
        stats_r = SBt("st_r", [128, 16])
        stats_b = SBt("st_b", [128, 16])
        stats_s1 = SBt("st_s1", [128, 16])
        stats_s2 = SBt("st_s2", [128, 16])
        stats_mx = SBt("st_mx", [128, 16])
        stats_mn = SBt("st_mn", [128, 16])
        tmp1 = SBt("tmp1", [128, 16])
        tmp2 = SBt("tmp2", [128, 16])
        gmax = SBt("gmax", [128, 1])
        gsc = SBt("gsc", [1, 136])
        inv2s = SBt("inv2s", [128, 1])
        tws = SBt("tws", [128, 1])
        onesrow = SBt("onesrow", [1, 128])
        ones_bf = SBt("ones_bf", [128, 1], BF16)
        cols = SBt("cols", [128, 80])
        cbq = SBt("cbq", [128, 16])
        bv_col = SBt("bv_col", [128, 1])
        nmx = [SBt("nmx0", [128, 4]), SBt("nmx1", [128, 4])]
        sumb = [SBt("sumb0", [128, 4]), SBt("sumb1", [128, 4])]
        rb = [SBt("rb0", [128, 4]), SBt("rb1", [128, 4])]

        psA = PSt("psA", [128, 512])
        psB = PSt("psB", [128, 512])
        psC = PSt("psC", [128, 512])
        psD = PSt("psD", [128, 512])
        psE = PSt("psE", [128, 512])
        psE2 = PSt("psE2", [128, 512])
        psF = PSt("psF", [1, 512])

        P = Prog()
        sems = {}
        for n in ["sld", "spc", "sbias", "swv", "swvs", "sx", "sxa", "sxv",
                  "ssta", "sva", "ssq", "sfin", "sgt1", "sgr", "scc1",
                  "scc", "sccb", "srct", "sbc", "sinv", "sxn", "stp", "stpe",
                  "swq", "sqkv", "sqkve", "satt", "sattm", "satte", "sattp",
                  "sattt", "sattpe", "sattrs", "sattre", "srs16", "swf",
                  "sw8t", "sw8e", "srsb", "srsbe", "swo", "syg", "ssilu",
                  "ssg", "sgt", "sgte", "sout", "spre", "stwc", "sfz", "svr", "srcb", "swvp", "swvt"]:
            sems[n] = es.enter_context(nc.semaphore(n))

        def OP(eng, waits, incs, build):
            waits = [(s, v) for s, v in waits if v > 0]
            def fn(e):
                for s, v in waits:
                    e.wait_ge(sems[s], v)
                inst = build(e)
                for s, a in incs:
                    inst = inst.then_inc(sems[s], a)
            P.op(eng, waits, incs, fn)

        OP("sync", [], [("sld", 16)], lambda e: e.dma_start(out=sIf[:, :], in_=d_ident))
        OP("sync", [], [("sld", 16)], lambda e: e.dma_start(out=xb[0][:, 0:512], in_=d_mask4))
        OP("sync", [], [("sld", 16)], lambda e: e.dma_start(out=xnb[0:16, :], in_=d_hsel))
        OP("sync", [], [("sld", 16)], lambda e: e.dma_start(out=xb[1][0:48, 0:128], in_=d_bqkv))
        OP("sync", [], [("sld", 16)], lambda e: e.dma_start(out=xb[1][0:16, 128:256], in_=d_bout))
        OP("sync", [], [("sld", 16)], lambda e: e.dma_start(out=xb[1][0:16, 256:384], in_=d_bffn))
        OP("sync", [], [("sld", 16)], lambda e: e.dma_start(out=gsc[0:1, 0:1], in_=d_ws))
        for i in range(4):
            OP("gpsimd", [], [("spre", 16)],
               lambda e, i=i: e.dma_start(out=d_out[i * 512:(i + 1) * 512, :],
                                          in_=d_x[i * 512:(i + 1) * 512, :]))
        NPC = 20
        OP("vector", [("sld", 112)], [("spc", 1)], lambda e: e.tensor_copy(I_bf(), sIf[:, :]))
        OP("vector", [], [("spc", 1)], lambda e: e.tensor_copy(mask4_bf(), xb[0][:, 0:512]))
        for h in range(16):
            OP("vector", [], [("spc", 1)],
               lambda e, h=h: e.tensor_copy(hsel_bf(h), xnb[0:16, h * 128:(h + 1) * 128]))
        OP("vector", [], [("spc", 1)], lambda e: e.memset(ones_bf[:, :], 1.0))
        OP("vector", [], [("spc", 1)], lambda e: e.memset(onesrow[:, :], 1.0))
        OP("tensor", [("spc", NPC)], [("sbias", 1)],
           lambda e: e.matmul(psE[:, 0:48], xb[1][0:48, 0:128], sIf[0:48, 0:48],
                              start=True, stop=True))
        OP("tensor", [], [("sbias", 1)],
           lambda e: e.matmul(psE[:, 48:64], xb[1][0:16, 128:256], sIf[0:16, 0:16],
                              start=True, stop=True))
        OP("tensor", [], [("sbias", 1)],
           lambda e: e.matmul(psE[:, 64:80], xb[1][0:16, 256:384], sIf[0:16, 0:16],
                              start=True, stop=True))
        OP("vector", [("sbias", 3)], [("sbias", 1)],
           lambda e: e.tensor_copy(cols[:, :], psE[:, 0:80]))
        bq_view = cols[:, 0:48].rearrange("p (h j) -> p j h", h=16, j=3)
        OP("vector", [("sbias", 4)], [("sbias", 1)],
           lambda e: e.tensor_scalar(cbq[:, :], bq_view[:, 0, :], C_INV, None, ALU.mult))
        OP("vector", [("sbias", 5)], [("sbias", 1)],
           lambda e: e.tensor_reduce(bv_col[:, :], bq_view[:, 2, :], AX.X, ALU.add))
        OP("tensor", [("sbias", 6)], [("stwc", 1)],
           lambda e: e.matmul(psE[:, 80:81], onesrow[:, :], gsc[0:1, 0:1],
                              start=True, stop=True))
        OP("vector", [("stwc", 1)], [("stwc", 1)],
           lambda e: e.tensor_scalar(tws[:, :], psE[:, 80:81], 2.0, None, ALU.mult))

        wq_v = d_wqkv.rearrange("m (h j d) -> m h j d", h=16, j=3, d=128)
        wvscr = [ar(22144, 128), ar(22272, 128)]
        for k in range(16):
            w = [("swvp", k - 1)] if k >= 2 else []
            OP("gpsimd", w, [("swv", 16)],
               lambda e, k=k: e.dma_start(out=wring[k % 2][:, :],
                                          in_=wq_v[k * 128:(k + 1) * 128, :, 2, :]))
            for h in range(16):
                waits = []
                if h == 0:
                    waits.append(("swv", 16 * (k + 1)))
                    if k == 0:
                        waits += [("sbias", 4), ("stwc", 2)]
                    if k >= 1:
                        waits.append(("svr", 2 * k - 1))
                incs = [("swvp", 1)] if h == 15 else []
                OP("tensor", waits, incs,
                   lambda e, k=k, h=h: e.matmul(
                       psE[:, 0:128], wring[k % 2][:, h * 128:(h + 1) * 128],
                       I_bf(), start=(h == 0), stop=(h == 15)))
            OP("vector", [("swvp", k + 1)], [("svr", 1)],
               lambda e, k=k: e.tensor_copy(wvscr[k % 2], psE[:, 0:128]))
            OP("tensor", [("svr", 2 * k + 1)], [("swvt", 1)],
               lambda e, k=k: e.matmul(psE2[:, 0:128], wvscr[k % 2], I_bf(),
                                       start=True, stop=True))
            OP("vector", [("swvt", k + 1)], [("svr", 1)],
               lambda e, k=k: e.tensor_copy(wvsum(k), psE2[:, 0:128]))

        def wf_load(oc):
            if oc < 2:
                w = [("swvp", 16)]
            else:
                w = [("sw8e", 4 * (oc - 1))]
            OP("gpsimd", w, [("swf", 16)],
               lambda e, oc=oc: e.dma_start(out=wring[oc % 2][:, :],
                                            in_=d_wffn[oc * 128:(oc + 1) * 128, :]))
        wf_load(0)
        wf_load(1)

        def w8t_oc(oc):
            for r in range(4):
                gri = oc * 4 + r
                ebank = psE if gri % 2 == 0 else psE2
                for i in range(4):
                    ic = r * 4 + i
                    waits = []
                    if i == 0:
                        waits.append(("swf", 16 * (oc + 1)))
                        if gri == 0:
                            waits.append(("svr", 32))
                            waits.append(("sinv", 1))
                        if gri >= 2:
                            waits.append(("sw8e", gri - 1))
                    incs = [("sw8t", 1)] if i == 3 else []
                    OP("tensor", waits, incs,
                       lambda e, eb=ebank, oc=oc, ic=ic, i=i: e.matmul(
                           eb[:, i * 128:(i + 1) * 128],
                           wring[oc % 2][:, ic * 128:(ic + 1) * 128], I_bf(),
                           start=True, stop=True))
                for i in range(4):
                    ic = r * 4 + i
                    waits = [("sw8t", gri + 1)] if i == 0 else []
                    incs = [("sw8e", 1)] if i == 3 else []
                    OP("vector", waits, incs,
                       lambda e, eb=ebank, oc=oc, ic=ic, i=i: e.tensor_copy(
                           w8T[:, ic * 2048 + oc * 128:ic * 2048 + (oc + 1) * 128],
                           eb[:, i * 128:(i + 1) * 128]))

        for c in range(NCH):
            if c == 0:
                w = [("spc", NPC)]
            elif c == 1:
                w = [("sbias", 3)]
            else:
                w = [("ssta", c - 1), ("sxv", c - 1)]
            OP("sync", w, [("sx", 16)],
               lambda e, c=c: e.dma_start(out=xb[c % 2][:, :], in_=d_x[c * 128:(c + 1) * 128, :]))
            xc = xb[c % 2]
            aw = [("sx", 16 * (c + 1))]
            if c == 0:
                aw.append(("spc", NPC))
            OP("scalar", aw, [],
               lambda e, xc=xc, c=c: e.activation(xnb[:, :], xc[:, :], AF.Square,
                                                  accum_out=stats_s2[:, c:c + 1]))
            OP("scalar", [], [("ssta", 1)],
               lambda e, xc=xc, c=c: e.activation(xnb[:, :], xc[:, :], AF.Identity,
                                                  accum_out=stats_s1[:, c:c + 1]))
            OP("vector", [("sx", 16 * (c + 1))], [],
               lambda e, xc=xc, c=c: e.tensor_reduce(stats_mx[:, c:c + 1], xc[:, :],
                                                     AX.X, ALU.max))
            OP("vector", [], [("sxv", 1)],
               lambda e, xc=xc, c=c: e.tensor_reduce(stats_mn[:, c:c + 1], xc[:, :],
                                                     AX.X, ALU.min))
        _fz = [0]
        def VC(waits, incs, build):
            assert not incs
            OP("vector", waits + [("sfz", _fz[0])], [("sfz", 1)], build)
            _fz[0] += 1
            return _fz[0]
        VC([("ssta", NCH), ("sxv", NCH)], [],
           lambda e: e.tensor_scalar(stats_m[:, :], stats_s1[:, :],
                                     1.0 / 2048.0, None, ALU.mult))
        VC([], [], lambda e: e.tensor_mul(tmp2[:, :], stats_m[:, :], stats_m[:, :]))
        VC([], [], lambda e: e.tensor_scalar(tmp1[:, :], stats_s2[:, :],
                                             1.0 / 2048.0, None, ALU.mult))
        VC([], [], lambda e: e.tensor_sub(tmp1[:, :], tmp1[:, :], tmp2[:, :]))
        _va = VC([], [], lambda e: e.tensor_scalar(tmp1[:, :], tmp1[:, :], 1e-5, None, ALU.add))
        OP("scalar", [("sfz", _va)], [("ssq", 1)],
           lambda e: e.activation(tmp2[:, :], tmp1[:, :], AF.Sqrt))
        VC([("ssq", 1)], [], lambda e: e.reciprocal(stats_r[:, :], tmp2[:, :]))
        VC([], [], lambda e: e.tensor_mul(tmp2[:, :], stats_r[:, :], stats_r[:, :]))
        VC([], [], lambda e: e.tensor_mul(tmp2[:, :], tmp2[:, :], tmp1[:, :]))
        VC([], [], lambda e: e.tensor_scalar(tmp2[:, :], tmp2[:, :], -0.5, 1.5,
                                             ALU.mult, ALU.add))
        VC([], [], lambda e: e.tensor_mul(stats_r[:, :], stats_r[:, :], tmp2[:, :]))
        VC([], [], lambda e: e.tensor_mul(tmp2[:, :], stats_m[:, :], stats_r[:, :]))
        VC([], [], lambda e: e.tensor_scalar(stats_b[:, :], tmp2[:, :], -1.0, None,
                                             ALU.mult))
        FZ_STATS = _fz[0]
        VC([], [], lambda e: e.tensor_sub(tmp2[:, :], stats_mx[:, :], stats_m[:, :]))
        VC([], [], lambda e: e.tensor_sub(tmp1[:, :], stats_m[:, :], stats_mn[:, :]))
        VC([], [], lambda e: e.tensor_max(tmp2[:, :], tmp2[:, :], tmp1[:, :]))
        VC([], [], lambda e: e.tensor_mul(tmp2[:, :], tmp2[:, :], stats_r[:, :]))
        _fin = VC([], [], lambda e: e.tensor_reduce(gmax[:, :], tmp2[:, :], AX.X, ALU.max))
        FZ_FIN = _fin

        OP("tensor", [("sfz", FZ_FIN)], [("sgt1", 1)],
           lambda e: e.transpose(psF[0:1, 0:128], gmax[:, :], sIf[:, :]))
        VC([("sgt1", 1)], [], lambda e: e.tensor_copy(gsc[0:1, 8:136], psF[0:1, 0:128]))
        _gr = VC([], [],
           lambda e: e.tensor_reduce(gsc[0:1, 1:2], gsc[0:1, 8:136], AX.X, ALU.max))
        OP("gpsimd", [("sfz", _gr)], [("scc1", 16)],
           lambda e: e.dma_start(out=d_ccin, in_=gsc[0:1, 1:2]))
        OP("gpsimd", [("scc1", 16)], [("scc", 1)],
           lambda e: e.collective_compute("AllReduce", ALU.max,
                                          replica_groups=[list(range(NCORES))],
                                          ins=[t_ccin.ap().opt()],
                                          outs=[t_ccout.ap().opt()]))
        OP("sync", [("scc", 1)], [("sccb", 16)],
           lambda e: e.dma_start(out=gsc[0:1, 2:3], in_=d_ccout))
        VC([("sccb", 16)], [],
           lambda e: e.tensor_scalar(gsc[0:1, 3:4], gsc[0:1, 2:3], 1.0 / 448.0, 1e-7,
                                     ALU.mult, ALU.max))
        VC([], [],
           lambda e: e.tensor_scalar(gsc[0:1, 4:5], gsc[0:1, 3:4], 2.0, None, ALU.mult))
        _rct = VC([], [],
           lambda e: e.reciprocal(gsc[0:1, 5:6], gsc[0:1, 4:5]))
        OP("tensor", [("sfz", _rct), ("svr", 32)], [("sbc", 1)],
           lambda e: e.matmul(psE[:, 81:82], onesrow[:, :], gsc[0:1, 5:6],
                              start=True, stop=True))
        OP("vector", [("sbc", 1)], [("sinv", 1)],
           lambda e: e.tensor_copy(inv2s[:, :], psE[:, 81:82]))

        for T in range(NT):
            for c8 in range(8):
                gc = T * 8 + c8
                widx = NCH + gc
                if gc >= 2:
                    w = [("sxn", gc - 1)]
                else:
                    w = [("sxv", NCH), ("ssta", NCH)]
                OP("sync", w, [("sx", 16)],
                   lambda e, c=gc, b=xb[gc % 2]: e.dma_start(
                       out=b[:, :], in_=d_x[c * 128:(c + 1) * 128, :]))
                aw = [("sx", 16 * (widx + 1)), ("sfz", FZ_STATS)]
                if gc >= 1:
                    aw.append(("stp", 4 * gc))
                OP("scalar", aw, [("sxn", 1)],
                   lambda e, c=gc, b=xb[gc % 2]: e.activation(
                       xnb[:, :], b[:, :], AF.Identity,
                       bias=stats_b[:, c:c + 1], scale=stats_r[:, c:c + 1]))
                for r in range(4):
                    gri = gc * 4 + r
                    bank = psC if gri % 2 == 0 else psD
                    for i in range(4):
                        k = r * 4 + i
                        waits = []
                        if i == 0:
                            waits.append(("sxn", gc + 1))
                            if gri >= 2:
                                waits.append(("stpe", gri - 1))
                            if T == 1 and r < 2 and c8 == 0:
                                waits.append(("ssg", 32))
                        incs = [("stp", 1)] if i == 3 else []
                        OP("tensor", waits, incs,
                           lambda e, bank=bank, i=i, k=k: e.transpose(
                               bank[:, i * 128:(i + 1) * 128],
                               xnb[:, k * 128:(k + 1) * 128], sIf[:, :]))
                    for i in range(4):
                        k = r * 4 + i
                        waits = []
                        if i == 0:
                            waits.append(("stp", gri + 1))
                            waits.append(("sinv", 1))
                        OP("vector", waits, [],
                           lambda e, bank=bank, i=i, k=k, c8=c8: e.tensor_copy(
                               xnT(k, c8 * 128, 128), bank[:, i * 128:(i + 1) * 128]))
                        incs = [("stpe", 1)] if i == 3 else []
                        OP("vector", [], incs,
                           lambda e, bank=bank, i=i, k=k, c8=c8: e.tensor_scalar(
                               xqT8[:, k * 1024 + c8 * 128:k * 1024 + c8 * 128 + 128],
                               bank[:, i * 128:(i + 1) * 128],
                               inv2s[:, :], None, ALU.mult))
                if T == 0:
                    if 2 * c8 + 2 < 16:
                        wf_load(2 * c8 + 2)
                    if 2 * c8 + 3 < 16:
                        wf_load(2 * c8 + 3)
                    w8t_oc(2 * c8)
                    w8t_oc(2 * c8 + 1)

            for j in range(33):
                jj = T * 33 + j
                if j < 32:
                    h = j % 16
                    colbase = (3 * h + (0 if j < 16 else 1)) * 128
                    wq_blk = d_wqkv.rearrange("(k p) f -> p k f", p=128)[:, :, colbase:colbase + 128]
                    if jj < 2:
                        w = [("sw8t", 64)]
                    else:
                        w = [("sqkv", 2 * (jj - 1))]
                    OP("gpsimd", w, [("swq", 16)],
                       lambda e, blk=wq_blk, j=j: e.dma_start(out=wring[j % 2][:, :], in_=blk))
                for half in range(2):
                    ihalf = jj * 2 + half
                    pbank = psA if ihalf % 2 == 0 else psB
                    for k in range(16):
                        waits = []
                        if k == 0:
                            if j < 32:
                                waits.append(("swq", 16 * (T * 32 + j + 1)))
                            else:
                                waits.append(("svr", 32))
                            if half == 0 and j == 0:
                                waits.append(("stpe", (T + 1) * 32))
                            if T == 1 and jj == 33 and half == 0 and k == 0:
                                waits.append(("ssg", 32))
                            if ihalf >= 2:
                                waits.append(("sqkve", ihalf - 1))
                        lhs = (wring[j % 2][:, k * 128:(k + 1) * 128] if j < 32
                               else wvsum(k))
                        incs = [("sqkv", 1)] if k == 15 else []
                        OP("tensor", waits, incs,
                           lambda e, p=pbank, lhs=lhs, k=k, half=half: e.matmul(
                               p[:, :], lhs, xnT(k, half * 512, 512),
                               start=(k == 0), stop=(k == 15)))
                    if j < 32:
                        dstf = Qg(0, 16384) if j < 16 else Kg(0, 16384)
                        dst3 = dstf.rearrange("p (g hh t) -> p g hh t", g=128, hh=16, t=8)
                        dst = dst3[:, half * 64:(half + 1) * 64, h, :]
                        sc = C_INV if j < 16 else 1.0
                        bias = cbq[:, h:h + 1] if j < 16 else cols[:, 3 * h + 1:3 * h + 2]
                        OP("scalar", [("sqkv", ihalf + 1)], [("sqkve", 1)],
                           lambda e, p=pbank, dst=dst, sc=sc, bias=bias: e.activation(
                               dst, p[:, :], AF.Identity, bias=bias, scale=sc))
                    else:
                        OP("scalar", [("sqkv", ihalf + 1)], [("sqkve", 1)],
                           lambda e, p=pbank, half=half: e.activation(
                               svT(half * 512, 512), p[:, :], AF.Identity,
                               bias=bv_col[:, :], scale=1.0))

            def c_qk(b):
                gb = T * NB + b
                bank = [psC, psD, psE, psE2][b % 4]
                for g in range(4):
                    waits = []
                    if g == 0:
                        waits.append(("sqkve", (T + 1) * 66))
                        if T == 0 and b == 2:
                            waits.append(("sinv", 1))
                        if T == 0 and b in (2, 3):
                            waits.append(("sw8e", 64))
                        if gb >= 4:
                            waits.append(("satte", gb - 3))
                    OP("tensor", waits, [],
                       lambda e, bank=bank, b=b, g=g: e.matmul(
                           bank[:, g * 128:(g + 1) * 128],
                           Qg(b * 512 + g * 128, 128), Kg(b * 512 + g * 128, 128),
                           start=(g == 0), stop=False))
                OP("tensor", [], [("satt", 1)],
                   lambda e, bank=bank: e.matmul(bank[:, :], I_bf(), mask4_bf(),
                                                 start=False, stop=True))

            def c_negmax(b):
                gb = T * NB + b
                bank = [psC, psD, psE, psE2][b % 4]
                b2 = b % 2
                mw = [("satt", gb + 1)]
                if gb >= 2:
                    mw.append(("satte", gb - 1))
                OP("vector", mw, [("sattm", 1)],
                   lambda e, bank=bank, b2=b2: e.tensor_reduce(
                       nmx[b2][:, :],
                       bank[:, :].rearrange("p (g n) -> p g n", g=4, n=128),
                       AX.X, ALU.max, negate=True))

            def c_exps(b):
                gb = T * NB + b
                bank = [psC, psD, psE, psE2][b % 4]
                b2 = b % 2
                for g in range(4):
                    waits = []
                    if g == 0:
                        waits.append(("sattm", gb + 1))
                        if gb >= 2:
                            waits.append(("sattp", gb - 1))
                    incs = [("satte", 1)] if g == 3 else []
                    OP("scalar", waits, incs,
                       lambda e, bank=bank, b2=b2, g=g: e.activation(
                           expb[b2][:, g * 128:(g + 1) * 128],
                           bank[:, g * 128:(g + 1) * 128], AF.Exp,
                           bias=nmx[b2][:, g:g + 1], scale=1.0,
                           accum_out=sumb[b2][:, g:g + 1]))

            def c_pmult(b):
                gb = T * NB + b
                b2 = b % 2
                rw = [("satte", gb + 1)]
                if gb >= 2:
                    rw.append(("sattt", gb - 1))
                OP("vector", rw, [("srcb", 1)],
                   lambda e, b2=b2: e.reciprocal(rb[b2][:, :], sumb[b2][:, :]))
                for g in range(4):
                    incs = [("sattp", 1)] if g == 3 else []
                    OP("vector", [("srcb", gb + 1)] if g == 0 else [], incs,
                       lambda e, b2=b2, g=g: e.tensor_scalar(
                           Pb(b2)[:, g * 128:(g + 1) * 128],
                           expb[b2][:, g * 128:(g + 1) * 128],
                           rb[b2][:, g:g + 1], None, ALU.mult))

            def c_pt(b):
                gb = T * NB + b
                b2 = b % 2
                ebank = psA if b % 2 == 0 else psB
                for g in range(4):
                    waits = []
                    if g == 0:
                        waits.append(("sattp", gb + 1))
                        if gb >= 2:
                            waits.append(("sattpe", gb - 1))
                    incs = [("sattt", 1)] if g == 3 else []
                    OP("tensor", waits, incs,
                       lambda e, eb=ebank, b2=b2, g=g: e.matmul(
                           eb[:, g * 128:(g + 1) * 128],
                           Pb(b2)[:, g * 128:(g + 1) * 128], I_bf(),
                           start=True, stop=True))

            def c_ptevac(b):
                gb = T * NB + b
                b2 = b % 2
                ebank = psA if b % 2 == 0 else psB
                pw = [("sattt", gb + 1)]
                if gb >= 2:
                    pw.append(("sattrs", gb - 1))
                OP("vector", pw, [("sattpe", 1)],
                   lambda e, eb=ebank, b2=b2: e.tensor_copy(Ptb(b2), eb[:, :]))

            def c_rsmm(b):
                gb = T * NB + b
                b2 = b % 2
                for g in range(4):
                    waits = []
                    if g == 0:
                        waits.append(("sattpe", gb + 1))
                        if gb >= 1:
                            waits.append(("sattre", gb))
                    incs = [("sattrs", 1)] if g == 3 else []
                    OP("tensor", waits, incs,
                       lambda e, b2=b2, g=g: e.matmul(
                           psF[0:1, g * 128:(g + 1) * 128], ones_bf[:, :],
                           Ptb(b2)[:, g * 128:(g + 1) * 128], start=True, stop=True))

            def c_rsevac(b):
                gb = T * NB + b
                b2 = b % 2
                rdst = rsst(b2).rearrange("p (q g t) -> p g q t", q=16, g=4, t=8)
                rsrc = psF[0:1, :].rearrange("p (g q t) -> p g q t", g=4, q=16, t=8)
                ew = [("sattrs", gb + 1)]
                if gb >= 2:
                    ew.append(("srs16", 16 * (gb - 1)))
                OP("scalar", ew, [("sattre", 1)],
                   lambda e, rdst=rdst, rsrc=rsrc: e.activation(rdst, rsrc, AF.Copy))
                rsv = rsst(b2).rearrange("p (q gt) -> p q gt", q=16, gt=32)
                OP("sync", [("sattre", gb + 1)], [("srs16", 16)],
                   lambda e, rsv=rsv, b=b: e.dma_start(out=rs16(b * 32, 32), in_=rsv))

            for it in range(NB + 2):
                if it < NB:
                    c_qk(it)
                    c_negmax(it)
                    c_exps(it)
                if 0 <= it - 1 < NB:
                    c_pmult(it - 1)
                    c_pt(it - 1)
                if 0 <= it - 2 < NB:
                    c_ptevac(it - 2)
                    c_rsmm(it - 2)
                    c_rsevac(it - 2)

            for h in range(16):
                for half in range(2):
                    ih = (T * 16 + h) * 2 + half
                    ebank = psE if ih % 2 == 0 else psE2
                    waits = [("srs16", 16 * (T + 1) * NB), ("sw8e", 64),
                             ("satte", (T + 1) * NB)]
                    if ih >= 2:
                        waits.append(("srsbe", ih - 1))
                    OP("tensor", waits, [("srsb", 1)],
                       lambda e, eb=ebank, h=h, half=half: e.matmul(
                           eb[:, :], hsel_bf(h), rs16(half * 512, 512),
                           start=True, stop=True))
                    OP("vector", [("srsb", ih + 1)], [("srsbe", 1)],
                       lambda e, eb=ebank, h=h, half=half: e.tensor_tensor(
                           attnT(h * 1024 + half * 512, 512), eb[:, :],
                           svT(half * 512, 512), ALU.mult))

            def f_load(j):
                jj = T * 16 + j
                if j < 2:
                    w = [("srsbe", (T + 1) * 32)]
                else:
                    w = [("syg", 2 * (jj - 1))]
                wo_blk = d_wout.rearrange("(k p) f -> p k f", p=128)[:, :, j * 128:(j + 1) * 128]
                OP("gpsimd", w, [("swo", 16)],
                   lambda e, blk=wo_blk, j=j: e.dma_start(out=wring[j % 2][:, :], in_=blk))

            def f_mm(j, half):
                ihalf = (T * 16 + j) * 2 + half
                i2 = ihalf % 2
                ybank = psA if i2 == 0 else psC
                gbank2 = psB if i2 == 0 else psD
                for k in range(16):
                    waits = []
                    if k == 0:
                        waits.append(("swo", 16 * (T * 16 + j + 1)))
                        waits.append(("srsbe", (T + 1) * 32))
                        if T == 1 and ihalf % 32 < 2:
                            waits.append(("ssg", 32))
                        if ihalf >= 2:
                            waits.append(("ssg", ihalf - 1))
                    OP("tensor", waits, [],
                       lambda e, j=j, k=k, half=half, yb=ybank: e.matmul(
                           yb[:, :], wring[j % 2][:, k * 128:(k + 1) * 128],
                           attnT(k * 1024 + half * 512, 512),
                           start=(k == 0), stop=(k == 15)))
                for k in range(16):
                    incs = [("syg", 1)] if k == 15 else []
                    OP("tensor", [], incs,
                       lambda e, j=j, k=k, half=half, gb2=gbank2: e.matmul(
                           gb2[:, :],
                           w8T[:, k * 2048 + j * 128:k * 2048 + (j + 1) * 128],
                           xqT8[:, k * 1024 + half * 512:k * 1024 + half * 512 + 512],
                           start=(k == 0), stop=(k == 15)))

            def f_silu_stt(j, half):
                ihalf = (T * 16 + j) * 2 + half
                i2 = ihalf % 2
                ybank = psA if i2 == 0 else psC
                gbank2 = psB if i2 == 0 else psD
                sw = [("syg", ihalf + 1)]
                if ihalf >= 2:
                    sw.append(("ssg", ihalf - 1))
                OP("scalar", sw, [("ssilu", 1)],
                   lambda e, j=j, gb2=gbank2, i2=i2: e.activation(
                       silu_sb[i2][:, :], gb2[:, :], AF.Silu,
                       bias=cols[:, 64 + j:65 + j], scale=tws[:, :]))
                gw = [("ssilu", ihalf + 1)]
                if ihalf >= 2:
                    gw.append(("sgt", ihalf - 1))
                OP("vector", gw, [("ssg", 1)],
                   lambda e, j=j, yb=ybank, i2=i2: e.scalar_tensor_tensor(
                       sg_sb[i2][:, :], yb[:, :], cols[:, 48 + j:49 + j],
                       silu_sb[i2][:, :], ALU.add, ALU.mult))

            def f_tgate(j, half):
                ihalf = (T * 16 + j) * 2 + half
                i2 = ihalf % 2
                tbank = psE if i2 == 0 else psE2
                for tg in range(4):
                    waits = []
                    if tg == 0:
                        waits.append(("ssg", ihalf + 1))
                        if ihalf % 32 < 2:
                            waits.append(("srsbe", (T + 1) * 32))
                        if ihalf >= 2:
                            waits.append(("sgte", ihalf - 1))
                    incs = [("sgt", 1)] if tg == 3 else []
                    OP("tensor", waits, incs,
                       lambda e, gb=tbank, tg=tg, i2=i2: e.transpose(
                           gb[:, tg * 128:(tg + 1) * 128],
                           sg_sb[i2][:, tg * 128:(tg + 1) * 128], sIf[:, :]))

            def f_out(j, half):
                ihalf = (T * 16 + j) * 2 + half
                i2 = ihalf % 2
                tbank = psE if i2 == 0 else psE2
                ow = [("sgt", ihalf + 1)]
                if ihalf >= 2:
                    ow.append(("sout", 16 * (ihalf - 1)))
                OP("vector", ow, [("sgte", 1)],
                   lambda e, gb=tbank, b2=i2: e.tensor_copy(outb[b2][:, :], gb[:, :]))
                rbase = T * 1024 + half * 512
                dst = d_out[rbase:rbase + 512, j * 128:(j + 1) * 128] \
                    .rearrange("(tg t) m -> t tg m", tg=4, t=128)
                srcb_ = outb[i2][:, :].rearrange("p (tg m) -> p tg m", tg=4, m=128)
                OP("gpsimd", [("sgte", ihalf + 1), ("spre", 64)], [("sout", 16)],
                   lambda e, dst=dst, src=srcb_: e.dma_start(out=dst, in_=src,
                                                             accum_op=ALU.add))

            prev = None
            for j in range(16):
                f_load(j)
                for half in range(2):
                    f_mm(j, half)
                    if prev is not None:
                        f_tgate(*prev)
                        f_out(*prev)
                    f_silu_stt(j, half)
                    prev = (j, half)
            f_tgate(*prev)
            f_out(*prev)
            prev = None

        OP("sync", [("sout", 16 * 64)], [], lambda e: e.nop())

        P.check()
        blk = es.enter_context(nc.Block())
        for engname in ENGS:
            lst = P.ops[engname]
            def make(lst):
                def emit(eng):
                    for _, _, fn in lst:
                        fn(eng)
                return emit
            getattr(blk, engname)(make(lst))

    return nc


_NC_CACHE = None


def kernel(L0_attn_x, L0_attn_Wqkv, L0_attn_bqkv, L0_attn_Wout, L0_attn_bout,
           L0_ffn_weight, L0_ffn_weight_scale, L0_ffn_bias):
    global _NC_CACHE
    x = np.ascontiguousarray(np.asarray(L0_attn_x, dtype=np.float32)).reshape(16384, 2048)
    wqkv = np.ascontiguousarray(np.asarray(L0_attn_Wqkv, dtype=np.float32)).reshape(2048, 6144)
    bqkv = np.ascontiguousarray(np.asarray(L0_attn_bqkv, dtype=np.float32)).reshape(48, 128)
    wout = np.ascontiguousarray(np.asarray(L0_attn_Wout, dtype=np.float32)).reshape(2048, 2048)
    bout = np.ascontiguousarray(np.asarray(L0_attn_bout, dtype=np.float32)).reshape(16, 128)
    wffn = np.ascontiguousarray(np.asarray(L0_ffn_weight, dtype=np.float32)).reshape(2048, 2048)
    bffn = np.ascontiguousarray(np.asarray(L0_ffn_bias, dtype=np.float32)).reshape(16, 128)
    ws = np.asarray(L0_ffn_weight_scale, dtype=np.float32).reshape(1, 1)

    ident = np.eye(128, dtype=np.float32)
    m = np.full((128, 128), np.float32(-998244352.0), dtype=np.float32)
    for qh in range(16):
        for tq in range(8):
            for kh in range(qh + 1):
                m[qh * 8 + tq, kh * 8 + tq] = 0.0
    mask4 = np.tile(m, (1, 4))
    hsel = np.zeros((16, 2048), dtype=np.float32)
    for h in range(16):
        hsel[h, h * 128:(h + 1) * 128] = 1.0

    if _NC_CACHE is None:
        _NC_CACHE = build_nc()
    nc = _NC_CACHE

    in_maps = []
    for c in range(NCORES):
        in_maps.append({
            "x": np.ascontiguousarray(x[c * TOK:(c + 1) * TOK, :]),
            "wqkv": wqkv, "bqkv": bqkv, "wout": wout, "bout": bout,
            "wffn": wffn, "bffn": bffn, "wscale": ws,
            "ident": ident, "mask4": mask4, "hsel": hsel,
        })
    res = run_bass_kernel_spmd(nc, in_maps, core_ids=list(range(NCORES)))
    out = np.concatenate([res.results[c]["out"] for c in range(NCORES)], axis=0)
    return np.ascontiguousarray(out.reshape(4, 4096, 2048).astype(np.float32, copy=False))


if __name__ == "__main__":
    nc = build_nc()
    print("build ok; instructions:",
          sum(1 for _ in nc.all_instructions()) if hasattr(nc, "all_instructions") else "?")


# version 09
# speedup vs baseline: 1.2264x; 1.0604x over previous
import numpy as np
import concourse.bass as bass
import concourse.mybir as mybir
from concourse.bass_utils import run_bass_kernel_spmd
from contextlib import ExitStack

F32 = mybir.dt.float32
BF16 = mybir.dt.bfloat16
FP8 = mybir.dt.float8e4
AF = mybir.ActivationFunctionType
ALU = mybir.AluOpType
AX = mybir.AxisListType

NCORES = 8
TOK = 2048
D = 2048
NT = 2
TT = 1024
NCH = 16
NB = 32
C_INV = float(1.0 / np.sqrt(np.float32(128.0)))
ENGS = ("sync", "tensor", "vector", "scalar", "gpsimd")


class Prog:
    def __init__(self):
        self.ops = {e: [] for e in ENGS}
        self.cnt = {}

    def op(self, eng, waits, incs, fn):
        for s, a in incs:
            self.cnt[s] = self.cnt.get(s, 0) + a
        for s, v in waits:
            self.cnt.setdefault(s, 0)
        self.ops[eng].append((tuple(waits), tuple(incs), fn))

    def check(self):
        pos = {e: 0 for e in ENGS}
        cur = {s: 0 for s in self.cnt}
        progress = True
        while progress:
            progress = False
            for e in ENGS:
                while pos[e] < len(self.ops[e]):
                    waits, incs, _ = self.ops[e][pos[e]]
                    if all(cur.get(s, 0) >= v for s, v in waits):
                        for s, a in incs:
                            cur[s] = cur.get(s, 0) + a
                        pos[e] += 1
                        progress = True
                    else:
                        break
        stuck = {e: pos[e] for e in ENGS if pos[e] < len(self.ops[e])}
        if stuck:
            det = {}
            for e in stuck:
                waits, _, _ = self.ops[e][pos[e]]
                det[e] = [(s, v, cur.get(s, 0)) for s, v in waits]
            raise RuntimeError(f"deadlock: stuck={stuck} waits={det}")


def build_nc():
    nc = bass.Bass("TRN2", target_bir_lowering=False)

    d_x = nc.dram_tensor("x", [TOK, D], F32, kind="ExternalInput").ap()
    d_wqkv = nc.dram_tensor("wqkv", [D, 6144], F32, kind="ExternalInput").ap()
    d_bqkv = nc.dram_tensor("bqkv", [48, 128], F32, kind="ExternalInput").ap()
    d_wout = nc.dram_tensor("wout", [D, D], F32, kind="ExternalInput").ap()
    d_bout = nc.dram_tensor("bout", [16, 128], F32, kind="ExternalInput").ap()
    d_wffn = nc.dram_tensor("wffn", [D, D], F32, kind="ExternalInput").ap()
    d_bffn = nc.dram_tensor("bffn", [16, 128], F32, kind="ExternalInput").ap()
    d_ws = nc.dram_tensor("wscale", [1, 1], F32, kind="ExternalInput").ap()
    d_ident = nc.dram_tensor("ident", [128, 128], F32, kind="ExternalInput").ap()
    d_mask4 = nc.dram_tensor("mask4", [128, 512], F32, kind="ExternalInput").ap()
    d_hsel = nc.dram_tensor("hsel", [16, 2048], F32, kind="ExternalInput").ap()
    d_out = nc.dram_tensor("out", [TOK, D], F32, kind="ExternalOutput").ap()
    t_ccin = nc.dram_tensor("cc_in", [1, 1], F32)
    t_ccout = nc.dram_tensor("cc_out", [1, 1], F32)
    d_ccin = t_ccin.ap()
    d_ccout = t_ccout.ap()

    es = ExitStack()
    SBt = lambda n, sh, dt=F32: es.enter_context(nc.sbuf_tensor(n, sh, dt))
    PSt = lambda n, sh: es.enter_context(nc.psum_tensor(n, sh, F32))

    with es:
        AR = SBt("AR", [128, 55936], BF16)
        def ar(lo, n, p=128):
            return AR[0:p, lo:lo + n]
        I_bf = lambda: ar(0, 128)
        wvsum = lambda k: ar(128 + k * 128, 128)
        mask4_bf = lambda: ar(2176, 512)
        hsel_bf = lambda h: ar(2688 + h * 128, 128, 16)
        rs16 = lambda c0, n: ar(4736 + c0, n, 16)
        xnT = lambda k, c0, n: ar(5760 + k * 1024 + c0, n)
        Pb = lambda b2: ar(5760 + b2 * 512, 512)
        Ptb = lambda b2: ar(6784 + b2 * 512, 512)
        rsst = lambda b2: ar(7808 + b2 * 512, 512, 1)
        attnT = lambda c0, n: ar(5760 + c0, n)
        Qg = lambda c0, n: ar(22144 + c0, n)
        Kg = lambda c0, n: ar(38528 + c0, n)
        svT = lambda c0, n: ar(54912 + c0, n)

        xb = [SBt("xb0", [128, D]), SBt("xb1", [128, D])]
        xnb = SBt("xnb", [128, D])
        wring = [SBt("wr0", [128, 2048], BF16), SBt("wr1", [128, 2048], BF16)]
        xqT8 = SBt("xqT8", [128, 16 * 1024], FP8)
        w8T = SBt("w8T", [128, 16 * 2048], FP8)
        expb = [SBt("expb0", [128, 512]), SBt("expb1", [128, 512])]
        sg_sb = [SBt("sg0", [128, 512]), SBt("sg1", [128, 512])]
        silu_sb = expb
        outb = [SBt("outb0", [128, 512]), SBt("outb1", [128, 512])]
        sIf = SBt("sIf", [128, 128])
        stats_m = SBt("st_m", [128, 16])
        stats_r = SBt("st_r", [128, 16])
        stats_b = SBt("st_b", [128, 16])
        stats_s1 = SBt("st_s1", [128, 16])
        stats_s2 = SBt("st_s2", [128, 16])
        stats_mx = SBt("st_mx", [128, 16])
        stats_mn = SBt("st_mn", [128, 16])
        tmp1 = SBt("tmp1", [128, 16])
        tmp2 = SBt("tmp2", [128, 16])
        gmax = SBt("gmax", [128, 1])
        gsc = SBt("gsc", [1, 136])
        inv2s = SBt("inv2s", [128, 1])
        tws = SBt("tws", [128, 1])
        onesrow = SBt("onesrow", [1, 128])
        ones_bf = SBt("ones_bf", [128, 1], BF16)
        cols = SBt("cols", [128, 80])
        cbq = SBt("cbq", [128, 16])
        bv_col = SBt("bv_col", [128, 1])
        nmx = [SBt("nmx0", [128, 4]), SBt("nmx1", [128, 4])]
        sumb = [SBt("sumb0", [128, 4]), SBt("sumb1", [128, 4])]
        rb = [SBt("rb0", [128, 4]), SBt("rb1", [128, 4])]

        psA = PSt("psA", [128, 512])
        psB = PSt("psB", [128, 512])
        psC = PSt("psC", [128, 512])
        psD = PSt("psD", [128, 512])
        psE = PSt("psE", [128, 512])
        psE2 = PSt("psE2", [128, 512])
        psF = PSt("psF", [1, 512])

        P = Prog()
        sems = {}
        for n in ["sld", "spc", "sbias", "swv", "swvs", "sx", "sxa", "sxv",
                  "ssta", "sva", "ssq", "sfin", "sgt1", "sgr", "scc1",
                  "scc", "sccb", "srct", "sbc", "sinv", "sxn", "stp", "stpe",
                  "swq", "sqkv", "sqkve", "satt", "sattm", "satte", "sattp",
                  "sattt", "sattpe", "sattrs", "sattre", "srs16", "swf",
                  "sw8t", "sw8e", "srsb", "srsbe", "swo", "syg", "ssilu",
                  "ssg", "sgt", "sgte", "sout", "spre", "stwc", "sfz", "svr", "srcb", "swvp", "swvt"]:
            sems[n] = es.enter_context(nc.semaphore(n))

        def OP(eng, waits, incs, build):
            waits = [(s, v) for s, v in waits if v > 0]
            def fn(e):
                for s, v in waits:
                    e.wait_ge(sems[s], v)
                inst = build(e)
                for s, a in incs:
                    inst = inst.then_inc(sems[s], a)
            P.op(eng, waits, incs, fn)

        OP("sync", [], [("sld", 16)], lambda e: e.dma_start(out=sIf[:, :], in_=d_ident))
        OP("sync", [], [("sld", 16)], lambda e: e.dma_start(out=xb[0][:, 0:512], in_=d_mask4))
        OP("sync", [], [("sld", 16)], lambda e: e.dma_start(out=xnb[0:16, :], in_=d_hsel))
        OP("sync", [], [("sld", 16)], lambda e: e.dma_start(out=xb[1][0:48, 0:128], in_=d_bqkv))
        OP("sync", [], [("sld", 16)], lambda e: e.dma_start(out=xb[1][0:16, 128:256], in_=d_bout))
        OP("sync", [], [("sld", 16)], lambda e: e.dma_start(out=xb[1][0:16, 256:384], in_=d_bffn))
        OP("sync", [], [("sld", 16)], lambda e: e.dma_start(out=gsc[0:1, 0:1], in_=d_ws))
        def prefill():
            for i in range(4):
                OP("gpsimd", [], [("spre", 16)],
                   lambda e, i=i: e.dma_start(out=d_out[i * 512:(i + 1) * 512, :],
                                              in_=d_x[i * 512:(i + 1) * 512, :]))
        NPC = 20
        OP("vector", [("sld", 112)], [("spc", 1)], lambda e: e.tensor_copy(I_bf(), sIf[:, :]))
        OP("vector", [], [("spc", 1)], lambda e: e.tensor_copy(mask4_bf(), xb[0][:, 0:512]))
        for h in range(16):
            OP("vector", [], [("spc", 1)],
               lambda e, h=h: e.tensor_copy(hsel_bf(h), xnb[0:16, h * 128:(h + 1) * 128]))
        OP("vector", [], [("spc", 1)], lambda e: e.memset(ones_bf[:, :], 1.0))
        OP("vector", [], [("spc", 1)], lambda e: e.memset(onesrow[:, :], 1.0))
        OP("tensor", [("spc", NPC)], [("sbias", 1)],
           lambda e: e.matmul(psE[:, 0:48], xb[1][0:48, 0:128], sIf[0:48, 0:48],
                              start=True, stop=True))
        OP("tensor", [], [("sbias", 1)],
           lambda e: e.matmul(psE[:, 48:64], xb[1][0:16, 128:256], sIf[0:16, 0:16],
                              start=True, stop=True))
        OP("tensor", [], [("sbias", 1)],
           lambda e: e.matmul(psE[:, 64:80], xb[1][0:16, 256:384], sIf[0:16, 0:16],
                              start=True, stop=True))
        OP("vector", [("sbias", 3)], [("sbias", 1)],
           lambda e: e.tensor_copy(cols[:, :], psE[:, 0:80]))
        bq_view = cols[:, 0:48].rearrange("p (h j) -> p j h", h=16, j=3)
        OP("vector", [("sbias", 4)], [("sbias", 1)],
           lambda e: e.tensor_scalar(cbq[:, :], bq_view[:, 0, :], C_INV, None, ALU.mult))
        OP("vector", [("sbias", 5)], [("sbias", 1)],
           lambda e: e.tensor_reduce(bv_col[:, :], bq_view[:, 2, :], AX.X, ALU.add))
        OP("tensor", [("sbias", 6)], [("stwc", 1)],
           lambda e: e.matmul(psE[:, 80:81], onesrow[:, :], gsc[0:1, 0:1],
                              start=True, stop=True))
        OP("vector", [("stwc", 1)], [("stwc", 1)],
           lambda e: e.tensor_scalar(tws[:, :], psE[:, 80:81], 2.0, None, ALU.mult))

        wq_v = d_wqkv.rearrange("m (h j d) -> m h j d", h=16, j=3, d=128)
        for k in range(16):
            w = [("swvp", k - 1)] if k >= 2 else []
            OP("gpsimd", w, [("swv", 16)],
               lambda e, k=k: e.dma_start(out=wring[k % 2][:, :],
                                          in_=wq_v[k * 128:(k + 1) * 128, :, 2, :]))
            for h in range(16):
                waits = []
                if h == 0:
                    waits.append(("swv", 16 * (k + 1)))
                    if k == 0:
                        waits.append(("spc", NPC))
                    if k >= 1:
                        waits.append(("svr", k))
                incs = [("swvp", 1)] if h == 15 else []
                OP("tensor", waits, incs,
                   lambda e, k=k, h=h: e.matmul(
                       psE2[:, 0:128], I_bf(), wring[k % 2][:, h * 128:(h + 1) * 128],
                       start=(h == 0), stop=(h == 15)))

        def wv_evac(k):
            OP("vector", [("swvp", k + 1)], [("svr", 1)],
               lambda e, k=k: e.tensor_copy(wvsum(k), psE2[:, 0:128]))

        def wf_load(oc):
            if oc < 2:
                w = [("swvp", 16)]
            else:
                w = [("sw8e", 4 * (oc - 1))]
            OP("gpsimd", w, [("swf", 16)],
               lambda e, oc=oc: e.dma_start(out=wring[oc % 2][:, :],
                                            in_=d_wffn[oc * 128:(oc + 1) * 128, :]))
        wf_load(0)
        wf_load(1)

        def w8t_oc(oc):
            for r in range(4):
                gri = oc * 4 + r
                if oc < 2:
                    ebank = psC if gri % 2 == 0 else psD
                else:
                    ebank = psE if gri % 2 == 0 else psE2
                for i in range(4):
                    ic = r * 4 + i
                    waits = []
                    if i == 0:
                        waits.append(("swf", 16 * (oc + 1)))
                        if gri == 0:
                            waits.append(("sbias", 4))
                            waits.append(("stwc", 2))
                        if gri == 8:
                            waits.append(("sinv", 1))
                            waits.append(("svr", 16))
                        if gri >= 2:
                            waits.append(("sw8e", gri - 1))
                    incs = [("sw8t", 1)] if i == 3 else []
                    OP("tensor", waits, incs,
                       lambda e, eb=ebank, oc=oc, ic=ic, i=i: e.matmul(
                           eb[:, i * 128:(i + 1) * 128],
                           wring[oc % 2][:, ic * 128:(ic + 1) * 128], I_bf(),
                           start=True, stop=True))
                for i in range(4):
                    ic = r * 4 + i
                    waits = [("sw8t", gri + 1)] if i == 0 else []
                    incs = [("sw8e", 1)] if i == 3 else []
                    OP("vector", waits, incs,
                       lambda e, eb=ebank, oc=oc, ic=ic, i=i: e.tensor_copy(
                           w8T[:, ic * 2048 + oc * 128:ic * 2048 + (oc + 1) * 128],
                           eb[:, i * 128:(i + 1) * 128]))

        for c in range(NCH):
            if c == 0:
                w = [("spc", NPC)]
            elif c == 1:
                w = [("sbias", 3)]
            else:
                w = [("ssta", c - 1), ("sxv", c - 1)]
            OP("sync", w, [("sx", 16)],
               lambda e, c=c: e.dma_start(out=xb[c % 2][:, :], in_=d_x[c * 128:(c + 1) * 128, :]))
            xc = xb[c % 2]
            aw = [("sx", 16 * (c + 1))]
            if c == 0:
                aw.append(("spc", NPC))
            OP("scalar", aw, [],
               lambda e, xc=xc, c=c: e.activation(xnb[:, :], xc[:, :], AF.Square,
                                                  accum_out=stats_s2[:, c:c + 1]))
            OP("scalar", [], [("ssta", 1)],
               lambda e, xc=xc, c=c: e.activation(xnb[:, :], xc[:, :], AF.Identity,
                                                  accum_out=stats_s1[:, c:c + 1]))
            OP("vector", [("sx", 16 * (c + 1))], [],
               lambda e, xc=xc, c=c: e.tensor_reduce(stats_mx[:, c:c + 1], xc[:, :],
                                                     AX.X, ALU.max))
            OP("vector", [], [("sxv", 1)],
               lambda e, xc=xc, c=c: e.tensor_reduce(stats_mn[:, c:c + 1], xc[:, :],
                                                     AX.X, ALU.min))
            if c >= 3:
                wv_evac(c - 3)
        for _k in (13, 14, 15):
            wv_evac(_k)
        w8t_oc(0)
        w8t_oc(1)

        _fz = [0]
        def VC(waits, incs, build):
            assert not incs
            OP("vector", waits + [("sfz", _fz[0])], [("sfz", 1)], build)
            _fz[0] += 1
            return _fz[0]
        VC([("ssta", NCH), ("sxv", NCH)], [],
           lambda e: e.tensor_scalar(stats_m[:, :], stats_s1[:, :],
                                     1.0 / 2048.0, None, ALU.mult))
        VC([], [], lambda e: e.tensor_mul(tmp2[:, :], stats_m[:, :], stats_m[:, :]))
        VC([], [], lambda e: e.tensor_scalar(tmp1[:, :], stats_s2[:, :],
                                             1.0 / 2048.0, None, ALU.mult))
        VC([], [], lambda e: e.tensor_sub(tmp1[:, :], tmp1[:, :], tmp2[:, :]))
        _va = VC([], [], lambda e: e.tensor_scalar(tmp1[:, :], tmp1[:, :], 1e-5, None, ALU.add))
        OP("scalar", [("sfz", _va)], [("ssq", 1)],
           lambda e: e.activation(tmp2[:, :], tmp1[:, :], AF.Sqrt))
        VC([("ssq", 1)], [], lambda e: e.reciprocal(stats_r[:, :], tmp2[:, :]))
        VC([], [], lambda e: e.tensor_mul(tmp2[:, :], stats_r[:, :], stats_r[:, :]))
        VC([], [], lambda e: e.tensor_mul(tmp2[:, :], tmp2[:, :], tmp1[:, :]))
        VC([], [], lambda e: e.tensor_scalar(tmp2[:, :], tmp2[:, :], -0.5, 1.5,
                                             ALU.mult, ALU.add))
        VC([], [], lambda e: e.tensor_mul(stats_r[:, :], stats_r[:, :], tmp2[:, :]))
        VC([], [], lambda e: e.tensor_mul(tmp2[:, :], stats_m[:, :], stats_r[:, :]))
        VC([], [], lambda e: e.tensor_scalar(stats_b[:, :], tmp2[:, :], -1.0, None,
                                             ALU.mult))
        FZ_STATS = _fz[0]
        VC([], [], lambda e: e.tensor_sub(tmp2[:, :], stats_mx[:, :], stats_m[:, :]))
        VC([], [], lambda e: e.tensor_sub(tmp1[:, :], stats_m[:, :], stats_mn[:, :]))
        VC([], [], lambda e: e.tensor_max(tmp2[:, :], tmp2[:, :], tmp1[:, :]))
        VC([], [], lambda e: e.tensor_mul(tmp2[:, :], tmp2[:, :], stats_r[:, :]))
        _fin = VC([], [], lambda e: e.tensor_reduce(gmax[:, :], tmp2[:, :], AX.X, ALU.max))
        FZ_FIN = _fin

        OP("tensor", [("sfz", FZ_FIN)], [("sgt1", 1)],
           lambda e: e.transpose(psF[0:1, 0:128], gmax[:, :], sIf[:, :]))
        VC([("sgt1", 1)], [], lambda e: e.tensor_copy(gsc[0:1, 8:136], psF[0:1, 0:128]))
        _gr = VC([], [],
           lambda e: e.tensor_reduce(gsc[0:1, 1:2], gsc[0:1, 8:136], AX.X, ALU.max))
        OP("gpsimd", [("sfz", _gr)], [("scc1", 16)],
           lambda e: e.dma_start(out=d_ccin, in_=gsc[0:1, 1:2]))
        OP("gpsimd", [("scc1", 16)], [("scc", 1)],
           lambda e: e.collective_compute("AllReduce", ALU.max,
                                          replica_groups=[list(range(NCORES))],
                                          ins=[t_ccin.ap().opt()],
                                          outs=[t_ccout.ap().opt()]))
        OP("sync", [("scc", 1)], [("sccb", 16)],
           lambda e: e.dma_start(out=gsc[0:1, 2:3], in_=d_ccout))
        VC([("sccb", 16)], [],
           lambda e: e.tensor_scalar(gsc[0:1, 3:4], gsc[0:1, 2:3], 1.0 / 448.0, 1e-7,
                                     ALU.mult, ALU.max))
        VC([], [],
           lambda e: e.tensor_scalar(gsc[0:1, 4:5], gsc[0:1, 3:4], 2.0, None, ALU.mult))
        _rct = VC([], [],
           lambda e: e.reciprocal(gsc[0:1, 5:6], gsc[0:1, 4:5]))
        OP("tensor", [("sfz", _rct), ("stwc", 2), ("sbias", 4)], [("sbc", 1)],
           lambda e: e.matmul(psE[:, 81:82], onesrow[:, :], gsc[0:1, 5:6],
                              start=True, stop=True))
        OP("vector", [("sbc", 1)], [("sinv", 1)],
           lambda e: e.tensor_copy(inv2s[:, :], psE[:, 81:82]))

        for T in range(NT):
            for c8 in range(8):
                gc = T * 8 + c8
                widx = NCH + gc
                if gc >= 2:
                    w = [("sxn", gc - 1)]
                else:
                    w = [("sxv", NCH), ("ssta", NCH)]
                OP("sync", w, [("sx", 16)],
                   lambda e, c=gc, b=xb[gc % 2]: e.dma_start(
                       out=b[:, :], in_=d_x[c * 128:(c + 1) * 128, :]))
                aw = [("sx", 16 * (widx + 1)), ("sfz", FZ_STATS)]
                if gc >= 1:
                    aw.append(("stp", 4 * gc))
                OP("scalar", aw, [("sxn", 1)],
                   lambda e, c=gc, b=xb[gc % 2]: e.activation(
                       xnb[:, :], b[:, :], AF.Identity,
                       bias=stats_b[:, c:c + 1], scale=stats_r[:, c:c + 1]))
                for r in range(4):
                    gri = gc * 4 + r
                    bank = psC if gri % 2 == 0 else psD
                    for i in range(4):
                        k = r * 4 + i
                        waits = []
                        if i == 0:
                            waits.append(("sxn", gc + 1))
                            if gri >= 2:
                                waits.append(("stpe", gri - 1))
                            if T == 1 and r < 2 and c8 == 0:
                                waits.append(("ssg", 32))
                            if T == 0 and r < 2 and c8 == 0:
                                waits.append(("sw8e", 8))
                        incs = [("stp", 1)] if i == 3 else []
                        OP("tensor", waits, incs,
                           lambda e, bank=bank, i=i, k=k: e.transpose(
                               bank[:, i * 128:(i + 1) * 128],
                               xnb[:, k * 128:(k + 1) * 128], sIf[:, :]))
                    for i in range(4):
                        k = r * 4 + i
                        waits = []
                        if i == 0:
                            waits.append(("stp", gri + 1))
                            waits.append(("sinv", 1))
                        OP("vector", waits, [],
                           lambda e, bank=bank, i=i, k=k, c8=c8: e.tensor_copy(
                               xnT(k, c8 * 128, 128), bank[:, i * 128:(i + 1) * 128]))
                        incs = [("stpe", 1)] if i == 3 else []
                        OP("vector", [], incs,
                           lambda e, bank=bank, i=i, k=k, c8=c8: e.tensor_scalar(
                               xqT8[:, k * 1024 + c8 * 128:k * 1024 + c8 * 128 + 128],
                               bank[:, i * 128:(i + 1) * 128],
                               inv2s[:, :], None, ALU.mult))
                if T == 0:
                    if 2 * c8 + 2 < 16:
                        wf_load(2 * c8 + 2)
                        w8t_oc(2 * c8 + 2)
                    if 2 * c8 + 3 < 16:
                        wf_load(2 * c8 + 3)
                        w8t_oc(2 * c8 + 3)

            for j in range(33):
                jj = T * 33 + j
                if j < 32:
                    h = j % 16
                    colbase = (3 * h + (0 if j < 16 else 1)) * 128
                    wq_blk = d_wqkv.rearrange("(k p) f -> p k f", p=128)[:, :, colbase:colbase + 128]
                    if jj < 2:
                        w = [("sw8t", 64)]
                    else:
                        w = [("sqkv", 2 * (jj - 1))]
                    OP("gpsimd", w, [("swq", 16)],
                       lambda e, blk=wq_blk, j=j: e.dma_start(out=wring[j % 2][:, :], in_=blk))
                for half in range(2):
                    ihalf = jj * 2 + half
                    pbank = psA if ihalf % 2 == 0 else psB
                    for k in range(16):
                        waits = []
                        if k == 0:
                            if j < 32:
                                waits.append(("swq", 16 * (T * 32 + j + 1)))
                            else:
                                waits.append(("svr", 16))
                            if half == 0 and j == 0:
                                waits.append(("stpe", (T + 1) * 32))
                            if T == 1 and jj == 33 and half == 0 and k == 0:
                                waits.append(("ssg", 32))
                            if ihalf >= 2:
                                waits.append(("sqkve", ihalf - 1))
                        lhs = (wring[j % 2][:, k * 128:(k + 1) * 128] if j < 32
                               else wvsum(k))
                        incs = [("sqkv", 1)] if k == 15 else []
                        OP("tensor", waits, incs,
                           lambda e, p=pbank, lhs=lhs, k=k, half=half: e.matmul(
                               p[:, :], lhs, xnT(k, half * 512, 512),
                               start=(k == 0), stop=(k == 15)))
                    if j < 32:
                        dstf = Qg(0, 16384) if j < 16 else Kg(0, 16384)
                        dst3 = dstf.rearrange("p (g hh t) -> p g hh t", g=128, hh=16, t=8)
                        dst = dst3[:, half * 64:(half + 1) * 64, h, :]
                        sc = C_INV if j < 16 else 1.0
                        bias = cbq[:, h:h + 1] if j < 16 else cols[:, 3 * h + 1:3 * h + 2]
                        OP("scalar", [("sqkv", ihalf + 1)], [("sqkve", 1)],
                           lambda e, p=pbank, dst=dst, sc=sc, bias=bias: e.activation(
                               dst, p[:, :], AF.Identity, bias=bias, scale=sc))
                    else:
                        OP("scalar", [("sqkv", ihalf + 1)], [("sqkve", 1)],
                           lambda e, p=pbank, half=half: e.activation(
                               svT(half * 512, 512), p[:, :], AF.Identity,
                               bias=bv_col[:, :], scale=1.0))

            if T == 0:
                prefill()

            def c_qk(b):
                gb = T * NB + b
                bank = [psC, psD, psE, psE2][b % 4]
                for g in range(4):
                    waits = []
                    if g == 0:
                        waits.append(("sqkve", (T + 1) * 66))
                        if T == 0 and b == 2:
                            waits.append(("sinv", 1))
                        if T == 0 and b in (2, 3):
                            waits.append(("sw8e", 64))
                        if gb >= 4:
                            waits.append(("satte", gb - 3))
                    OP("tensor", waits, [],
                       lambda e, bank=bank, b=b, g=g: e.matmul(
                           bank[:, g * 128:(g + 1) * 128],
                           Qg(b * 512 + g * 128, 128), Kg(b * 512 + g * 128, 128),
                           start=(g == 0), stop=False))
                OP("tensor", [], [("satt", 1)],
                   lambda e, bank=bank: e.matmul(bank[:, :], I_bf(), mask4_bf(),
                                                 start=False, stop=True))

            def c_negmax(b):
                gb = T * NB + b
                bank = [psC, psD, psE, psE2][b % 4]
                b2 = b % 2
                mw = [("satt", gb + 1)]
                if gb >= 2:
                    mw.append(("satte", gb - 1))
                OP("vector", mw, [("sattm", 1)],
                   lambda e, bank=bank, b2=b2: e.tensor_reduce(
                       nmx[b2][:, :],
                       bank[:, :].rearrange("p (g n) -> p g n", g=4, n=128),
                       AX.X, ALU.max, negate=True))

            def c_exps(b):
                gb = T * NB + b
                bank = [psC, psD, psE, psE2][b % 4]
                b2 = b % 2
                for g in range(4):
                    waits = []
                    if g == 0:
                        waits.append(("sattm", gb + 1))
                        if gb >= 2:
                            waits.append(("sattp", gb - 1))
                    incs = [("satte", 1)] if g == 3 else []
                    OP("scalar", waits, incs,
                       lambda e, bank=bank, b2=b2, g=g: e.activation(
                           expb[b2][:, g * 128:(g + 1) * 128],
                           bank[:, g * 128:(g + 1) * 128], AF.Exp,
                           bias=nmx[b2][:, g:g + 1], scale=1.0,
                           accum_out=sumb[b2][:, g:g + 1]))

            def c_pmult(b):
                gb = T * NB + b
                b2 = b % 2
                rw = [("satte", gb + 1)]
                if gb >= 2:
                    rw.append(("sattt", gb - 1))
                OP("vector", rw, [("srcb", 1)],
                   lambda e, b2=b2: e.reciprocal(rb[b2][:, :], sumb[b2][:, :]))
                for g in range(4):
                    incs = [("sattp", 1)] if g == 3 else []
                    OP("vector", [("srcb", gb + 1)] if g == 0 else [], incs,
                       lambda e, b2=b2, g=g: e.tensor_scalar(
                           Pb(b2)[:, g * 128:(g + 1) * 128],
                           expb[b2][:, g * 128:(g + 1) * 128],
                           rb[b2][:, g:g + 1], None, ALU.mult))

            def c_pt(b):
                gb = T * NB + b
                b2 = b % 2
                ebank = psA if b % 2 == 0 else psB
                for g in range(4):
                    waits = []
                    if g == 0:
                        waits.append(("sattp", gb + 1))
                        if gb >= 2:
                            waits.append(("sattpe", gb - 1))
                    incs = [("sattt", 1)] if g == 3 else []
                    OP("tensor", waits, incs,
                       lambda e, eb=ebank, b2=b2, g=g: e.matmul(
                           eb[:, g * 128:(g + 1) * 128],
                           Pb(b2)[:, g * 128:(g + 1) * 128], I_bf(),
                           start=True, stop=True))

            def c_ptevac(b):
                gb = T * NB + b
                b2 = b % 2
                ebank = psA if b % 2 == 0 else psB
                pw = [("sattt", gb + 1)]
                if gb >= 2:
                    pw.append(("sattrs", gb - 1))
                OP("vector", pw, [("sattpe", 1)],
                   lambda e, eb=ebank, b2=b2: e.tensor_copy(Ptb(b2), eb[:, :]))

            def c_rsmm(b):
                gb = T * NB + b
                b2 = b % 2
                for g in range(4):
                    waits = []
                    if g == 0:
                        waits.append(("sattpe", gb + 1))
                        if gb >= 1:
                            waits.append(("sattre", gb))
                    incs = [("sattrs", 1)] if g == 3 else []
                    OP("tensor", waits, incs,
                       lambda e, b2=b2, g=g: e.matmul(
                           psF[0:1, g * 128:(g + 1) * 128], ones_bf[:, :],
                           Ptb(b2)[:, g * 128:(g + 1) * 128], start=True, stop=True))

            def c_rsevac(b):
                gb = T * NB + b
                b2 = b % 2
                rdst = rsst(b2).rearrange("p (q g t) -> p g q t", q=16, g=4, t=8)
                rsrc = psF[0:1, :].rearrange("p (g q t) -> p g q t", g=4, q=16, t=8)
                ew = [("sattrs", gb + 1)]
                if gb >= 2:
                    ew.append(("srs16", 16 * (gb - 1)))
                OP("scalar", ew, [("sattre", 1)],
                   lambda e, rdst=rdst, rsrc=rsrc: e.activation(rdst, rsrc, AF.Copy))
                rsv = rsst(b2).rearrange("p (q gt) -> p q gt", q=16, gt=32)
                OP("sync", [("sattre", gb + 1)], [("srs16", 16)],
                   lambda e, rsv=rsv, b=b: e.dma_start(out=rs16(b * 32, 32), in_=rsv))

            for it in range(NB + 2):
                if it < NB:
                    c_qk(it)
                    c_negmax(it)
                    c_exps(it)
                if 0 <= it - 1 < NB:
                    c_pmult(it - 1)
                    c_pt(it - 1)
                if 0 <= it - 2 < NB:
                    c_ptevac(it - 2)
                    c_rsmm(it - 2)
                    c_rsevac(it - 2)

            for h in range(16):
                for half in range(2):
                    ih = (T * 16 + h) * 2 + half
                    ebank = psE if ih % 2 == 0 else psE2
                    waits = [("srs16", 16 * (T + 1) * NB), ("sw8e", 64),
                             ("satte", (T + 1) * NB)]
                    if ih >= 2:
                        waits.append(("srsbe", ih - 1))
                    OP("tensor", waits, [("srsb", 1)],
                       lambda e, eb=ebank, h=h, half=half: e.matmul(
                           eb[:, :], hsel_bf(h), rs16(half * 512, 512),
                           start=True, stop=True))
                    OP("vector", [("srsb", ih + 1)], [("srsbe", 1)],
                       lambda e, eb=ebank, h=h, half=half: e.tensor_tensor(
                           attnT(h * 1024 + half * 512, 512), eb[:, :],
                           svT(half * 512, 512), ALU.mult))

            def f_load(j):
                jj = T * 16 + j
                if j < 2:
                    w = [("srsbe", (T + 1) * 32)]
                else:
                    w = [("syg", 2 * (jj - 1))]
                wo_blk = d_wout.rearrange("(k p) f -> p k f", p=128)[:, :, j * 128:(j + 1) * 128]
                OP("gpsimd", w, [("swo", 16)],
                   lambda e, blk=wo_blk, j=j: e.dma_start(out=wring[j % 2][:, :], in_=blk))

            def f_mm(j, half):
                ihalf = (T * 16 + j) * 2 + half
                i2 = ihalf % 2
                ybank = psA if i2 == 0 else psC
                gbank2 = psB if i2 == 0 else psD
                for k in range(16):
                    waits = []
                    if k == 0:
                        waits.append(("swo", 16 * (T * 16 + j + 1)))
                        waits.append(("srsbe", (T + 1) * 32))
                        if T == 1 and ihalf % 32 < 2:
                            waits.append(("ssg", 32))
                        if ihalf >= 2:
                            waits.append(("ssg", ihalf - 1))
                    OP("tensor", waits, [],
                       lambda e, j=j, k=k, half=half, yb=ybank: e.matmul(
                           yb[:, :], wring[j % 2][:, k * 128:(k + 1) * 128],
                           attnT(k * 1024 + half * 512, 512),
                           start=(k == 0), stop=(k == 15)))
                for k in range(16):
                    incs = [("syg", 1)] if k == 15 else []
                    OP("tensor", [], incs,
                       lambda e, j=j, k=k, half=half, gb2=gbank2: e.matmul(
                           gb2[:, :],
                           w8T[:, k * 2048 + j * 128:k * 2048 + (j + 1) * 128],
                           xqT8[:, k * 1024 + half * 512:k * 1024 + half * 512 + 512],
                           start=(k == 0), stop=(k == 15)))

            def f_silu_stt(j, half):
                ihalf = (T * 16 + j) * 2 + half
                i2 = ihalf % 2
                ybank = psA if i2 == 0 else psC
                gbank2 = psB if i2 == 0 else psD
                sw = [("syg", ihalf + 1)]
                if ihalf >= 2:
                    sw.append(("ssg", ihalf - 1))
                OP("scalar", sw, [("ssilu", 1)],
                   lambda e, j=j, gb2=gbank2, i2=i2: e.activation(
                       silu_sb[i2][:, :], gb2[:, :], AF.Silu,
                       bias=cols[:, 64 + j:65 + j], scale=tws[:, :]))
                gw = [("ssilu", ihalf + 1)]
                if ihalf >= 2:
                    gw.append(("sgt", ihalf - 1))
                OP("vector", gw, [("ssg", 1)],
                   lambda e, j=j, yb=ybank, i2=i2: e.scalar_tensor_tensor(
                       sg_sb[i2][:, :], yb[:, :], cols[:, 48 + j:49 + j],
                       silu_sb[i2][:, :], ALU.add, ALU.mult))

            def f_tgate(j, half):
                ihalf = (T * 16 + j) * 2 + half
                i2 = ihalf % 2
                tbank = psE if i2 == 0 else psE2
                for tg in range(4):
                    waits = []
                    if tg == 0:
                        waits.append(("ssg", ihalf + 1))
                        if ihalf % 32 < 2:
                            waits.append(("srsbe", (T + 1) * 32))
                        if ihalf >= 2:
                            waits.append(("sgte", ihalf - 1))
                    incs = [("sgt", 1)] if tg == 3 else []
                    OP("tensor", waits, incs,
                       lambda e, gb=tbank, tg=tg, i2=i2: e.transpose(
                           gb[:, tg * 128:(tg + 1) * 128],
                           sg_sb[i2][:, tg * 128:(tg + 1) * 128], sIf[:, :]))

            def f_out(j, half):
                ihalf = (T * 16 + j) * 2 + half
                i2 = ihalf % 2
                tbank = psE if i2 == 0 else psE2
                ow = [("sgt", ihalf + 1)]
                if ihalf >= 2:
                    ow.append(("sout", 16 * (ihalf - 1)))
                OP("vector", ow, [("sgte", 1)],
                   lambda e, gb=tbank, b2=i2: e.tensor_copy(outb[b2][:, :], gb[:, :]))
                rbase = T * 1024 + half * 512
                dst = d_out[rbase:rbase + 512, j * 128:(j + 1) * 128] \
                    .rearrange("(tg t) m -> t tg m", tg=4, t=128)
                srcb_ = outb[i2][:, :].rearrange("p (tg m) -> p tg m", tg=4, m=128)
                OP("gpsimd", [("sgte", ihalf + 1), ("spre", 64)], [("sout", 16)],
                   lambda e, dst=dst, src=srcb_: e.dma_start(out=dst, in_=src,
                                                             accum_op=ALU.add))

            prev = None
            for j in range(16):
                f_load(j)
                for half in range(2):
                    f_mm(j, half)
                    if prev is not None:
                        f_tgate(*prev)
                        f_out(*prev)
                    f_silu_stt(j, half)
                    prev = (j, half)
            f_tgate(*prev)
            f_out(*prev)
            prev = None

        OP("sync", [("sout", 16 * 64)], [], lambda e: e.nop())

        P.check()
        blk = es.enter_context(nc.Block())
        for engname in ENGS:
            lst = P.ops[engname]
            def make(lst):
                def emit(eng):
                    for _, _, fn in lst:
                        fn(eng)
                return emit
            getattr(blk, engname)(make(lst))

    return nc


_NC_CACHE = None


def kernel(L0_attn_x, L0_attn_Wqkv, L0_attn_bqkv, L0_attn_Wout, L0_attn_bout,
           L0_ffn_weight, L0_ffn_weight_scale, L0_ffn_bias):
    global _NC_CACHE
    x = np.ascontiguousarray(np.asarray(L0_attn_x, dtype=np.float32)).reshape(16384, 2048)
    wqkv = np.ascontiguousarray(np.asarray(L0_attn_Wqkv, dtype=np.float32)).reshape(2048, 6144)
    bqkv = np.ascontiguousarray(np.asarray(L0_attn_bqkv, dtype=np.float32)).reshape(48, 128)
    wout = np.ascontiguousarray(np.asarray(L0_attn_Wout, dtype=np.float32)).reshape(2048, 2048)
    bout = np.ascontiguousarray(np.asarray(L0_attn_bout, dtype=np.float32)).reshape(16, 128)
    wffn = np.ascontiguousarray(np.asarray(L0_ffn_weight, dtype=np.float32)).reshape(2048, 2048)
    bffn = np.ascontiguousarray(np.asarray(L0_ffn_bias, dtype=np.float32)).reshape(16, 128)
    ws = np.asarray(L0_ffn_weight_scale, dtype=np.float32).reshape(1, 1)

    ident = np.eye(128, dtype=np.float32)
    m = np.full((128, 128), np.float32(-998244352.0), dtype=np.float32)
    for qh in range(16):
        for tq in range(8):
            for kh in range(qh + 1):
                m[qh * 8 + tq, kh * 8 + tq] = 0.0
    mask4 = np.tile(m, (1, 4))
    hsel = np.zeros((16, 2048), dtype=np.float32)
    for h in range(16):
        hsel[h, h * 128:(h + 1) * 128] = 1.0

    if _NC_CACHE is None:
        _NC_CACHE = build_nc()
    nc = _NC_CACHE

    in_maps = []
    for c in range(NCORES):
        in_maps.append({
            "x": np.ascontiguousarray(x[c * TOK:(c + 1) * TOK, :]),
            "wqkv": wqkv, "bqkv": bqkv, "wout": wout, "bout": bout,
            "wffn": wffn, "bffn": bffn, "wscale": ws,
            "ident": ident, "mask4": mask4, "hsel": hsel,
        })
    res = run_bass_kernel_spmd(nc, in_maps, core_ids=list(range(NCORES)))
    out = np.concatenate([res.results[c]["out"] for c in range(NCORES)], axis=0)
    return np.ascontiguousarray(out.reshape(4, 4096, 2048).astype(np.float32, copy=False))


if __name__ == "__main__":
    nc = build_nc()
    print("build ok; instructions:",
          sum(1 for _ in nc.all_instructions()) if hasattr(nc, "all_instructions") else "?")


# version 15
# speedup vs baseline: 1.3627x; 1.1111x over previous
import numpy as np
import concourse.bass as bass
import concourse.mybir as mybir
from concourse.bass_utils import run_bass_kernel_spmd
from contextlib import ExitStack

F32 = mybir.dt.float32
BF16 = mybir.dt.bfloat16
FP8 = mybir.dt.float8e4
AF = mybir.ActivationFunctionType
ALU = mybir.AluOpType
AX = mybir.AxisListType

NCORES = 8
TOK = 2048
D = 2048
NT = 2
NCH = 16
NB = 32
C_INV = float(1.0 / np.sqrt(np.float32(128.0)))
ENGS = ("sync", "tensor", "vector", "scalar", "gpsimd")


class Prog:
    def __init__(self):
        self.ops = {e: [] for e in ENGS}
        self.cnt = {}

    def op(self, eng, waits, incs, fn):
        for s, a in incs:
            self.cnt[s] = self.cnt.get(s, 0) + a
        for s, v in waits:
            self.cnt.setdefault(s, 0)
        self.ops[eng].append((tuple(waits), tuple(incs), fn))

    def check(self):
        pos = {e: 0 for e in ENGS}
        cur = {s: 0 for s in self.cnt}
        progress = True
        while progress:
            progress = False
            for e in ENGS:
                while pos[e] < len(self.ops[e]):
                    waits, incs, _ = self.ops[e][pos[e]]
                    if all(cur.get(s, 0) >= v for s, v in waits):
                        for s, a in incs:
                            cur[s] = cur.get(s, 0) + a
                        pos[e] += 1
                        progress = True
                    else:
                        break
        stuck = {e: pos[e] for e in ENGS if pos[e] < len(self.ops[e])}
        if stuck:
            det = {}
            for e in stuck:
                waits, _, _ = self.ops[e][pos[e]]
                det[e] = [(s, v, cur.get(s, 0)) for s, v in waits]
            raise RuntimeError(f"deadlock: stuck={stuck} waits={det}")


def build_nc():
    nc = bass.Bass("TRN2", target_bir_lowering=False)

    d_x = nc.dram_tensor("x", [TOK, D], F32, kind="ExternalInput").ap()
    d_wqkv = nc.dram_tensor("wqkv", [D, 6144], F32, kind="ExternalInput").ap()
    d_bqkv = nc.dram_tensor("bqkv", [48, 128], F32, kind="ExternalInput").ap()
    d_wout = nc.dram_tensor("wout", [D, D], F32, kind="ExternalInput").ap()
    d_bout = nc.dram_tensor("bout", [16, 128], F32, kind="ExternalInput").ap()
    d_wffn = nc.dram_tensor("wffn", [D, D], F32, kind="ExternalInput").ap()
    d_bffn = nc.dram_tensor("bffn", [16, 128], F32, kind="ExternalInput").ap()
    d_ws = nc.dram_tensor("wscale", [1, 1], F32, kind="ExternalInput").ap()
    d_ident = nc.dram_tensor("ident", [128, 128], F32, kind="ExternalInput").ap()
    d_mask4 = nc.dram_tensor("mask4", [128, 512], F32, kind="ExternalInput").ap()
    d_hsel = nc.dram_tensor("hsel", [16, 2048], F32, kind="ExternalInput").ap()
    d_out = nc.dram_tensor("out", [TOK, D], F32, kind="ExternalOutput").ap()
    t_ccin = nc.dram_tensor("cc_in", [1, 1], F32)
    t_ccout = nc.dram_tensor("cc_out", [1, 1], F32)
    d_ccin = t_ccin.ap()
    d_ccout = t_ccout.ap()

    es = ExitStack()
    SBt = lambda n, sh, dt=F32: es.enter_context(nc.sbuf_tensor(n, sh, dt))
    PSt = lambda n, sh: es.enter_context(nc.psum_tensor(n, sh, F32))

    with es:
        AR = SBt("AR", [128, 55936], BF16)
        def ar(lo, n, p=128):
            return AR[0:p, lo:lo + n]
        I_bf = lambda: ar(0, 128)
        wvsum = lambda k: ar(128 + k * 128, 128)
        mask4_bf = lambda: ar(2176, 512)
        hsel_bf = lambda h: ar(2688 + h * 128, 128, 16)
        rs16 = lambda c0, n: ar(4736 + c0, n, 16)
        xnT = lambda k, c0, n: ar(5760 + k * 1024 + c0, n)
        Pb = lambda b2: ar(5760 + b2 * 512, 512)
        Ptb = lambda b2: ar(6784 + b2 * 512, 512)
        rsst = lambda b4: ar(7808 + b4 * 512, 512, 1)
        attnT = lambda c0, n: ar(5760 + c0, n)
        Qg = lambda c0, n: ar(22144 + c0, n)
        Kg = lambda c0, n: ar(38528 + c0, n)
        svT = lambda c0, n: ar(54912 + c0, n)

        xbQK = SBt("xbQK", [128, 2 * D])
        xb = [xbQK[:, 0:D], xbQK[:, D:2 * D]]
        xnb = SBt("xnb", [128, D])
        wring = [SBt("wr0", [128, 2048], BF16), SBt("wr1", [128, 2048], BF16)]
        xqT8 = SBt("xqT8", [128, 16 * 1024], FP8)
        w8T = SBt("w8T", [128, 16 * 2048], FP8)
        expb = [SBt("expb0", [128, 512], BF16), SBt("expb1", [128, 512], BF16)]
        sg_sb = [SBt("sg0", [128, 512]), SBt("sg1", [128, 512])]
        silu_sb = expb
        outb = [SBt("outb0", [128, 512]), SBt("outb1", [128, 512])]
        xld = [SBt("xld0", [128, 512]), SBt("xld1", [128, 512])]
        wvst = [outb[0], outb[1], sg_sb[0], sg_sb[1]]
        sIf = SBt("sIf", [128, 128])
        stats_m = SBt("st_m", [128, 16])
        stats_r = SBt("st_r", [128, 16])
        stats_b = SBt("st_b", [128, 16])
        stats_s1 = SBt("st_s1", [128, 16])
        stats_s2 = SBt("st_s2", [128, 16])
        stats_mx = SBt("st_mx", [128, 16])
        stats_mn = SBt("st_mn", [128, 16])
        tmp1 = SBt("tmp1", [128, 16])
        tmp2 = SBt("tmp2", [128, 16])
        gmax = SBt("gmax", [128, 1])
        gsc = SBt("gsc", [1, 136])
        inv2s = SBt("inv2s", [128, 1])
        tws = SBt("tws", [128, 1])
        onesrow = SBt("onesrow", [1, 128])
        ones_bf = SBt("ones_bf", [128, 1], BF16)
        cols = SBt("cols", [128, 80])
        cbq = SBt("cbq", [128, 16])
        bv_col = SBt("bv_col", [128, 1])
        nmx = [SBt("nmx0", [128, 4]), SBt("nmx1", [128, 4])]
        sumb = [SBt("sumb0", [128, 4]), SBt("sumb1", [128, 4])]
        rb = [SBt("rb0", [128, 4]), SBt("rb1", [128, 4])]

        psA = PSt("psA", [128, 512])
        psB = PSt("psB", [128, 512])
        psC = PSt("psC", [128, 512])
        psD = PSt("psD", [128, 512])
        psE = PSt("psE", [128, 512])
        psE2 = PSt("psE2", [128, 512])
        psF = PSt("psF", [1, 512])
        psF2 = PSt("psF2", [1, 512])

        P = Prog()
        sems = {}
        for n in ["sld", "spc", "sbias", "swv", "swvp", "svr", "sx", "sxv",
                  "ssta", "sva", "ssq", "sgt1", "scc1", "scc", "sccb",
                  "sbc", "sinv", "sxn", "stp", "stpe", "sxne",
                  "swq", "swqc", "sqkv", "sqkve", "satt", "sattm", "satte",
                  "sattp", "sattt", "sattpe", "sattrs", "sattre", "srs16",
                  "swf", "sw8t", "sw8e", "srsb", "srsbe", "swo", "swoc",
                  "syg", "ssilu", "ssg", "sgt", "sgte", "sout", "sxl", "sadd",
                  "stwc", "sfz", "srcb"]:
            sems[n] = es.enter_context(nc.semaphore(n))

        def OP(eng, waits, incs, build):
            waits = [(s, v) for s, v in waits if v > 0]
            def fn(e):
                for s, v in waits:
                    e.wait_ge(sems[s], v)
                inst = build(e)
                for s, a in incs:
                    inst = inst.then_inc(sems[s], a)
            P.op(eng, waits, incs, fn)

        _fz = [0]
        def VC(waits, build):
            OP("vector", waits + [("sfz", _fz[0])], [("sfz", 1)], build)
            _fz[0] += 1
            return _fz[0]

        OP("sync", [], [("sld", 16)], lambda e: e.dma_start(out=sIf[:, :], in_=d_ident))
        OP("sync", [], [("sld", 16)], lambda e: e.dma_start(out=xb[0][:, 0:512], in_=d_mask4))
        OP("sync", [], [("sld", 16)], lambda e: e.dma_start(out=xnb[0:16, :], in_=d_hsel))
        OP("sync", [], [("sld", 16)], lambda e: e.dma_start(out=xb[1][0:48, 0:128], in_=d_bqkv))
        OP("sync", [], [("sld", 16)], lambda e: e.dma_start(out=xb[1][0:16, 128:256], in_=d_bout))
        OP("sync", [], [("sld", 16)], lambda e: e.dma_start(out=xb[1][0:16, 256:384], in_=d_bffn))
        OP("sync", [], [("sld", 16)], lambda e: e.dma_start(out=gsc[0:1, 0:1], in_=d_ws))
        NPC = 20
        OP("vector", [("sld", 112)], [("spc", 1)], lambda e: e.tensor_copy(I_bf(), sIf[:, :]))
        OP("vector", [], [("spc", 1)], lambda e: e.tensor_copy(mask4_bf(), xb[0][:, 0:512]))
        for h in range(16):
            OP("vector", [], [("spc", 1)],
               lambda e, h=h: e.tensor_copy(hsel_bf(h), xnb[0:16, h * 128:(h + 1) * 128]))
        OP("vector", [], [("spc", 1)], lambda e: e.memset(ones_bf[:, :], 1.0))
        OP("vector", [], [("spc", 1)], lambda e: e.memset(onesrow[:, :], 1.0))
        OP("tensor", [("spc", NPC)], [("sbias", 1)],
           lambda e: e.matmul(psE[:, 0:48], xb[1][0:48, 0:128], sIf[0:48, 0:48],
                              start=True, stop=True))
        OP("tensor", [], [("sbias", 1)],
           lambda e: e.matmul(psE[:, 48:64], xb[1][0:16, 128:256], sIf[0:16, 0:16],
                              start=True, stop=True))
        OP("tensor", [], [("sbias", 1)],
           lambda e: e.matmul(psE[:, 64:80], xb[1][0:16, 256:384], sIf[0:16, 0:16],
                              start=True, stop=True))
        OP("vector", [("sbias", 3)], [("sbias", 1)],
           lambda e: e.tensor_copy(cols[:, :], psE[:, 0:80]))
        bq_view = cols[:, 0:48].rearrange("p (h j) -> p j h", h=16, j=3)
        OP("vector", [("sbias", 4)], [("sbias", 1)],
           lambda e: e.tensor_scalar(cbq[:, :], bq_view[:, 0, :], C_INV, None, ALU.mult))
        OP("vector", [("sbias", 5)], [("sbias", 1)],
           lambda e: e.tensor_reduce(bv_col[:, :], bq_view[:, 2, :], AX.X, ALU.add))
        OP("tensor", [("sbias", 6)], [("stwc", 1)],
           lambda e: e.matmul(psE[:, 80:81], onesrow[:, :], gsc[0:1, 0:1],
                              start=True, stop=True))
        OP("vector", [("stwc", 1)], [("stwc", 1)],
           lambda e: e.tensor_scalar(tws[:, :], psE[:, 80:81], 2.0, None, ALU.mult))

        def wf_load(oc):
            if oc < 2:
                w = [("spc", NPC)]
            else:
                w = [("sw8e", 4 * (oc - 1))]
            OP("gpsimd", w, [("swf", 16)],
               lambda e, oc=oc: e.dma_start(out=wring[oc % 2][:, :],
                                            in_=d_wffn[oc * 128:(oc + 1) * 128, :]))
        for oc in range(16):
            wf_load(oc)

        def w8t_pe(oc):
            for r in range(4):
                gri = oc * 4 + r
                ebank = psE if gri % 2 == 0 else psE2
                for i in range(4):
                    ic = r * 4 + i
                    waits = []
                    if i == 0:
                        waits.append(("swf", 16 * (oc + 1)))
                        if gri == 0:
                            waits.append(("sbias", 4))
                            waits.append(("stwc", 2))
                        if gri >= 2:
                            waits.append(("sw8e", gri - 1))
                    incs = [("sw8t", 1)] if i == 3 else []
                    OP("tensor", waits, incs,
                       lambda e, eb=ebank, oc=oc, ic=ic, i=i: e.matmul(
                           eb[:, i * 128:(i + 1) * 128],
                           wring[oc % 2][:, ic * 128:(ic + 1) * 128], I_bf(),
                           start=True, stop=True))

        def w8t_evac_round(gri):
            oc, r = gri // 4, gri % 4
            ebank = psE if gri % 2 == 0 else psE2
            dst = w8T[:, :].rearrange("p (ic f) -> p ic f", f=2048)[
                :, r * 4:(r + 1) * 4, oc * 128:(oc + 1) * 128]
            src = ebank[:, :].rearrange("p (i f) -> p i f", f=128)
            OP("vector", [("sw8t", gri + 1)], [("sw8e", 1)],
               lambda e, dst=dst, src=src: e.tensor_copy(dst, src))

        for oc in range(16):
            w8t_pe(oc)

        for c in range(NCH):
            if c == 0:
                w = [("spc", NPC)]
            elif c == 1:
                w = [("sbias", 3)]
            else:
                w = [("ssta", c - 1), ("sxv", c - 1)]
            OP("sync" if c % 2 == 0 else "scalar", w, [("sx", 16)],
               lambda e, c=c: e.dma_start(out=xb[c % 2][:, :], in_=d_x[c * 128:(c + 1) * 128, :]))
            xc = xb[c % 2]
            aw = [("sx", 16 * (c + 1))]
            if c == 0:
                aw.append(("spc", NPC))
            OP("scalar", aw, [],
               lambda e, xc=xc, c=c: e.activation(xnb[:, :], xc[:, :], AF.Square,
                                                  accum_out=stats_s2[:, c:c + 1]))
            OP("scalar", [], [("ssta", 1)],
               lambda e, xc=xc, c=c: e.activation(xnb[:, :], xc[:, :], AF.Identity,
                                                  accum_out=stats_s1[:, c:c + 1]))
            OP("vector", [("sx", 16 * (c + 1))], [],
               lambda e, xc=xc, c=c: e.tensor_reduce(stats_mx[:, c:c + 1], xc[:, :],
                                                     AX.X, ALU.max))
            OP("vector", [], [("sxv", 1)],
               lambda e, xc=xc, c=c: e.tensor_reduce(stats_mn[:, c:c + 1], xc[:, :],
                                                     AX.X, ALU.min))
            for _r in range(4):
                w8t_evac_round(4 * c + _r)

        VC([("ssta", NCH), ("sxv", NCH)],
           lambda e: e.tensor_scalar(stats_m[:, :], stats_s1[:, :],
                                     1.0 / 2048.0, None, ALU.mult))
        VC([], lambda e: e.tensor_mul(tmp2[:, :], stats_m[:, :], stats_m[:, :]))
        VC([], lambda e: e.tensor_scalar(tmp1[:, :], stats_s2[:, :],
                                         1.0 / 2048.0, None, ALU.mult))
        VC([], lambda e: e.tensor_sub(tmp1[:, :], tmp1[:, :], tmp2[:, :]))
        _va = VC([], lambda e: e.tensor_scalar(tmp1[:, :], tmp1[:, :], 1e-5, None, ALU.add))
        OP("scalar", [("sfz", _va)], [("ssq", 1)],
           lambda e: e.activation(tmp2[:, :], tmp1[:, :], AF.Sqrt))
        VC([("ssq", 1)], lambda e: e.reciprocal(stats_r[:, :], tmp2[:, :]))
        VC([], lambda e: e.tensor_mul(tmp2[:, :], stats_r[:, :], stats_r[:, :]))
        VC([], lambda e: e.tensor_mul(tmp2[:, :], tmp2[:, :], tmp1[:, :]))
        VC([], lambda e: e.tensor_scalar(tmp2[:, :], tmp2[:, :], -0.5, 1.5,
                                         ALU.mult, ALU.add))
        VC([], lambda e: e.tensor_mul(stats_r[:, :], stats_r[:, :], tmp2[:, :]))
        VC([], lambda e: e.tensor_mul(tmp2[:, :], stats_m[:, :], stats_r[:, :]))
        FZ_STATS = VC([], lambda e: e.tensor_scalar(stats_b[:, :], tmp2[:, :],
                                                    -1.0, None, ALU.mult))
        VC([], lambda e: e.tensor_sub(tmp2[:, :], stats_mx[:, :], stats_m[:, :]))
        VC([], lambda e: e.tensor_sub(tmp1[:, :], stats_m[:, :], stats_mn[:, :]))
        VC([], lambda e: e.tensor_max(tmp2[:, :], tmp2[:, :], tmp1[:, :]))
        VC([], lambda e: e.tensor_mul(tmp2[:, :], tmp2[:, :], stats_r[:, :]))
        FZ_FIN = VC([], lambda e: e.tensor_reduce(gmax[:, :], tmp2[:, :], AX.X, ALU.max))

        OP("tensor", [("sfz", FZ_FIN), ("sw8e", 64)], [("sgt1", 1)],
           lambda e: e.transpose(psF[0:1, 0:128], gmax[:, :], sIf[:, :]))
        VC([("sgt1", 1)], lambda e: e.tensor_copy(gsc[0:1, 8:136], psF[0:1, 0:128]))
        _gr = VC([], lambda e: e.tensor_reduce(gsc[0:1, 1:2], gsc[0:1, 8:136],
                                               AX.X, ALU.max))
        OP("gpsimd", [("sfz", _gr)], [("scc1", 16)],
           lambda e: e.dma_start(out=d_ccin, in_=gsc[0:1, 1:2]))
        OP("gpsimd", [("scc1", 16)], [("scc", 1)],
           lambda e: e.collective_compute("AllReduce", ALU.max,
                                          replica_groups=[list(range(NCORES))],
                                          ins=[t_ccin.ap().opt()],
                                          outs=[t_ccout.ap().opt()]))
        OP("sync", [("scc", 1)], [("sccb", 16)],
           lambda e: e.dma_start(out=gsc[0:1, 2:3], in_=d_ccout))
        VC([("sccb", 16)],
           lambda e: e.tensor_scalar(gsc[0:1, 3:4], gsc[0:1, 2:3], 1.0 / 448.0, 1e-7,
                                     ALU.mult, ALU.max))
        VC([], lambda e: e.tensor_scalar(gsc[0:1, 4:5], gsc[0:1, 3:4], 2.0, None, ALU.mult))
        _rct = VC([], lambda e: e.reciprocal(gsc[0:1, 5:6], gsc[0:1, 4:5]))
        OP("tensor", [("sfz", _rct), ("stwc", 2), ("sbias", 4)], [("sbc", 1)],
           lambda e: e.matmul(psE[:, 81:82], onesrow[:, :], gsc[0:1, 5:6],
                              start=True, stop=True))
        OP("vector", [("sbc", 1)], [("sinv", 1)],
           lambda e: e.tensor_copy(inv2s[:, :], psE[:, 81:82]))

        wq_v = d_wqkv.rearrange("m (h j d) -> m h j d", h=16, j=3, d=128)
        for k in range(16):
            for g in range(4):
                w = []
                if g == 0 and k >= 1:
                    w = [("swvp", k)]
                OP("gpsimd", w, [("swv", 16)],
                   lambda e, k=k, g=g: e.dma_start(
                       out=wvst[g][:, :],
                       in_=wq_v[k * 128:(k + 1) * 128, g * 4:(g + 1) * 4, 2, :]))

        def wv_sum_pe(k):
            for g in range(4):
                for h2 in range(4):
                    idx = g * 4 + h2
                    waits = []
                    if idx == 0:
                        waits.append(("swv", 64 * (k + 1)))
                        if k >= 1:
                            waits.append(("svr", k))
                    incs = [("swvp", 1)] if idx == 15 else []
                    OP("tensor", waits, incs,
                       lambda e, g=g, h2=h2, idx=idx: e.matmul(
                           psE2[:, 0:128], sIf[:, :],
                           wvst[g][:, h2 * 128:(h2 + 1) * 128],
                           start=(idx == 0), stop=(idx == 15)))

        def wv_evac(k):
            OP("vector", [("swvp", k + 1)], [("svr", 1)],
               lambda e, k=k: e.tensor_copy(wvsum(k), psE2[:, 0:128]))

        for T in range(NT):
            for c8 in range(8):
                gc = T * 8 + c8
                widx = NCH + gc
                if gc >= 2:
                    w = [("sxn", gc - 1)]
                    if T == 1 and c8 < 2:
                        w.append(("swoc", 16))
                else:
                    w = [("sxv", NCH), ("ssta", NCH)]
                OP("sync" if c8 % 2 == 0 else "scalar", w, [("sx", 16)],
                   lambda e, c=gc, b=xb[gc % 2]: e.dma_start(
                       out=b[:, :], in_=d_x[c * 128:(c + 1) * 128, :]))
                aw = [("sx", 16 * (widx + 1)), ("sfz", FZ_STATS)]
                if gc >= 1:
                    aw.append(("stp", 4 * gc))
                OP("scalar", aw, [("sxn", 1)],
                   lambda e, c=gc, b=xb[gc % 2]: e.activation(
                       xnb[:, :], b[:, :], AF.Identity,
                       bias=stats_b[:, c:c + 1], scale=stats_r[:, c:c + 1]))
                for r in range(4):
                    gri = gc * 4 + r
                    bank = psC if gri % 2 == 0 else psD
                    for i in range(4):
                        k = r * 4 + i
                        waits = []
                        if i == 0:
                            waits.append(("sxn", gc + 1))
                            if gri >= 2:
                                waits.append(("stpe", gri - 1))
                            if T == 1 and r < 2 and c8 == 0:
                                waits.append(("ssg", 32))
                        incs = [("stp", 1)] if i == 3 else []
                        OP("tensor", waits, incs,
                           lambda e, bank=bank, i=i, k=k: e.transpose(
                               bank[:, i * 128:(i + 1) * 128],
                               xnb[:, k * 128:(k + 1) * 128], sIf[:, :]))
                    for i in range(4):
                        k = r * 4 + i
                        waits = []
                        if i == 0:
                            waits.append(("stp", gri + 1))
                            waits.append(("sinv", 1))
                        OP("vector", waits, [],
                           lambda e, bank=bank, i=i, k=k, c8=c8: e.tensor_copy(
                               xnT(k, c8 * 128, 128), bank[:, i * 128:(i + 1) * 128]))
                        incs = [("stpe", 1)] if i == 3 else []
                        OP("vector", [], incs,
                           lambda e, bank=bank, i=i, k=k, c8=c8: e.tensor_scalar(
                               xqT8[:, k * 1024 + c8 * 128:k * 1024 + c8 * 128 + 128],
                               bank[:, i * 128:(i + 1) * 128],
                               inv2s[:, :], None, ALU.mult))

            for s in range(33):
                jjs = T * 33 + s
                if s < 32:
                    h = s // 2
                    is_q = (s % 2 == 0)
                    j = h if is_q else 16 + h
                    if is_q:
                        wq_blk = d_wqkv.rearrange("(k p) f -> p k f", p=128)[
                            :, :, (3 * h) * 128:(3 * h) * 128 + 256]
                        lw = []
                        if s < 2:
                            lw.append(("sxn", 8 * (T + 1)))
                            if T == 1:
                                lw.append(("swoc", 16))
                        else:
                            lw.append(("swqc", 32 * T + s))
                        OP("sync", lw, [("swq", 16)],
                           lambda e, blk=wq_blk: e.dma_start(
                               out=xbQK[:, :].rearrange("p (k f) -> p k f", f=256),
                               in_=blk))
                    cw = [("swq", 16 * (T * 16 + s // 2 + 1))]
                    if s < 2:
                        cw.append(("sw8t", 64) if T == 0 else ("syg", 32))
                    else:
                        cw.append(("sqkv", 2 * (jjs - 1)))
                    fl = 0 if is_q else 128
                    src3 = xbQK[:, :].rearrange("p (k f) -> p k f", f=256)[
                        :, :, fl:fl + 128]
                    dst3 = wring[s % 2][:, :].rearrange("p (k f) -> p k f", f=128)
                    OP("vector", cw, [("swqc", 1)],
                       lambda e, dst3=dst3, src3=src3: e.tensor_copy(dst3, src3))
                else:
                    j = 32
                if T == 0 and s % 2 == 0 and s < 32:
                    wv_sum_pe(s // 2)
                    wv_evac(s // 2)
                for half in range(2):
                    ihalf = jjs * 2 + half
                    pbank = psA if ihalf % 2 == 0 else psB
                    for k in range(16):
                        waits = []
                        if k == 0:
                            if s < 32:
                                waits.append(("swqc", 32 * T + s + 1))
                            else:
                                waits.append(("svr", 16))
                            if half == 0 and s == 0:
                                waits.append(("stpe", (T + 1) * 32))
                            if T == 1 and s == 0 and half == 0:
                                waits.append(("ssg", 32))
                            if ihalf >= 2:
                                waits.append(("sqkve", ihalf - 1))
                        lhs = (wring[s % 2][:, k * 128:(k + 1) * 128] if s < 32
                               else wvsum(k))
                        incs = [("sqkv", 1)] if k == 15 else []
                        OP("tensor", waits, incs,
                           lambda e, p=pbank, lhs=lhs, k=k, half=half: e.matmul(
                               p[:, :], lhs, xnT(k, half * 512, 512),
                               start=(k == 0), stop=(k == 15)))
                    if s < 32:
                        dstf = Qg(0, 16384) if j < 16 else Kg(0, 16384)
                        dst3 = dstf.rearrange("p (g hh t) -> p g hh t", g=128, hh=16, t=8)
                        dst = dst3[:, half * 64:(half + 1) * 64, h, :]
                        sc = C_INV if j < 16 else 1.0
                        bias = cbq[:, h:h + 1] if j < 16 else cols[:, 3 * h + 1:3 * h + 2]
                        OP("scalar", [("sqkv", ihalf + 1)], [("sqkve", 1)],
                           lambda e, p=pbank, dst=dst, sc=sc, bias=bias: e.activation(
                               dst, p[:, :], AF.Identity, bias=bias, scale=sc))
                    else:
                        OP("scalar", [("sqkv", ihalf + 1)], [("sqkve", 1)],
                           lambda e, p=pbank, half=half: e.activation(
                               svT(half * 512, 512), p[:, :], AF.Identity,
                               bias=bv_col[:, :], scale=1.0))

            def c_qk(b):
                gb = T * NB + b
                bank = [psC, psD, psE, psE2][b % 4]
                for g in range(4):
                    waits = []
                    if g == 0:
                        waits.append(("sqkve", (T + 1) * 66))
                        if T == 0 and b == 2:
                            waits.append(("sinv", 1))
                        if T == 0 and b == 3:
                            waits.append(("svr", 16))
                        if gb >= 4:
                            waits.append(("satte", gb - 3))
                    OP("tensor", waits, [],
                       lambda e, bank=bank, b=b, g=g: e.matmul(
                           bank[:, g * 128:(g + 1) * 128],
                           Qg(b * 512 + g * 128, 128), Kg(b * 512 + g * 128, 128),
                           start=(g == 0), stop=False))
                OP("tensor", [], [("satt", 1)],
                   lambda e, bank=bank: e.matmul(bank[:, :], I_bf(), mask4_bf(),
                                                 start=False, stop=True))

            def c_negmax(b):
                gb = T * NB + b
                bank = [psC, psD, psE, psE2][b % 4]
                b2 = b % 2
                mw = [("satt", gb + 1)]
                if gb >= 2:
                    mw.append(("satte", gb - 1))
                OP("vector", mw, [("sattm", 1)],
                   lambda e, bank=bank, b2=b2: e.tensor_reduce(
                       nmx[b2][:, :],
                       bank[:, :].rearrange("p (g n) -> p g n", g=4, n=128),
                       AX.X, ALU.max, negate=True))

            def c_exps(b):
                gb = T * NB + b
                bank = [psC, psD, psE, psE2][b % 4]
                b2 = b % 2
                for g in range(4):
                    waits = []
                    if g == 0:
                        waits.append(("sattm", gb + 1))
                        if gb >= 2:
                            waits.append(("sattp", gb - 1))
                    incs = [("satte", 1)] if g == 3 else []
                    OP("scalar", waits, incs,
                       lambda e, bank=bank, b2=b2, g=g: e.activation(
                           expb[b2][:, g * 128:(g + 1) * 128],
                           bank[:, g * 128:(g + 1) * 128], AF.Exp,
                           bias=nmx[b2][:, g:g + 1], scale=1.0,
                           accum_out=sumb[b2][:, g:g + 1]))

            def c_pmult(b):
                gb = T * NB + b
                b2 = b % 2
                rw = [("satte", gb + 1)]
                if gb >= 2:
                    rw.append(("sattt", gb - 1))
                OP("vector", rw, [("srcb", 1)],
                   lambda e, b2=b2: e.reciprocal(rb[b2][:, :], sumb[b2][:, :]))
                for g in range(4):
                    incs = [("sattp", 1)] if g == 3 else []
                    OP("vector", [("srcb", gb + 1)] if g == 0 else [], incs,
                       lambda e, b2=b2, g=g: e.tensor_scalar(
                           Pb(b2)[:, g * 128:(g + 1) * 128],
                           expb[b2][:, g * 128:(g + 1) * 128],
                           rb[b2][:, g:g + 1], None, ALU.mult))

            def c_pt(b):
                gb = T * NB + b
                b2 = b % 2
                ebank = psA if b % 2 == 0 else psB
                for g in range(4):
                    waits = []
                    if g == 0:
                        waits.append(("sattp", gb + 1))
                        if gb >= 2:
                            waits.append(("sattpe", gb - 1))
                    incs = [("sattt", 1)] if g == 3 else []
                    OP("tensor", waits, incs,
                       lambda e, eb=ebank, b2=b2, g=g: e.matmul(
                           eb[:, g * 128:(g + 1) * 128],
                           Pb(b2)[:, g * 128:(g + 1) * 128], I_bf(),
                           start=True, stop=True))

            def c_ptevac(b):
                gb = T * NB + b
                b2 = b % 2
                ebank = psA if b % 2 == 0 else psB
                pw = [("sattt", gb + 1)]
                if gb >= 2:
                    pw.append(("sattrs", gb - 1))
                OP("vector", pw, [("sattpe", 1)],
                   lambda e, eb=ebank, b2=b2: e.tensor_copy(Ptb(b2), eb[:, :]))

            def c_rsmm(b):
                gb = T * NB + b
                b2 = b % 2
                fbank = psF if b % 2 == 0 else psF2
                for g in range(4):
                    waits = []
                    if g == 0:
                        waits.append(("sattpe", gb + 1))
                        if gb >= 2:
                            waits.append(("sattre", gb - 1))
                    incs = [("sattrs", 1)] if g == 3 else []
                    OP("tensor", waits, incs,
                       lambda e, b2=b2, g=g, fb=fbank: e.matmul(
                           fb[0:1, g * 128:(g + 1) * 128], ones_bf[:, :],
                           Ptb(b2)[:, g * 128:(g + 1) * 128], start=True, stop=True))

            def c_rsevac(b):
                gb = T * NB + b
                b4 = b % 4
                fbank = psF if b % 2 == 0 else psF2
                rdst = rsst(b4).rearrange("p (q g t) -> p g q t", q=16, g=4, t=8)
                rsrc = fbank[0:1, :].rearrange("p (g q t) -> p g q t", g=4, q=16, t=8)
                ew = [("sattrs", gb + 1)]
                if gb >= 4:
                    ew.append(("srs16", 16 * (gb - 3)))
                OP("scalar", ew, [("sattre", 1)],
                   lambda e, rdst=rdst, rsrc=rsrc: e.activation(rdst, rsrc, AF.Copy))
                rsv = rsst(b4).rearrange("p (q gt) -> p q gt", q=16, gt=32)
                OP("scalar", [], [("srs16", 16)],
                   lambda e, rsv=rsv, b=b: e.dma_start(out=rs16(b * 32, 32), in_=rsv))

            for it in range(NB + 2):
                if it < NB:
                    c_qk(it)
                    c_negmax(it)
                    c_exps(it)
                if 0 <= it - 1 < NB:
                    c_pmult(it - 1)
                    c_pt(it - 1)
                if 0 <= it - 2 < NB:
                    c_ptevac(it - 2)
                    c_rsmm(it - 2)
                    c_rsevac(it - 2)

            for h in range(16):
                for half in range(2):
                    ih = (T * 16 + h) * 2 + half
                    ebank = psE if ih % 2 == 0 else psE2
                    waits = [("srs16", 16 * (T + 1) * NB), ("sw8e", 64),
                             ("satte", (T + 1) * NB)]
                    if ih >= 2:
                        waits.append(("srsbe", ih - 1))
                    OP("tensor", waits, [("srsb", 1)],
                       lambda e, eb=ebank, h=h, half=half: e.matmul(
                           eb[:, :], hsel_bf(h), rs16(half * 512, 512),
                           start=True, stop=True))
                    OP("vector", [("srsb", ih + 1)], [("srsbe", 1)],
                       lambda e, eb=ebank, h=h, half=half: e.tensor_tensor(
                           attnT(h * 1024 + half * 512, 512), eb[:, :],
                           svT(half * 512, 512), ALU.mult))

            def f_load(j):
                q = j // 2
                if j % 2 == 0:
                    lw = []
                    if q == 0:
                        lw.append(("srsbe", (T + 1) * 32))
                        lw.append(("swqc", 32 * (T + 1)))
                    else:
                        lw.append(("swoc", 16 * T + 2 * q))
                    wo_blk = d_wout.rearrange("(k p) f -> p k f", p=128)[
                        :, :, j * 128:j * 128 + 256]
                    OP("sync", lw, [("swo", 16)],
                       lambda e, blk=wo_blk: e.dma_start(
                           out=xbQK[:, :].rearrange("p (k f) -> p k f", f=256),
                           in_=blk))
                cw = [("swo", 16 * (T * 8 + q + 1))]
                if j < 2:
                    cw.append(("sqkv", 66 * (T + 1)))
                else:
                    cw.append(("syg", 2 * (T * 16 + j - 1)))
                fl = (j % 2) * 128
                src3 = xbQK[:, :].rearrange("p (k f) -> p k f", f=256)[:, :, fl:fl + 128]
                dst3 = wring[j % 2][:, :].rearrange("p (k f) -> p k f", f=128)
                OP("vector", cw, [("swoc", 1)],
                   lambda e, dst3=dst3, src3=src3: e.tensor_copy(dst3, src3))

            def f_mm(j, half):
                ihalf = (T * 16 + j) * 2 + half
                i2 = ihalf % 2
                ybank = psA if i2 == 0 else psC
                gbank2 = psB if i2 == 0 else psD
                rbase = T * 1024 + half * 512
                xsrc = d_x[rbase:rbase + 512, j * 128:(j + 1) * 128] \
                    .rearrange("(tg t) m -> t tg m", tg=4, t=128)
                xdst = xld[i2][:, :].rearrange("p (tg m) -> p tg m", tg=4, m=128)
                OP("scalar",
                   [("sadd", ihalf - 1)] if ihalf >= 2 else [], [("sxl", 16)],
                   lambda e, xdst=xdst, xsrc=xsrc: e.dma_start(out=xdst, in_=xsrc))
                for k in range(16):
                    waits = []
                    if k == 0:
                        waits.append(("swoc", 16 * T + j + 1))
                        waits.append(("srsbe", (T + 1) * 32))
                        if T == 1 and ihalf % 32 < 2:
                            waits.append(("ssg", 32))
                        if ihalf >= 2:
                            waits.append(("ssg", ihalf - 1))
                    OP("tensor", waits, [],
                       lambda e, j=j, k=k, half=half, yb=ybank: e.matmul(
                           yb[:, :], wring[j % 2][:, k * 128:(k + 1) * 128],
                           attnT(k * 1024 + half * 512, 512),
                           start=(k == 0), stop=(k == 15)))
                for k in range(16):
                    incs = [("syg", 1)] if k == 15 else []
                    OP("tensor", [], incs,
                       lambda e, j=j, k=k, half=half, gb2=gbank2: e.matmul(
                           gb2[:, :],
                           w8T[:, k * 2048 + j * 128:k * 2048 + (j + 1) * 128],
                           xqT8[:, k * 1024 + half * 512:k * 1024 + half * 512 + 512],
                           start=(k == 0), stop=(k == 15)))

            def f_silu_stt(j, half):
                ihalf = (T * 16 + j) * 2 + half
                i2 = ihalf % 2
                ybank = psA if i2 == 0 else psC
                gbank2 = psB if i2 == 0 else psD
                sw = [("syg", ihalf + 1)]
                if ihalf >= 2:
                    sw.append(("ssg", ihalf - 1))
                OP("scalar", sw, [("ssilu", 1)],
                   lambda e, j=j, gb2=gbank2, i2=i2: e.activation(
                       silu_sb[i2][:, :], gb2[:, :], AF.Silu,
                       bias=cols[:, 64 + j:65 + j], scale=tws[:, :]))
                gw = [("ssilu", ihalf + 1)]
                if ihalf >= 2:
                    gw.append(("sgt", ihalf - 1))
                OP("vector", gw, [("ssg", 1)],
                   lambda e, j=j, yb=ybank, i2=i2: e.scalar_tensor_tensor(
                       sg_sb[i2][:, :], yb[:, :], cols[:, 48 + j:49 + j],
                       silu_sb[i2][:, :], ALU.add, ALU.mult))

            def f_tgate(j, half):
                ihalf = (T * 16 + j) * 2 + half
                i2 = ihalf % 2
                tbank = psE if i2 == 0 else psE2
                for tg in range(4):
                    waits = []
                    if tg == 0:
                        waits.append(("ssg", ihalf + 1))
                        if ihalf % 32 < 2:
                            waits.append(("srsbe", (T + 1) * 32))
                        if ihalf >= 2:
                            waits.append(("sgte", ihalf - 1))
                    incs = [("sgt", 1)] if tg == 3 else []
                    OP("tensor", waits, incs,
                       lambda e, gb=tbank, tg=tg, i2=i2: e.transpose(
                           gb[:, tg * 128:(tg + 1) * 128],
                           sg_sb[i2][:, tg * 128:(tg + 1) * 128], sIf[:, :]))

            def f_out(j, half):
                ihalf = (T * 16 + j) * 2 + half
                i2 = ihalf % 2
                tbank = psE if i2 == 0 else psE2
                ow = [("sgt", ihalf + 1)]
                if ihalf >= 2:
                    ow.append(("sout", 16 * (ihalf - 1)))
                OP("vector", ow, [("sgte", 1)],
                   lambda e, gb=tbank, b2=i2: e.tensor_copy(outb[b2][:, :], gb[:, :]))
                OP("vector", [("sxl", 16 * (ihalf + 1))], [("sadd", 1)],
                   lambda e, b2=i2: e.tensor_tensor(outb[b2][:, :], outb[b2][:, :],
                                                    xld[b2][:, :], ALU.add))
                rbase = T * 1024 + half * 512
                dst = d_out[rbase:rbase + 512, j * 128:(j + 1) * 128] \
                    .rearrange("(tg t) m -> t tg m", tg=4, t=128)
                srcb_ = outb[i2][:, :].rearrange("p (tg m) -> p tg m", tg=4, m=128)
                OP("gpsimd", [("sadd", ihalf + 1)], [("sout", 16)],
                   lambda e, dst=dst, src=srcb_: e.dma_start(out=dst, in_=src))

            prev = None
            for j in range(16):
                f_load(j)
                for half in range(2):
                    f_mm(j, half)
                    if prev is not None:
                        f_tgate(*prev)
                        f_out(*prev)
                    f_silu_stt(j, half)
                    prev = (j, half)
            f_tgate(*prev)
            f_out(*prev)
            prev = None

        OP("sync", [("sout", 16 * 64)], [], lambda e: e.nop())

        P.check()
        blk = es.enter_context(nc.Block())
        for engname in ENGS:
            lst = P.ops[engname]
            def make(lst):
                def emit(eng):
                    for _, _, fn in lst:
                        fn(eng)
                return emit
            getattr(blk, engname)(make(lst))

    return nc


_NC_CACHE = None


def kernel(L0_attn_x, L0_attn_Wqkv, L0_attn_bqkv, L0_attn_Wout, L0_attn_bout,
           L0_ffn_weight, L0_ffn_weight_scale, L0_ffn_bias):
    global _NC_CACHE
    x = np.ascontiguousarray(np.asarray(L0_attn_x, dtype=np.float32)).reshape(16384, 2048)
    wqkv = np.ascontiguousarray(np.asarray(L0_attn_Wqkv, dtype=np.float32)).reshape(2048, 6144)
    bqkv = np.ascontiguousarray(np.asarray(L0_attn_bqkv, dtype=np.float32)).reshape(48, 128)
    wout = np.ascontiguousarray(np.asarray(L0_attn_Wout, dtype=np.float32)).reshape(2048, 2048)
    bout = np.ascontiguousarray(np.asarray(L0_attn_bout, dtype=np.float32)).reshape(16, 128)
    wffn = np.ascontiguousarray(np.asarray(L0_ffn_weight, dtype=np.float32)).reshape(2048, 2048)
    bffn = np.ascontiguousarray(np.asarray(L0_ffn_bias, dtype=np.float32)).reshape(16, 128)
    ws = np.asarray(L0_ffn_weight_scale, dtype=np.float32).reshape(1, 1)

    ident = np.eye(128, dtype=np.float32)
    m = np.full((128, 128), np.float32(-998244352.0), dtype=np.float32)
    for qh in range(16):
        for tq in range(8):
            for kh in range(qh + 1):
                m[qh * 8 + tq, kh * 8 + tq] = 0.0
    mask4 = np.tile(m, (1, 4))
    hsel = np.zeros((16, 2048), dtype=np.float32)
    for h in range(16):
        hsel[h, h * 128:(h + 1) * 128] = 1.0

    if _NC_CACHE is None:
        _NC_CACHE = build_nc()
    nc = _NC_CACHE

    in_maps = []
    for c in range(NCORES):
        in_maps.append({
            "x": np.ascontiguousarray(x[c * TOK:(c + 1) * TOK, :]),
            "wqkv": wqkv, "bqkv": bqkv, "wout": wout, "bout": bout,
            "wffn": wffn, "bffn": bffn, "wscale": ws,
            "ident": ident, "mask4": mask4, "hsel": hsel,
        })
    res = run_bass_kernel_spmd(nc, in_maps, core_ids=list(range(NCORES)))
    out = np.concatenate([res.results[c]["out"] for c in range(NCORES)], axis=0)
    return np.ascontiguousarray(out.reshape(4, 4096, 2048).astype(np.float32, copy=False))


if __name__ == "__main__":
    nc = build_nc()
    print("build ok; instructions:",
          sum(1 for _ in nc.all_instructions()) if hasattr(nc, "all_instructions") else "?")


# version 16
# speedup vs baseline: 1.3631x; 1.0003x over previous
import numpy as np
import concourse.bass as bass
import concourse.mybir as mybir
from concourse.bass_utils import run_bass_kernel_spmd
from contextlib import ExitStack

F32 = mybir.dt.float32
BF16 = mybir.dt.bfloat16
FP8 = mybir.dt.float8e4
AF = mybir.ActivationFunctionType
ALU = mybir.AluOpType
AX = mybir.AxisListType

NCORES = 8
TOK = 2048
D = 2048
NT = 2
NCH = 16
NB = 32
C_INV = float(1.0 / np.sqrt(np.float32(128.0)))
ENGS = ("sync", "tensor", "vector", "scalar", "gpsimd")


class Prog:
    def __init__(self):
        self.ops = {e: [] for e in ENGS}
        self.cnt = {}

    def op(self, eng, waits, incs, fn):
        for s, a in incs:
            self.cnt[s] = self.cnt.get(s, 0) + a
        for s, v in waits:
            self.cnt.setdefault(s, 0)
        self.ops[eng].append((tuple(waits), tuple(incs), fn))

    def check(self):
        pos = {e: 0 for e in ENGS}
        cur = {s: 0 for s in self.cnt}
        progress = True
        while progress:
            progress = False
            for e in ENGS:
                while pos[e] < len(self.ops[e]):
                    waits, incs, _ = self.ops[e][pos[e]]
                    if all(cur.get(s, 0) >= v for s, v in waits):
                        for s, a in incs:
                            cur[s] = cur.get(s, 0) + a
                        pos[e] += 1
                        progress = True
                    else:
                        break
        stuck = {e: pos[e] for e in ENGS if pos[e] < len(self.ops[e])}
        if stuck:
            det = {}
            for e in stuck:
                waits, _, _ = self.ops[e][pos[e]]
                det[e] = [(s, v, cur.get(s, 0)) for s, v in waits]
            raise RuntimeError(f"deadlock: stuck={stuck} waits={det}")


def build_nc():
    nc = bass.Bass("TRN2", target_bir_lowering=False)

    d_x = nc.dram_tensor("x", [TOK, D], F32, kind="ExternalInput").ap()
    d_wqkv = nc.dram_tensor("wqkv", [D, 6144], F32, kind="ExternalInput").ap()
    d_bqkv = nc.dram_tensor("bqkv", [48, 128], F32, kind="ExternalInput").ap()
    d_wout = nc.dram_tensor("wout", [D, D], F32, kind="ExternalInput").ap()
    d_bout = nc.dram_tensor("bout", [16, 128], F32, kind="ExternalInput").ap()
    d_wffn = nc.dram_tensor("wffn", [D, D], F32, kind="ExternalInput").ap()
    d_bffn = nc.dram_tensor("bffn", [16, 128], F32, kind="ExternalInput").ap()
    d_ws = nc.dram_tensor("wscale", [1, 1], F32, kind="ExternalInput").ap()
    d_ident = nc.dram_tensor("ident", [128, 128], F32, kind="ExternalInput").ap()
    d_mask4 = nc.dram_tensor("mask4", [128, 512], F32, kind="ExternalInput").ap()
    d_hsel = nc.dram_tensor("hsel", [16, 2048], F32, kind="ExternalInput").ap()
    d_out = nc.dram_tensor("out", [TOK, D], F32, kind="ExternalOutput").ap()
    t_ccin = nc.dram_tensor("cc_in", [1, 1], F32)
    t_ccout = nc.dram_tensor("cc_out", [1, 1], F32)
    d_ccin = t_ccin.ap()
    d_ccout = t_ccout.ap()

    es = ExitStack()
    SBt = lambda n, sh, dt=F32: es.enter_context(nc.sbuf_tensor(n, sh, dt))
    PSt = lambda n, sh: es.enter_context(nc.psum_tensor(n, sh, F32))

    with es:
        AR = SBt("AR", [128, 55936], BF16)
        def ar(lo, n, p=128):
            return AR[0:p, lo:lo + n]
        I_bf = lambda: ar(0, 128)
        wvsum = lambda k: ar(128 + k * 128, 128)
        mask4_bf = lambda: ar(2176, 512)
        hsel_bf = lambda h: ar(2688 + h * 128, 128, 16)
        rs16 = lambda c0, n: ar(4736 + c0, n, 16)
        xnT = lambda k, c0, n: ar(5760 + k * 1024 + c0, n)
        Pb = lambda b2: ar(5760 + b2 * 512, 512)
        Ptb = lambda b2: ar(6784 + b2 * 512, 512)
        rsst = lambda b4: ar(7808 + b4 * 512, 512, 1)
        attnT = lambda c0, n: ar(5760 + c0, n)
        Qg = lambda c0, n: ar(22144 + c0, n)
        Kg = lambda c0, n: ar(38528 + c0, n)
        svT = lambda c0, n: ar(54912 + c0, n)

        xbQK = SBt("xbQK", [128, 2 * D])
        xb = [xbQK[:, 0:D], xbQK[:, D:2 * D]]
        xnb = SBt("xnb", [128, D])
        wring = [SBt("wr0", [128, 2048], BF16), SBt("wr1", [128, 2048], BF16)]
        xqT8 = SBt("xqT8", [128, 16 * 1024], FP8)
        w8T = SBt("w8T", [128, 16 * 2048], FP8)
        expb = [SBt("expb0", [128, 512], BF16), SBt("expb1", [128, 512], BF16)]
        sg_sb = [SBt("sg0", [128, 512]), SBt("sg1", [128, 512])]
        silu_sb = expb
        outb = [SBt("outb0", [128, 512]), SBt("outb1", [128, 512])]
        xld = [SBt("xld0", [128, 512]), SBt("xld1", [128, 512])]
        wvst = [outb[0], outb[1], sg_sb[0], sg_sb[1]]
        sIf = SBt("sIf", [128, 128])
        stats_m = SBt("st_m", [128, 16])
        stats_r = SBt("st_r", [128, 16])
        stats_b = SBt("st_b", [128, 16])
        stats_s1 = SBt("st_s1", [128, 16])
        stats_s2 = SBt("st_s2", [128, 16])
        stats_mx = SBt("st_mx", [128, 16])
        stats_mn = SBt("st_mn", [128, 16])
        tmp1 = SBt("tmp1", [128, 16])
        tmp2 = SBt("tmp2", [128, 16])
        gmax = SBt("gmax", [128, 1])
        gsc = SBt("gsc", [1, 136])
        inv2s = SBt("inv2s", [128, 1])
        tws = SBt("tws", [128, 1])
        onesrow = SBt("onesrow", [1, 128])
        ones_bf = SBt("ones_bf", [128, 1], BF16)
        cols = SBt("cols", [128, 80])
        cbq = SBt("cbq", [128, 16])
        bv_col = SBt("bv_col", [128, 1])
        nmx = [SBt("nmx0", [128, 4]), SBt("nmx1", [128, 4])]
        sumb = [SBt("sumb0", [128, 4]), SBt("sumb1", [128, 4])]
        rb = [SBt("rb0", [128, 4]), SBt("rb1", [128, 4])]

        psA = PSt("psA", [128, 512])
        psB = PSt("psB", [128, 512])
        psC = PSt("psC", [128, 512])
        psD = PSt("psD", [128, 512])
        psE = PSt("psE", [128, 512])
        psE2 = PSt("psE2", [128, 512])
        psF = PSt("psF", [1, 512])
        psF2 = PSt("psF2", [1, 512])

        P = Prog()
        sems = {}
        for n in ["sld", "spc", "sbias", "swv", "swvp", "svr", "sx", "sxv",
                  "ssta", "sva", "ssq", "sgt1", "scc1", "scc", "sccb",
                  "sbc", "sinv", "sxn", "stp", "stpe", "sxne",
                  "swq", "swqc", "sqkv", "sqkve", "satt", "sattm", "satte",
                  "sattp", "sattt", "sattpe", "sattrs", "sattre", "srs16",
                  "swf", "sw8t", "sw8e", "srsb", "srsbe", "swo", "swoc",
                  "syg", "ssilu", "ssg", "sgt", "sgte", "sout", "sxl", "sadd",
                  "stwc", "sfz", "srcb"]:
            sems[n] = es.enter_context(nc.semaphore(n))

        def OP(eng, waits, incs, build):
            waits = [(s, v) for s, v in waits if v > 0]
            def fn(e):
                for s, v in waits:
                    e.wait_ge(sems[s], v)
                inst = build(e)
                for s, a in incs:
                    inst = inst.then_inc(sems[s], a)
            P.op(eng, waits, incs, fn)

        _fz = [0]
        def VC(waits, build):
            OP("vector", waits + [("sfz", _fz[0])], [("sfz", 1)], build)
            _fz[0] += 1
            return _fz[0]

        OP("sync", [], [("sld", 16)], lambda e: e.dma_start(out=sIf[:, :], in_=d_ident))
        OP("sync", [], [("sld", 16)], lambda e: e.dma_start(out=xb[0][:, 0:512], in_=d_mask4))
        OP("sync", [], [("sld", 16)], lambda e: e.dma_start(out=xnb[0:16, :], in_=d_hsel))
        OP("sync", [], [("sld", 16)], lambda e: e.dma_start(out=xb[1][0:48, 0:128], in_=d_bqkv))
        OP("sync", [], [("sld", 16)], lambda e: e.dma_start(out=xb[1][0:16, 128:256], in_=d_bout))
        OP("sync", [], [("sld", 16)], lambda e: e.dma_start(out=xb[1][0:16, 256:384], in_=d_bffn))
        OP("sync", [], [("sld", 16)], lambda e: e.dma_start(out=gsc[0:1, 0:1], in_=d_ws))
        NPC = 20
        OP("vector", [("sld", 112)], [("spc", 1)], lambda e: e.tensor_copy(I_bf(), sIf[:, :]))
        OP("vector", [], [("spc", 1)], lambda e: e.tensor_copy(mask4_bf(), xb[0][:, 0:512]))
        for h in range(16):
            OP("vector", [], [("spc", 1)],
               lambda e, h=h: e.tensor_copy(hsel_bf(h), xnb[0:16, h * 128:(h + 1) * 128]))
        OP("vector", [], [("spc", 1)], lambda e: e.memset(ones_bf[:, :], 1.0))
        OP("vector", [], [("spc", 1)], lambda e: e.memset(onesrow[:, :], 1.0))
        OP("tensor", [("spc", NPC)], [("sbias", 1)],
           lambda e: e.matmul(psE[:, 0:48], xb[1][0:48, 0:128], sIf[0:48, 0:48],
                              start=True, stop=True))
        OP("tensor", [], [("sbias", 1)],
           lambda e: e.matmul(psE[:, 48:64], xb[1][0:16, 128:256], sIf[0:16, 0:16],
                              start=True, stop=True))
        OP("tensor", [], [("sbias", 1)],
           lambda e: e.matmul(psE[:, 64:80], xb[1][0:16, 256:384], sIf[0:16, 0:16],
                              start=True, stop=True))
        OP("vector", [("sbias", 3)], [("sbias", 1)],
           lambda e: e.tensor_copy(cols[:, :], psE[:, 0:80]))
        bq_view = cols[:, 0:48].rearrange("p (h j) -> p j h", h=16, j=3)
        OP("vector", [("sbias", 4)], [("sbias", 1)],
           lambda e: e.tensor_scalar(cbq[:, :], bq_view[:, 0, :], C_INV, None, ALU.mult))
        OP("vector", [("sbias", 5)], [("sbias", 1)],
           lambda e: e.tensor_reduce(bv_col[:, :], bq_view[:, 2, :], AX.X, ALU.add))
        OP("tensor", [("sbias", 6)], [("stwc", 1)],
           lambda e: e.matmul(psE[:, 80:81], onesrow[:, :], gsc[0:1, 0:1],
                              start=True, stop=True))
        OP("vector", [("stwc", 1)], [("stwc", 1)],
           lambda e: e.tensor_scalar(tws[:, :], psE[:, 80:81], 2.0, None, ALU.mult))

        def wf_load(oc):
            if oc < 2:
                w = [("spc", NPC)]
            else:
                w = [("sw8e", 4 * (oc - 1))]
            OP("gpsimd", w, [("swf", 16)],
               lambda e, oc=oc: e.dma_start(out=wring[oc % 2][:, :],
                                            in_=d_wffn[oc * 128:(oc + 1) * 128, :]))
        for oc in range(16):
            wf_load(oc)

        def w8t_pe(oc):
            for r in range(4):
                gri = oc * 4 + r
                ebank = psE if gri % 2 == 0 else psE2
                for i in range(4):
                    ic = r * 4 + i
                    waits = []
                    if i == 0:
                        waits.append(("swf", 16 * (oc + 1)))
                        if gri == 0:
                            waits.append(("sbias", 4))
                            waits.append(("stwc", 2))
                        if gri >= 2:
                            waits.append(("sw8e", gri - 1))
                    incs = [("sw8t", 1)] if i == 3 else []
                    OP("tensor", waits, incs,
                       lambda e, eb=ebank, oc=oc, ic=ic, i=i: e.matmul(
                           eb[:, i * 128:(i + 1) * 128],
                           wring[oc % 2][:, ic * 128:(ic + 1) * 128], I_bf(),
                           start=True, stop=True))

        def w8t_evac_round(gri):
            oc, r = gri // 4, gri % 4
            ebank = psE if gri % 2 == 0 else psE2
            dst = w8T[:, :].rearrange("p (ic f) -> p ic f", f=2048)[
                :, r * 4:(r + 1) * 4, oc * 128:(oc + 1) * 128]
            src = ebank[:, :].rearrange("p (i f) -> p i f", f=128)
            OP("vector", [("sw8t", gri + 1)], [("sw8e", 1)],
               lambda e, dst=dst, src=src: e.tensor_copy(dst, src))

        for oc in range(16):
            w8t_pe(oc)

        for c in range(NCH):
            if c == 0:
                w = [("spc", NPC)]
            elif c == 1:
                w = [("sbias", 3)]
            else:
                w = [("ssta", c - 1), ("sxv", c - 1)]
            OP("sync" if c % 2 == 0 else "scalar", w, [("sx", 16)],
               lambda e, c=c: e.dma_start(out=xb[c % 2][:, :], in_=d_x[c * 128:(c + 1) * 128, :]))
            xc = xb[c % 2]
            aw = [("sx", 16 * (c + 1))]
            if c == 0:
                aw.append(("spc", NPC))
            OP("scalar", aw, [],
               lambda e, xc=xc, c=c: e.activation(xnb[:, :], xc[:, :], AF.Square,
                                                  accum_out=stats_s2[:, c:c + 1]))
            OP("scalar", [], [("ssta", 1)],
               lambda e, xc=xc, c=c: e.activation(xnb[:, :], xc[:, :], AF.Identity,
                                                  accum_out=stats_s1[:, c:c + 1]))
            OP("vector", [("sx", 16 * (c + 1))], [],
               lambda e, xc=xc, c=c: e.tensor_reduce(stats_mx[:, c:c + 1], xc[:, :],
                                                     AX.X, ALU.max))
            OP("vector", [], [("sxv", 1)],
               lambda e, xc=xc, c=c: e.tensor_reduce(stats_mn[:, c:c + 1], xc[:, :],
                                                     AX.X, ALU.min))
            for _r in range(4):
                w8t_evac_round(4 * c + _r)

        VC([("ssta", NCH), ("sxv", NCH)],
           lambda e: e.tensor_scalar(stats_m[:, :], stats_s1[:, :],
                                     1.0 / 2048.0, None, ALU.mult))
        VC([], lambda e: e.tensor_mul(tmp2[:, :], stats_m[:, :], stats_m[:, :]))
        VC([], lambda e: e.tensor_scalar(tmp1[:, :], stats_s2[:, :],
                                         1.0 / 2048.0, None, ALU.mult))
        VC([], lambda e: e.tensor_sub(tmp1[:, :], tmp1[:, :], tmp2[:, :]))
        _va = VC([], lambda e: e.tensor_scalar(tmp1[:, :], tmp1[:, :], 1e-5, None, ALU.add))
        OP("scalar", [("sfz", _va)], [("ssq", 1)],
           lambda e: e.activation(tmp2[:, :], tmp1[:, :], AF.Sqrt))
        VC([("ssq", 1)], lambda e: e.reciprocal(stats_r[:, :], tmp2[:, :]))
        VC([], lambda e: e.tensor_mul(tmp2[:, :], stats_r[:, :], stats_r[:, :]))
        VC([], lambda e: e.tensor_mul(tmp2[:, :], tmp2[:, :], tmp1[:, :]))
        VC([], lambda e: e.tensor_scalar(tmp2[:, :], tmp2[:, :], -0.5, 1.5,
                                         ALU.mult, ALU.add))
        VC([], lambda e: e.tensor_mul(stats_r[:, :], stats_r[:, :], tmp2[:, :]))
        VC([], lambda e: e.tensor_mul(tmp2[:, :], stats_m[:, :], stats_r[:, :]))
        FZ_STATS = VC([], lambda e: e.tensor_scalar(stats_b[:, :], tmp2[:, :],
                                                    -1.0, None, ALU.mult))
        VC([], lambda e: e.tensor_sub(tmp2[:, :], stats_mx[:, :], stats_m[:, :]))
        VC([], lambda e: e.tensor_sub(tmp1[:, :], stats_m[:, :], stats_mn[:, :]))
        VC([], lambda e: e.tensor_max(tmp2[:, :], tmp2[:, :], tmp1[:, :]))
        VC([], lambda e: e.tensor_mul(tmp2[:, :], tmp2[:, :], stats_r[:, :]))
        FZ_FIN = VC([], lambda e: e.tensor_reduce(gmax[:, :], tmp2[:, :], AX.X, ALU.max))

        OP("tensor", [("sfz", FZ_FIN), ("sw8e", 64)], [("sgt1", 1)],
           lambda e: e.transpose(psF[0:1, 0:128], gmax[:, :], sIf[:, :]))
        VC([("sgt1", 1)], lambda e: e.tensor_copy(gsc[0:1, 8:136], psF[0:1, 0:128]))
        _gr = VC([], lambda e: e.tensor_reduce(gsc[0:1, 1:2], gsc[0:1, 8:136],
                                               AX.X, ALU.max))
        OP("gpsimd", [("sfz", _gr)], [("scc1", 16)],
           lambda e: e.dma_start(out=d_ccin, in_=gsc[0:1, 1:2]))
        OP("gpsimd", [("scc1", 16)], [("scc", 1)],
           lambda e: e.collective_compute("AllReduce", ALU.max,
                                          replica_groups=[list(range(NCORES))],
                                          ins=[t_ccin.ap().opt()],
                                          outs=[t_ccout.ap().opt()]))
        OP("sync", [("scc", 1)], [("sccb", 16)],
           lambda e: e.dma_start(out=gsc[0:1, 2:3], in_=d_ccout))
        VC([("sccb", 16)],
           lambda e: e.tensor_scalar(gsc[0:1, 3:4], gsc[0:1, 2:3], 1.0 / 448.0, 1e-7,
                                     ALU.mult, ALU.max))
        VC([], lambda e: e.tensor_scalar(gsc[0:1, 4:5], gsc[0:1, 3:4], 2.0, None, ALU.mult))
        _rct = VC([], lambda e: e.reciprocal(gsc[0:1, 5:6], gsc[0:1, 4:5]))
        OP("tensor", [("sfz", _rct), ("stwc", 2), ("sbias", 4)], [("sbc", 1)],
           lambda e: e.matmul(psE[:, 81:82], onesrow[:, :], gsc[0:1, 5:6],
                              start=True, stop=True))
        OP("vector", [("sbc", 1)], [("sinv", 1)],
           lambda e: e.tensor_copy(inv2s[:, :], psE[:, 81:82]))

        wq_v = d_wqkv.rearrange("m (h j d) -> m h j d", h=16, j=3, d=128)
        for k in range(16):
            for g in range(4):
                w = []
                if g == 0 and k >= 1:
                    w = [("swvp", k)]
                OP("gpsimd", w, [("swv", 16)],
                   lambda e, k=k, g=g: e.dma_start(
                       out=wvst[g][:, :],
                       in_=wq_v[k * 128:(k + 1) * 128, g * 4:(g + 1) * 4, 2, :]))

        def wv_sum_pe(k):
            for g in range(4):
                for h2 in range(4):
                    idx = g * 4 + h2
                    waits = []
                    if idx == 0:
                        waits.append(("swv", 64 * (k + 1)))
                        if k >= 1:
                            waits.append(("svr", k))
                    incs = [("swvp", 1)] if idx == 15 else []
                    OP("tensor", waits, incs,
                       lambda e, g=g, h2=h2, idx=idx: e.matmul(
                           psE2[:, 0:128], sIf[:, :],
                           wvst[g][:, h2 * 128:(h2 + 1) * 128],
                           start=(idx == 0), stop=(idx == 15)))

        def wv_evac(k):
            OP("vector", [("swvp", k + 1)], [("svr", 1)],
               lambda e, k=k: e.tensor_copy(wvsum(k), psE2[:, 0:128]))

        for T in range(NT):
            for c8 in range(8):
                gc = T * 8 + c8
                widx = NCH + gc
                if gc >= 2:
                    w = [("sxn", gc - 1)]
                    if T == 1 and c8 < 2:
                        w.append(("swoc", 16))
                else:
                    w = [("sxv", NCH), ("ssta", NCH)]
                OP("sync" if c8 % 2 == 0 else "scalar", w, [("sx", 16)],
                   lambda e, c=gc, b=xb[gc % 2]: e.dma_start(
                       out=b[:, :], in_=d_x[c * 128:(c + 1) * 128, :]))
                aw = [("sx", 16 * (widx + 1)), ("sfz", FZ_STATS)]
                if gc >= 1:
                    aw.append(("stp", 4 * gc))
                OP("scalar", aw, [("sxn", 1)],
                   lambda e, c=gc, b=xb[gc % 2]: e.activation(
                       xnb[:, :], b[:, :], AF.Identity,
                       bias=stats_b[:, c:c + 1], scale=stats_r[:, c:c + 1]))
                for r in range(4):
                    gri = gc * 4 + r
                    bank = psC if gri % 2 == 0 else psD
                    for i in range(4):
                        k = r * 4 + i
                        waits = []
                        if i == 0:
                            waits.append(("sxn", gc + 1))
                            if gri >= 2:
                                waits.append(("stpe", gri - 1))
                            if T == 1 and r < 2 and c8 == 0:
                                waits.append(("ssg", 32))
                        incs = [("stp", 1)] if i == 3 else []
                        OP("tensor", waits, incs,
                           lambda e, bank=bank, i=i, k=k: e.transpose(
                               bank[:, i * 128:(i + 1) * 128],
                               xnb[:, k * 128:(k + 1) * 128], sIf[:, :]))
                    bank3 = bank[:, :].rearrange("p (i f) -> p i f", f=128)
                    xn_sl = AR[0:128, 5760:22144].rearrange(
                        "p (k t) -> p k t", t=1024)[:, r * 4:(r + 1) * 4,
                                                    c8 * 128:c8 * 128 + 128]
                    xq_sl = xqT8[:, :].rearrange(
                        "p (k t) -> p k t", t=1024)[:, r * 4:(r + 1) * 4,
                                                    c8 * 128:c8 * 128 + 128]
                    OP("vector", [("stp", gri + 1), ("sinv", 1)], [],
                       lambda e, xn_sl=xn_sl, bank3=bank3: e.tensor_copy(
                           xn_sl, bank3))
                    OP("vector", [], [("stpe", 1)],
                       lambda e, xq_sl=xq_sl, bank3=bank3: e.tensor_scalar(
                           xq_sl, bank3, inv2s[:, :], None, ALU.mult))

            for s in range(33):
                jjs = T * 33 + s
                if s < 32:
                    h = s // 2
                    is_q = (s % 2 == 0)
                    j = h if is_q else 16 + h
                    if is_q:
                        wq_blk = d_wqkv.rearrange("(k p) f -> p k f", p=128)[
                            :, :, (3 * h) * 128:(3 * h) * 128 + 256]
                        lw = []
                        if s < 2:
                            lw.append(("sxn", 8 * (T + 1)))
                            if T == 1:
                                lw.append(("swoc", 16))
                        else:
                            lw.append(("swqc", 32 * T + s))
                        OP("sync", lw, [("swq", 16)],
                           lambda e, blk=wq_blk: e.dma_start(
                               out=xbQK[:, :].rearrange("p (k f) -> p k f", f=256),
                               in_=blk))
                    cw = [("swq", 16 * (T * 16 + s // 2 + 1))]
                    if s < 2:
                        cw.append(("sw8t", 64) if T == 0 else ("syg", 32))
                    else:
                        cw.append(("sqkv", 2 * (jjs - 1)))
                    fl = 0 if is_q else 128
                    src3 = xbQK[:, :].rearrange("p (k f) -> p k f", f=256)[
                        :, :, fl:fl + 128]
                    dst3 = wring[s % 2][:, :].rearrange("p (k f) -> p k f", f=128)
                    OP("vector", cw, [("swqc", 1)],
                       lambda e, dst3=dst3, src3=src3: e.tensor_copy(dst3, src3))
                else:
                    j = 32
                if T == 0 and s % 2 == 0 and s < 32:
                    wv_sum_pe(s // 2)
                    wv_evac(s // 2)
                for half in range(2):
                    ihalf = jjs * 2 + half
                    pbank = psA if ihalf % 2 == 0 else psB
                    for k in range(16):
                        waits = []
                        if k == 0:
                            if s < 32:
                                waits.append(("swqc", 32 * T + s + 1))
                            else:
                                waits.append(("svr", 16))
                            if half == 0 and s == 0:
                                waits.append(("stpe", (T + 1) * 32))
                            if T == 1 and s == 0 and half == 0:
                                waits.append(("ssg", 32))
                            if ihalf >= 2:
                                waits.append(("sqkve", ihalf - 1))
                        lhs = (wring[s % 2][:, k * 128:(k + 1) * 128] if s < 32
                               else wvsum(k))
                        incs = [("sqkv", 1)] if k == 15 else []
                        OP("tensor", waits, incs,
                           lambda e, p=pbank, lhs=lhs, k=k, half=half: e.matmul(
                               p[:, :], lhs, xnT(k, half * 512, 512),
                               start=(k == 0), stop=(k == 15)))
                    if s < 32:
                        dstf = Qg(0, 16384) if j < 16 else Kg(0, 16384)
                        dst3 = dstf.rearrange("p (g hh t) -> p g hh t", g=128, hh=16, t=8)
                        dst = dst3[:, half * 64:(half + 1) * 64, h, :]
                        sc = C_INV if j < 16 else 1.0
                        bias = cbq[:, h:h + 1] if j < 16 else cols[:, 3 * h + 1:3 * h + 2]
                        OP("scalar", [("sqkv", ihalf + 1)], [("sqkve", 1)],
                           lambda e, p=pbank, dst=dst, sc=sc, bias=bias: e.activation(
                               dst, p[:, :], AF.Identity, bias=bias, scale=sc))
                    else:
                        OP("scalar", [("sqkv", ihalf + 1)], [("sqkve", 1)],
                           lambda e, p=pbank, half=half: e.activation(
                               svT(half * 512, 512), p[:, :], AF.Identity,
                               bias=bv_col[:, :], scale=1.0))

            def c_qk(b):
                gb = T * NB + b
                bank = [psC, psD, psE, psE2][b % 4]
                for g in range(4):
                    waits = []
                    if g == 0:
                        waits.append(("sqkve", (T + 1) * 66))
                        if T == 0 and b == 2:
                            waits.append(("sinv", 1))
                        if T == 0 and b == 3:
                            waits.append(("svr", 16))
                        if gb >= 4:
                            waits.append(("satte", gb - 3))
                    OP("tensor", waits, [],
                       lambda e, bank=bank, b=b, g=g: e.matmul(
                           bank[:, g * 128:(g + 1) * 128],
                           Qg(b * 512 + g * 128, 128), Kg(b * 512 + g * 128, 128),
                           start=(g == 0), stop=False))
                OP("tensor", [], [("satt", 1)],
                   lambda e, bank=bank: e.matmul(bank[:, :], I_bf(), mask4_bf(),
                                                 start=False, stop=True))

            def c_negmax(b):
                gb = T * NB + b
                bank = [psC, psD, psE, psE2][b % 4]
                b2 = b % 2
                mw = [("satt", gb + 1)]
                if gb >= 2:
                    mw.append(("satte", gb - 1))
                OP("vector", mw, [("sattm", 1)],
                   lambda e, bank=bank, b2=b2: e.tensor_reduce(
                       nmx[b2][:, :],
                       bank[:, :].rearrange("p (g n) -> p g n", g=4, n=128),
                       AX.X, ALU.max, negate=True))

            def c_exps(b):
                gb = T * NB + b
                bank = [psC, psD, psE, psE2][b % 4]
                b2 = b % 2
                for g in range(4):
                    waits = []
                    if g == 0:
                        waits.append(("sattm", gb + 1))
                        if gb >= 2:
                            waits.append(("sattp", gb - 1))
                    incs = [("satte", 1)] if g == 3 else []
                    OP("scalar", waits, incs,
                       lambda e, bank=bank, b2=b2, g=g: e.activation(
                           expb[b2][:, g * 128:(g + 1) * 128],
                           bank[:, g * 128:(g + 1) * 128], AF.Exp,
                           bias=nmx[b2][:, g:g + 1], scale=1.0,
                           accum_out=sumb[b2][:, g:g + 1]))

            def c_pmult(b):
                gb = T * NB + b
                b2 = b % 2
                rw = [("satte", gb + 1)]
                if gb >= 2:
                    rw.append(("sattt", gb - 1))
                OP("vector", rw, [("srcb", 1)],
                   lambda e, b2=b2: e.reciprocal(rb[b2][:, :], sumb[b2][:, :]))
                for g in range(4):
                    incs = [("sattp", 1)] if g == 3 else []
                    OP("vector", [("srcb", gb + 1)] if g == 0 else [], incs,
                       lambda e, b2=b2, g=g: e.tensor_scalar(
                           Pb(b2)[:, g * 128:(g + 1) * 128],
                           expb[b2][:, g * 128:(g + 1) * 128],
                           rb[b2][:, g:g + 1], None, ALU.mult))

            def c_pt(b):
                gb = T * NB + b
                b2 = b % 2
                ebank = psA if b % 2 == 0 else psB
                for g in range(4):
                    waits = []
                    if g == 0:
                        waits.append(("sattp", gb + 1))
                        if gb >= 2:
                            waits.append(("sattpe", gb - 1))
                    incs = [("sattt", 1)] if g == 3 else []
                    OP("tensor", waits, incs,
                       lambda e, eb=ebank, b2=b2, g=g: e.matmul(
                           eb[:, g * 128:(g + 1) * 128],
                           Pb(b2)[:, g * 128:(g + 1) * 128], I_bf(),
                           start=True, stop=True))

            def c_ptevac(b):
                gb = T * NB + b
                b2 = b % 2
                ebank = psA if b % 2 == 0 else psB
                pw = [("sattt", gb + 1)]
                if gb >= 2:
                    pw.append(("sattrs", gb - 1))
                OP("vector", pw, [("sattpe", 1)],
                   lambda e, eb=ebank, b2=b2: e.tensor_copy(Ptb(b2), eb[:, :]))

            def c_rsmm(b):
                gb = T * NB + b
                b2 = b % 2
                fbank = psF if b % 2 == 0 else psF2
                for g in range(4):
                    waits = []
                    if g == 0:
                        waits.append(("sattpe", gb + 1))
                        if gb >= 2:
                            waits.append(("sattre", gb - 1))
                    incs = [("sattrs", 1)] if g == 3 else []
                    OP("tensor", waits, incs,
                       lambda e, b2=b2, g=g, fb=fbank: e.matmul(
                           fb[0:1, g * 128:(g + 1) * 128], ones_bf[:, :],
                           Ptb(b2)[:, g * 128:(g + 1) * 128], start=True, stop=True))

            def c_rsevac(b):
                gb = T * NB + b
                b4 = b % 4
                fbank = psF if b % 2 == 0 else psF2
                rdst = rsst(b4).rearrange("p (q g t) -> p g q t", q=16, g=4, t=8)
                rsrc = fbank[0:1, :].rearrange("p (g q t) -> p g q t", g=4, q=16, t=8)
                ew = [("sattrs", gb + 1)]
                if gb >= 4:
                    ew.append(("srs16", 16 * (gb - 3)))
                OP("scalar", ew, [("sattre", 1)],
                   lambda e, rdst=rdst, rsrc=rsrc: e.activation(rdst, rsrc, AF.Copy))
                rsv = rsst(b4).rearrange("p (q gt) -> p q gt", q=16, gt=32)
                OP("scalar", [], [("srs16", 16)],
                   lambda e, rsv=rsv, b=b: e.dma_start(out=rs16(b * 32, 32), in_=rsv))

            for it in range(NB + 2):
                if it < NB:
                    c_qk(it)
                    c_negmax(it)
                    c_exps(it)
                if 0 <= it - 1 < NB:
                    c_pmult(it - 1)
                    c_pt(it - 1)
                if 0 <= it - 2 < NB:
                    c_ptevac(it - 2)
                    c_rsmm(it - 2)
                    c_rsevac(it - 2)

            for h in range(16):
                for half in range(2):
                    ih = (T * 16 + h) * 2 + half
                    ebank = psE if ih % 2 == 0 else psE2
                    waits = [("srs16", 16 * (T + 1) * NB), ("sw8e", 64),
                             ("satte", (T + 1) * NB)]
                    if ih >= 2:
                        waits.append(("srsbe", ih - 1))
                    OP("tensor", waits, [("srsb", 1)],
                       lambda e, eb=ebank, h=h, half=half: e.matmul(
                           eb[:, :], hsel_bf(h), rs16(half * 512, 512),
                           start=True, stop=True))
                    OP("vector", [("srsb", ih + 1)], [("srsbe", 1)],
                       lambda e, eb=ebank, h=h, half=half: e.tensor_tensor(
                           attnT(h * 1024 + half * 512, 512), eb[:, :],
                           svT(half * 512, 512), ALU.mult))

            def f_load(j):
                q = j // 2
                if j % 2 == 0:
                    lw = []
                    if q == 0:
                        lw.append(("srsbe", (T + 1) * 32))
                        lw.append(("swqc", 32 * (T + 1)))
                    else:
                        lw.append(("swoc", 16 * T + 2 * q))
                    wo_blk = d_wout.rearrange("(k p) f -> p k f", p=128)[
                        :, :, j * 128:j * 128 + 256]
                    OP("sync", lw, [("swo", 16)],
                       lambda e, blk=wo_blk: e.dma_start(
                           out=xbQK[:, :].rearrange("p (k f) -> p k f", f=256),
                           in_=blk))
                cw = [("swo", 16 * (T * 8 + q + 1))]
                if j < 2:
                    cw.append(("sqkv", 66 * (T + 1)))
                else:
                    cw.append(("syg", 2 * (T * 16 + j - 1)))
                fl = (j % 2) * 128
                src3 = xbQK[:, :].rearrange("p (k f) -> p k f", f=256)[:, :, fl:fl + 128]
                dst3 = wring[j % 2][:, :].rearrange("p (k f) -> p k f", f=128)
                OP("vector", cw, [("swoc", 1)],
                   lambda e, dst3=dst3, src3=src3: e.tensor_copy(dst3, src3))

            def f_mm(j, half):
                ihalf = (T * 16 + j) * 2 + half
                i2 = ihalf % 2
                ybank = psA if i2 == 0 else psC
                gbank2 = psB if i2 == 0 else psD
                rbase = T * 1024 + half * 512
                xsrc = d_x[rbase:rbase + 512, j * 128:(j + 1) * 128] \
                    .rearrange("(tg t) m -> t tg m", tg=4, t=128)
                xdst = xld[i2][:, :].rearrange("p (tg m) -> p tg m", tg=4, m=128)
                OP("scalar",
                   [("sadd", ihalf - 1)] if ihalf >= 2 else [], [("sxl", 16)],
                   lambda e, xdst=xdst, xsrc=xsrc: e.dma_start(out=xdst, in_=xsrc))
                for k in range(16):
                    waits = []
                    if k == 0:
                        waits.append(("swoc", 16 * T + j + 1))
                        waits.append(("srsbe", (T + 1) * 32))
                        if T == 1 and ihalf % 32 < 2:
                            waits.append(("ssg", 32))
                        if ihalf >= 2:
                            waits.append(("ssg", ihalf - 1))
                    OP("tensor", waits, [],
                       lambda e, j=j, k=k, half=half, yb=ybank: e.matmul(
                           yb[:, :], wring[j % 2][:, k * 128:(k + 1) * 128],
                           attnT(k * 1024 + half * 512, 512),
                           start=(k == 0), stop=(k == 15)))
                for k in range(16):
                    incs = [("syg", 1)] if k == 15 else []
                    OP("tensor", [], incs,
                       lambda e, j=j, k=k, half=half, gb2=gbank2: e.matmul(
                           gb2[:, :],
                           w8T[:, k * 2048 + j * 128:k * 2048 + (j + 1) * 128],
                           xqT8[:, k * 1024 + half * 512:k * 1024 + half * 512 + 512],
                           start=(k == 0), stop=(k == 15)))

            def f_silu_stt(j, half):
                ihalf = (T * 16 + j) * 2 + half
                i2 = ihalf % 2
                ybank = psA if i2 == 0 else psC
                gbank2 = psB if i2 == 0 else psD
                sw = [("syg", ihalf + 1)]
                if ihalf >= 2:
                    sw.append(("ssg", ihalf - 1))
                OP("scalar", sw, [("ssilu", 1)],
                   lambda e, j=j, gb2=gbank2, i2=i2: e.activation(
                       silu_sb[i2][:, :], gb2[:, :], AF.Silu,
                       bias=cols[:, 64 + j:65 + j], scale=tws[:, :]))
                gw = [("ssilu", ihalf + 1)]
                if ihalf >= 2:
                    gw.append(("sgt", ihalf - 1))
                OP("vector", gw, [("ssg", 1)],
                   lambda e, j=j, yb=ybank, i2=i2: e.scalar_tensor_tensor(
                       sg_sb[i2][:, :], yb[:, :], cols[:, 48 + j:49 + j],
                       silu_sb[i2][:, :], ALU.add, ALU.mult))

            def f_tgate(j, half):
                ihalf = (T * 16 + j) * 2 + half
                i2 = ihalf % 2
                tbank = psE if i2 == 0 else psE2
                for tg in range(4):
                    waits = []
                    if tg == 0:
                        waits.append(("ssg", ihalf + 1))
                        if ihalf % 32 < 2:
                            waits.append(("srsbe", (T + 1) * 32))
                        if ihalf >= 2:
                            waits.append(("sgte", ihalf - 1))
                    incs = [("sgt", 1)] if tg == 3 else []
                    OP("tensor", waits, incs,
                       lambda e, gb=tbank, tg=tg, i2=i2: e.transpose(
                           gb[:, tg * 128:(tg + 1) * 128],
                           sg_sb[i2][:, tg * 128:(tg + 1) * 128], sIf[:, :]))

            def f_out(j, half):
                ihalf = (T * 16 + j) * 2 + half
                i2 = ihalf % 2
                tbank = psE if i2 == 0 else psE2
                ow = [("sgt", ihalf + 1)]
                if ihalf >= 2:
                    ow.append(("sout", 16 * (ihalf - 1)))
                OP("vector", ow, [("sgte", 1)],
                   lambda e, gb=tbank, b2=i2: e.tensor_copy(outb[b2][:, :], gb[:, :]))
                OP("vector", [("sxl", 16 * (ihalf + 1))], [("sadd", 1)],
                   lambda e, b2=i2: e.tensor_tensor(outb[b2][:, :], outb[b2][:, :],
                                                    xld[b2][:, :], ALU.add))
                rbase = T * 1024 + half * 512
                dst = d_out[rbase:rbase + 512, j * 128:(j + 1) * 128] \
                    .rearrange("(tg t) m -> t tg m", tg=4, t=128)
                srcb_ = outb[i2][:, :].rearrange("p (tg m) -> p tg m", tg=4, m=128)
                OP("gpsimd", [("sadd", ihalf + 1)], [("sout", 16)],
                   lambda e, dst=dst, src=srcb_: e.dma_start(out=dst, in_=src))

            prev = None
            for j in range(16):
                f_load(j)
                for half in range(2):
                    f_mm(j, half)
                    if prev is not None:
                        f_tgate(*prev)
                        f_out(*prev)
                    f_silu_stt(j, half)
                    prev = (j, half)
            f_tgate(*prev)
            f_out(*prev)
            prev = None

        OP("sync", [("sout", 16 * 64)], [], lambda e: e.nop())

        P.check()
        blk = es.enter_context(nc.Block())
        for engname in ENGS:
            lst = P.ops[engname]
            def make(lst):
                def emit(eng):
                    for _, _, fn in lst:
                        fn(eng)
                return emit
            getattr(blk, engname)(make(lst))

    return nc


_NC_CACHE = None


def kernel(L0_attn_x, L0_attn_Wqkv, L0_attn_bqkv, L0_attn_Wout, L0_attn_bout,
           L0_ffn_weight, L0_ffn_weight_scale, L0_ffn_bias):
    global _NC_CACHE
    x = np.ascontiguousarray(np.asarray(L0_attn_x, dtype=np.float32)).reshape(16384, 2048)
    wqkv = np.ascontiguousarray(np.asarray(L0_attn_Wqkv, dtype=np.float32)).reshape(2048, 6144)
    bqkv = np.ascontiguousarray(np.asarray(L0_attn_bqkv, dtype=np.float32)).reshape(48, 128)
    wout = np.ascontiguousarray(np.asarray(L0_attn_Wout, dtype=np.float32)).reshape(2048, 2048)
    bout = np.ascontiguousarray(np.asarray(L0_attn_bout, dtype=np.float32)).reshape(16, 128)
    wffn = np.ascontiguousarray(np.asarray(L0_ffn_weight, dtype=np.float32)).reshape(2048, 2048)
    bffn = np.ascontiguousarray(np.asarray(L0_ffn_bias, dtype=np.float32)).reshape(16, 128)
    ws = np.asarray(L0_ffn_weight_scale, dtype=np.float32).reshape(1, 1)

    ident = np.eye(128, dtype=np.float32)
    m = np.full((128, 128), np.float32(-998244352.0), dtype=np.float32)
    for qh in range(16):
        for tq in range(8):
            for kh in range(qh + 1):
                m[qh * 8 + tq, kh * 8 + tq] = 0.0
    mask4 = np.tile(m, (1, 4))
    hsel = np.zeros((16, 2048), dtype=np.float32)
    for h in range(16):
        hsel[h, h * 128:(h + 1) * 128] = 1.0

    if _NC_CACHE is None:
        _NC_CACHE = build_nc()
    nc = _NC_CACHE

    in_maps = []
    for c in range(NCORES):
        in_maps.append({
            "x": np.ascontiguousarray(x[c * TOK:(c + 1) * TOK, :]),
            "wqkv": wqkv, "bqkv": bqkv, "wout": wout, "bout": bout,
            "wffn": wffn, "bffn": bffn, "wscale": ws,
            "ident": ident, "mask4": mask4, "hsel": hsel,
        })
    res = run_bass_kernel_spmd(nc, in_maps, core_ids=list(range(NCORES)))
    out = np.concatenate([res.results[c]["out"] for c in range(NCORES)], axis=0)
    return np.ascontiguousarray(out.reshape(4, 4096, 2048).astype(np.float32, copy=False))


if __name__ == "__main__":
    nc = build_nc()
    print("build ok; instructions:",
          sum(1 for _ in nc.all_instructions()) if hasattr(nc, "all_instructions") else "?")
